# version 26
# baseline (speedup 1.0000x reference)
import contextlib

CE = ("pe", "act", "dve", "pool", "sp")
DQ = ("sp", "pool", "act")


def _conflict(tag_a, tag_b):
    return tag_a is None or tag_b is None or tag_a == tag_b


class _Region:
    __slots__ = ("w", "r")
    def __init__(self):
        self.w = {}
        self.r = {}


class Prog:
    def __init__(self, nc):
        self.nc = nc
        self.ops = {e: [] for e in CE}
        self.marks = {e: set() for e in CE}
        self.ch_cnt = {}
        self.regions = {}
        self.wm = {e: {} for e in CE}

    def _reg(self, name):
        return self.regions.setdefault(name, _Region())

    def _covered(self, eng, tok):
        kind, key, val = tok
        return self.wm[eng].get(("c", key) if kind == "c" else ("d", key), -1) >= val

    def _note(self, eng, tok):
        kind, key, val = tok
        k = ("c", key) if kind == "c" else ("d", key)
        self.wm[eng][k] = max(self.wm[eng].get(k, -1), val)
        if kind == "c":
            self.marks[key].add(val)

    def op(self, eng, emit, r=(), w=(), ch=None, tag=None):
        def norm(lst):
            return [((it, None) if isinstance(it, str) else tuple(it)) for it in lst]
        r, w = norm(r), norm(w)
        deps = []
        for name, t in r:
            reg = self._reg(name)
            for t2, tok in reg.w.items():
                if _conflict(t, t2):
                    deps.append(tok)
        for name, t in w:
            reg = self._reg(name)
            for t2, tok in reg.w.items():
                if _conflict(t, t2):
                    deps.append(tok)
            for t2, toks in reg.r.items():
                if _conflict(t, t2):
                    deps.extend(toks)
        if ch is not None and ch in self.ch_cnt:
            deps.append(("d", ch, self.ch_cnt[ch]))
        waits = []
        for tok in deps:
            kind, key, val = tok
            if kind == "c" and key == eng == "pe":
                continue
            if self._covered(eng, tok):
                continue
            waits.append(tok)
            self._note(eng, tok)
        idx = len(self.ops[eng])
        if ch is not None:
            self.ch_cnt[ch] = self.ch_cnt.get(ch, 0) + 16
            tok = ("d", ch, self.ch_cnt[ch])
        else:
            tok = ("c", eng, idx)
        self.ops[eng].append({"emit": emit, "waits": waits, "ch": ch, "tag": tag})
        for name, t in w:
            reg = self._reg(name)
            if t is None:
                reg.w = {}
                reg.r = {}
            else:
                reg.w.pop(t, None)
                reg.r.pop(t, None)
            reg.w[t] = tok
        for name, t in r:
            self._reg(name).r.setdefault(t, []).append(tok)
        return tok

    def pe(self, emit, r=(), w=(), tag=None):
        return self.op("pe", emit, r, w, tag=tag)
    def act(self, emit, r=(), w=(), tag=None):
        return self.op("act", emit, r, w, tag=tag)
    def dve(self, emit, r=(), w=(), tag=None):
        return self.op("dve", emit, r, w, tag=tag)
    def pool(self, emit, r=(), w=(), tag=None):
        return self.op("pool", emit, r, w, tag=tag)
    def dma(self, q, ch, emit, r=(), w=(), tag=None):
        assert q in DQ
        return self.op(q, emit, r, w, ch=ch, tag=tag)

    def barrier(self):
        toks = []
        for e in CE:
            for i in range(len(self.ops[e]) - 1, -1, -1):
                o = self.ops[e][i]
                if o["emit"] is not None and o["ch"] is None:
                    toks.append(("c", e, i))
                    break
        for chn, c in self.ch_cnt.items():
            toks.append(("d", chn, c))
        for e in CE:
            waits = []
            for tok in toks:
                kind, key, val = tok
                if kind == "c" and key == e:
                    continue
                if self._covered(e, tok):
                    continue
                waits.append(tok)
                self._note(e, tok)
            if waits:
                self.ops[e].append({"emit": None, "waits": waits, "ch": None,
                                    "tag": "barrier"})

    def finalize(self):
        nc = self.nc
        with contextlib.ExitStack() as es:
            sems = {}
            for s in CE:
                if self.marks[s]:
                    sems[s] = es.enter_context(nc.semaphore("sem_" + s))
            chsems = {c: es.enter_context(nc.semaphore("sch_" + c))
                      for c in self.ch_cnt}
            rank = {}
            for e in CE:
                ms = sorted(self.marks[e])
                rank[e] = {m: i + 1 for i, m in enumerate(ms)}

            block = es.enter_context(nc.Block())
            meths = {"sp": block.sync, "pe": block.tensor, "act": block.scalar,
                     "dve": block.vector, "pool": block.gpsimd}
            for ename in CE:
                def play(eng, ename=ename):
                    for i, o in enumerate(self.ops[ename]):
                        for kind, key, val in o["waits"]:
                            if kind == "d":
                                eng.wait_ge(chsems[key], val)
                            else:
                                eng.wait_ge(sems[key], rank[key][val])
                        if o["emit"] is None:
                            continue
                        instr = o["emit"](eng)
                        if o["ch"] is not None:
                            instr.then_inc(chsems[o["ch"]], 16)
                        elif i in self.marks[ename]:
                            instr.then_inc(sems[ename], 1)
                meths[ename](play)

    def stats(self):
        n = {e: len(self.ops[e]) for e in CE}
        waits = sum(len(o["waits"]) for e in CE for o in self.ops[e])
        return (f"ops={n} waits={waits} "
                f"marks={ {e: len(m) for e, m in self.marks.items()} } "
                f"nch={len(self.ch_cnt)}")


"""Megakernel emitters for mk_seq_postln_d2 (d_model 1024, T 2048/core).

Per-core layout conventions:
- residual stream: token-major fp32 SBUF [128, 16*1024]; tile t at cols
  [1024*t, 1024*t+1024); stream[p, 1024*t+d] = x[128*t+p, d].
- feature-major activations X^T: bf16 [128, 8*2048]; chunk c at cols
  [2048*c, ...): xt[p, 2048*c+tok] = x[tok, 128*c+p].
- matmul: out = lhsT.T @ rhs; contraction on partitions.
"""
import contextlib
import numpy as np
import ml_dtypes
import concourse.bass as bass
import concourse.mybir as mybir

F32 = mybir.dt.float32
I32 = mybir.dt.int32
BF16 = mybir.dt.bfloat16
AF = mybir.ActivationFunctionType
ALU = mybir.AluOpType
AX = mybir.AxisListType

T = 2048
NT = 16
D = 1024
NC = 8


def host_constants():
    c = {}
    c["c_ident"] = np.eye(128, dtype=ml_dtypes.bfloat16)
    c["c_ones"] = np.ones((2, 128), dtype=ml_dtypes.bfloat16)
    c["c_onesf"] = np.ones((128, 2), dtype=np.float32)
    a4 = np.arange(0, 64, 2, dtype=np.float32) / np.int32(64)
    inv4 = (1.0 / (10000.0 ** a4.astype(np.float64))).astype(np.float32)
    freqs4 = (np.arange(T, dtype=np.float32)[:, None] * inv4[None, :]).astype(np.float32)
    pos = np.concatenate([freqs4, freqs4], axis=1).astype(np.float64)
    cosT = np.ascontiguousarray(np.cos(pos).T)
    sinT = np.ascontiguousarray(np.sin(pos).T)
    sinT = np.concatenate([sinT[32:64], sinT[0:32]], axis=0)
    c["c_cos"] = cosT.astype(np.float32)
    c["c_cosl"] = (cosT - c["c_cos"].astype(np.float64)).astype(np.float32)
    c["c_sin"] = sinT.astype(np.float32)
    c["c_sinl"] = (sinT - c["c_sin"].astype(np.float64)).astype(np.float32)
    tri = (np.arange(128)[:, None] <= np.arange(128)[None, :])
    c["c_tri"] = tri.astype(ml_dtypes.bfloat16)
    c["c_recipn"] = np.tile(1.0 / np.arange(1, T + 1, dtype=np.float32)[None, :],
                            (128, 1))
    c["c_onesfr"] = np.ones((1, 128), dtype=np.float32)
    em = np.zeros((8, 4 * 128), dtype=ml_dtypes.bfloat16)
    for krc in range(4):
        for j in range(128):
            em[2 * krc + j // 64, 128 * krc + j] = 1
    c["c_emat"] = em
    return c


CONST_SPECS = [("c_ident", [128, 128], BF16), ("c_ones", [2, 128], BF16),
               ("c_emat", [8, 512], BF16),
               ("c_onesf", [128, 2], F32), ("c_onesfr", [1, 128], F32),
               ("c_cos", [64, T], F32), ("c_cosl", [64, T], F32),
               ("c_sin", [64, T], F32), ("c_sinl", [64, T], F32),
               ("c_tri", [128, 128], BF16),
               ("c_recipn", [128, T], F32)]



class WQuant:
    def __init__(self, core, wname, hbm, R, C, es, wT=None, sink=None,
                 csplit=None, q="sp"):
        self.core, self.p, self.u = core, core.p, core.uid()
        self.q = q
        cw = csplit or C
        self.R, self.C, self.cw = R, C, cw
        self.nrc, self.npc = R // 128, C // cw
        self.nch = self.nrc * self.npc
        self.nic = cw // 128
        assert self.nic <= 16
        self.wT, self.sink = wT, sink
        self.sinkname = wname + "_sink"
        u = self.u
        sb = core.sb
        self.macc = sb(f"macc{u}", [128, self.nch], F32, es)
        self.m1 = sb(f"m1{u}", [128, 1], F32, es)
        self.thr = sb(f"thr{u}", [128, 3], F32, es)
        self.msb = sb(f"msb{u}", [1, 1], F32, es)
        self.hbr = hbm.rearrange("(r p) c -> p r c", p=128)

    def alloc_mean(self, es, junk_ext=None):
        u, cw, sb = self.u, self.cw, self.core.sb
        self.mwch = sb(f"wmch{u}", [128, 2 * cw], F32, es)
        self.mjunk = junk_ext if junk_ext is not None else sb(
            f"wmjk{u}", [128, cw], BF16, es)

    def alloc_comp(self, es, ps_m=None):
        u, cw, sb, ps = self.u, self.cw, self.core.sb, self.core.ps
        self.wchunk = sb(f"wch{u}", [128, 2 * cw], F32, es)
        self.ge = sb(f"ge{u}", [128, 2 * cw], BF16, es)
        self.sg = sb(f"sg{u}", [128, 2 * cw], BF16, es)
        if self.sink is not None:
            self.stg = sb(f"wst{u}", [128, 2 * cw], BF16, es)
        self.ps_m = ps_m if ps_m is not None else ps(f"ps_wm{u}", [128, 512], F32, es)
        self.ps_t = ps(f"ps_w{u}", [128, cw], BF16, es)

    def _src(self, i):
        rc, pc = divmod(i, self.npc)
        return self.hbr[:, rc, self.cw * pc: self.cw * pc + self.cw]

    def mean_chunk(self, i):
        p, u = self.p, self.u
        s = i % 2
        wsl = self.mwch[:, self.cw * s: self.cw * s + self.cw]
        src = self._src(i)
        p.dma(self.q, f"wm{u}_{s}", lambda e, wsl=wsl, src=src: e.dma_start(
            out=wsl, in_=src), w=[(self.mwch.name, s)])
        p.act(lambda e, wsl=wsl, i=i: e.activation(
            self.mjunk[:], wsl, AF.Abs, accum_out=self.macc[:, i:i+1]),
              r=[(self.mwch.name, s)], w=[(self.macc.name, i), self.mjunk.name])

    def calc_thr(self, m_out=None, mkey=None):
        p, core = self.p, self.core
        ps_m = self.ps_m
        p.dve(lambda e: e.reduce_sum(self.m1[:], self.macc[:, 0:self.nch], axis=AX.X),
              r=[(self.macc.name, i) for i in range(self.nch)], w=[self.m1.name])
        p.pe(lambda e: e.matmul(ps_m[0:1, 0:1], core.t["c_onesf"][:, 0:1], self.m1[:],
                                start=True, stop=True),
             r=[self.m1.name, "c_onesf"], w=[(ps_m.name, "s")])
        p.dve(lambda e: e.tensor_scalar(self.msb[:], ps_m[0:1, 0:1],
                                        1.0 / (self.R * self.C), 1e-5,
                                        ALU.mult, ALU.max),
              r=[(ps_m.name, "s")], w=[self.msb.name])
        p.pe(lambda e: e.matmul(ps_m[:, 2:3], core.t["c_onesfr"][0:1, :],
                                self.msb[0:1, 0:1], start=True, stop=True),
             r=[self.msb.name, "c_onesfr"], w=[(ps_m.name, "bc")])
        p.dve(lambda e: e.tensor_scalar(self.thr[:, 0:1], ps_m[:, 2:3], 0.5, None,
                                        ALU.mult),
              r=[(ps_m.name, "bc")], w=[(self.thr.name, 0)])
        p.dve(lambda e: e.tensor_copy(self.thr[:, 1:2], ps_m[:, 2:3]),
              r=[(ps_m.name, "bc")], w=[(self.thr.name, 1)])
        p.dve(lambda e: e.tensor_scalar(self.thr[:, 2:3], ps_m[:, 2:3], -0.5, None,
                                        ALU.mult),
              r=[(ps_m.name, "bc")], w=[(self.thr.name, 2)])
        if m_out is not None:
            p.dve(lambda e: e.tensor_copy(m_out, self.thr[:, 1:2]),
                  r=[(self.thr.name, 1)], w=[mkey])

    def comp_load(self, i):
        if i >= self.nch:
            return
        p, u = self.p, self.u
        s = i % 2
        cw = self.cw
        wsl = self.wchunk[:, cw * s: cw * s + cw]
        src = self._src(i)
        p.dma(self.q, f"wl{u}_{s}", lambda e, wsl=wsl, src=src: e.dma_start(
            out=wsl, in_=src), w=[(self.wchunk.name, s)])

    def comp_calc(self, i):
        p, u = self.p, self.u
        s = i % 2
        cw, nic = self.cw, self.nic
        wsl = self.wchunk[:, cw * s: cw * s + cw]
        gesl = self.ge[:, cw * s: cw * s + cw]
        sgsl = self.sg[:, cw * s: cw * s + cw]
        p.dve(lambda e, wsl=wsl, gesl=gesl: e.tensor_single_scalar(
            gesl, wsl, self.thr[:, 0:1], ALU.is_ge),
              r=[(self.wchunk.name, s), (self.thr.name, 0)], w=[(self.ge.name, s)])
        p.dve(lambda e, wsl=wsl, sgsl=sgsl: e.tensor_single_scalar(
            sgsl, wsl, self.thr[:, 2:3], ALU.is_le),
              r=[(self.wchunk.name, s), (self.thr.name, 2)], w=[(self.sg.name, s)])
        p.dve(lambda e, gesl=gesl, sgsl=sgsl: e.tensor_sub(gesl, gesl, sgsl),
              r=[(self.sg.name, s)], w=[(self.ge.name, s)])
        for j in range(nic):
            p.pe(lambda e, gesl=gesl, j=j, s=s: e.transpose(
                self.ps_t[:, 128*j:128*j+128], gesl[:, 128*j:128*j+128],
                self.core.t["c_ident"][:]),
                 r=[(self.ge.name, s), "c_ident"], w=[(self.ps_t.name, j)])
        rks = [(self.ps_t.name, j) for j in range(nic)]
        rc, pc = divmod(i, self.npc)
        psrc = self.ps_t[:, 0:128*nic].rearrange("p (ic r) -> p ic r", r=128)
        if self.sink is None:
            dst = self.wT[:].rearrange("p (ic r) -> p ic r", r=self.R)[
                :, nic*pc:nic*pc+nic, 128*rc:128*rc+128]
            if i % 2 == 0:
                p.dve(lambda e, dst=dst: e.tensor_copy(dst, psrc),
                      r=rks, w=[(self.wT.name, ("rc", i))])
            else:
                p.act(lambda e, dst=dst: e.activation(dst, psrc, AF.Copy),
                      r=rks, w=[(self.wT.name, ("rc", i))])
        else:
            stg_sl = self.stg[:, cw*s:cw*s+cw]
            if i % 2 == 0:
                p.dve(lambda e, stg_sl=stg_sl: e.tensor_copy(stg_sl, psrc),
                      r=rks, w=[(self.stg.name, s)])
            else:
                p.act(lambda e, stg_sl=stg_sl: e.activation(stg_sl, psrc, AF.Copy),
                      r=rks, w=[(self.stg.name, s)])
            p.dma(self.q, f"ws{u}", lambda e, stg_sl=stg_sl, i=i: e.dma_start(
                out=self.sink[i], in_=stg_sl),
                  r=[(self.stg.name, s)], w=[(self.sinkname, i)])


class Core:
    def __init__(self, nc):
        self.nc = nc
        self.p = Prog(nc)
        self.es = contextlib.ExitStack()
        self.t = {}
        self.inp = {}
        self.out = {}
        self._uid = 0

    def dram_in(self, name, shape, dtype=F32):
        self.inp[name] = self.nc.dram_tensor(name, shape, dtype,
                                             kind="ExternalInput").ap()
        return self.inp[name]

    def dram_out(self, name, shape, dtype=F32):
        self.out[name] = self.nc.dram_tensor(name, shape, dtype,
                                             kind="ExternalOutput").ap()
        return self.out[name]

    def dram_tmp(self, name, shape, dtype):
        return self.nc.dram_tensor(name, shape, dtype, kind="Internal").ap()

    def sb(self, name, shape, dtype, es=None, side=None):
        kw = {"side": side} if side else {}
        h = (es or self.es).enter_context(self.nc.sbuf_tensor(name, shape, dtype, **kw))
        self.t[name] = h
        return h

    def ps(self, name, shape, dtype=F32, es=None):
        h = (es or self.es).enter_context(self.nc.psum_tensor(name, shape, dtype))
        self.t[name] = h
        return h

    def uid(self):
        self._uid += 1
        return self._uid

    def load_constants(self, names=("c_ident", "c_ones", "c_onesf", "c_onesfr", "c_emat")):
        p = self.p
        for name, shape, dt in CONST_SPECS:
            if name not in names:
                self.dram_in(name, shape, dt)
                continue
            ap = self.dram_in(name, shape, dt)
            t = self.sb(name + "_s", shape, dt)
            self.t[name] = t
            p.dma("sp", "const", lambda e, t=t, ap=ap: e.dma_start(out=t[:], in_=ap),
                  w=[name])
        eps = self.sb("c_eps", [128, 1], F32)
        p.dve(lambda e: e.memset(eps[:], 1e-5), w=["c_eps"])
        half = self.sb("c_half", [128, 1], F32)
        p.dve(lambda e: e.memset(half[:], 0.5), w=["c_half"])

    def load_const_named(self, name, es):
        spec = {n: (s, d) for n, s, d in CONST_SPECS}[name]
        t = self.sb(name + "_s", spec[0], spec[1], es)
        self.t[name] = t
        self.p.dma("sp", "const2", lambda e: e.dma_start(out=t[:], in_=self.inp[name]),
                   w=[name])
        return t

    def load_stream(self, x_hbm, stream):
        p = self.p
        xr = x_hbm.rearrange("(t p) d -> p t d", p=128)
        for g in range(4):
            dst = stream[:].rearrange("p (t d) -> p t d", d=D)[:, 4*g:4*g+4, :]
            p.dma("sp", f"ldst{g % 2}",
                  lambda e, dst=dst, g=g: e.dma_start(out=dst, in_=xr[:, 4*g:4*g+4, :]),
                  w=[(stream.name, ("tile", tt)) for tt in range(4*g, 4*g+4)])

    def store_stream(self, stream, y_hbm, wreg=None):
        p = self.p
        yr = y_hbm.rearrange("(t p) d -> p t d", p=128)
        for g in range(4):
            src = stream[:].rearrange("p (t d) -> p t d", d=D)[:, 4*g:4*g+4, :]
            p.dma("sp", f"stst{g % 2}",
                  lambda e, src=src, g=g: e.dma_start(out=yr[:, 4*g:4*g+4, :], in_=src),
                  r=[(stream.name, ("tile", tt)) for tt in range(4*g, 4*g+4)],
                  w=[] if wreg is None else [(wreg, g)])

    def transpose_stream(self, stream, xt, es, src_bf16=None, tiles=range(NT),
                         xt_cols=None, colbase=0, ptr_ext=None, xb_ext=None,
                         cast_eng="dve"):
        p = self.p
        u = self.uid()
        ptr = ptr_ext if ptr_ext is not None else self.ps(f"ps_tr{u}", [128, 2048], BF16, es)
        xb = xb_ext
        xbtag_none = xb_ext is not None
        if src_bf16 is None and xb is None:
            xb = self.sb(f"xb{u}", [128, 2 * D], BF16, es)
        for t in tiles:
            s = t % 2
            if src_bf16 is None:
                xbs = xb[:, D*s:D*s+D]
                ceng = p.pool if cast_eng == "pool" else p.dve
                ceng(lambda e, xbs=xbs, t=t: e.tensor_copy(xbs, stream[:, D*t:D*t+D]),
                     r=[(stream.name, ("tile", t))],
                     w=[(xb.name, None if xbtag_none else s)])
                src_name = xb.name
                src_tag = None if xbtag_none else s
            else:
                xbs = src_bf16[:, D*t:D*t+D]
                src_name = src_bf16.name
                src_tag = ("tile", t)
            for c in range(NC):
                p.pe(lambda e, xbs=xbs, c=c, s=s:
                     e.transpose(ptr[:, 1024*s+128*c:1024*s+128*c+128],
                                 xbs[:, 128*c:128*c+128], self.t["c_ident"][:]),
                     r=[(src_name, src_tag), "c_ident"], w=[(ptr.name, (s, c))])
            qw = xt_cols if xt_cols is not None else T
            dst = xt[:].rearrange("p (c q) -> p c q", q=qw)[:, :, colbase+128*t:colbase+128*t+128]
            src = ptr[:, 1024*s:1024*s+1024].rearrange("p (c q) -> p c q", q=128)
            rks = [(ptr.name, (s, c)) for c in range(NC)]
            if t % 2 == 0:
                p.dve(lambda e, dst=dst, src=src: e.tensor_copy(dst, src),
                      r=rks, w=[(xt.name, ("tt", t))])
            else:
                p.act(lambda e, dst=dst, src=src: e.activation(dst, src, AF.Copy),
                      r=rks, w=[(xt.name, ("tt", t))])

    def ln_stages(self, r_sl, out_sl, keys_r, keys_w, scr, s=0):
        p = self.p
        u = self.uid()
        sx, sq, mt, vt, junk = scr["sx"], scr["sq"], scr["mt"], scr["vt"], scr["junk"]
        sxs, sqs = sx[:, s:s+1], sq[:, s:s+1]
        mts, vts = mt[:, s:s+1], vt[:, s:s+1]
        return [
            lambda: p.dve(lambda e: e.reduce_sum(sxs, r_sl, axis=AX.X),
                          r=keys_r, w=[(sx.name, (u, s))]),
            lambda: p.act(lambda e: e.activation(junk[:], r_sl, AF.Square,
                                                 accum_out=sqs),
                          r=keys_r, w=[(sq.name, (u, s)), junk.name]),
            lambda: p.dve(lambda e: e.tensor_scalar(mts, sxs, 1.0 / D, None,
                                                    ALU.mult),
                          r=[(sx.name, (u, s))], w=[(mt.name, (u, s))]),
            lambda: p.dve(lambda e: e.scalar_tensor_tensor(vts, sqs, 1.0 / D, sqs,
                                                           ALU.mult, ALU.bypass),
                          r=[(sq.name, (u, s))], w=[(vt.name, (u, s, 0))]),
            lambda: p.dve(lambda e: e.scalar_tensor_tensor(vts, mts, mts, vts,
                                                           ALU.mult, ALU.subtract),
                          r=[(mt.name, (u, s)), (vt.name, (u, s, 0))],
                          w=[(vt.name, (u, s, 1))]),
            lambda: p.act(lambda e: e.activation(vts, vts, AF.Ln, scale=-1.0,
                                                 bias=self.t["c_eps"][:, 0:1]),
                          r=[(vt.name, (u, s, 1)), "c_eps"], w=[(vt.name, (u, s, 2))]),
            lambda: p.act(lambda e: e.activation(vts, vts, AF.Exp, scale=-0.5),
                          r=[(vt.name, (u, s, 2))], w=[(vt.name, (u, s, 3))]),
            lambda: p.dve(lambda e: e.tensor_scalar(out_sl, r_sl, mts, vts,
                                                    ALU.subtract, ALU.mult),
                          r=keys_r + [(mt.name, (u, s)), (vt.name, (u, s, 3))],
                          w=keys_w)]

    def ln_tile(self, r_sl, out_sl, keys_r, keys_w, scr, s=0):
        for f in self.ln_stages(r_sl, out_sl, keys_r, keys_w, scr, s):
            f()

    def ln_pair(self, specs, scr):
        chains = [self.ln_stages(r, o, kr, kw, scr, s=i)
                  for i, (r, o, kr, kw) in enumerate(specs)]
        for stg in range(8):
            for ch in chains:
                ch[stg]()

    def ln_scratch(self, pref, es):
        return {"sx": self.sb(pref + "_sx", [128, 2], F32, es),
                "sq": self.sb(pref + "_sq", [128, 2], F32, es),
                "mt": self.sb(pref + "_mt", [128, 2], F32, es),
                "vt": self.sb(pref + "_vt", [128, 2], F32, es),
                "junk": self.sb(pref + "_junk", [128, D], BF16, es)}

    def emit_l0_attn(self, stream, aux_alloc=None, aux_step=None):
        p = self.p
        with contextlib.ExitStack() as es:
            if aux_alloc is not None:
                aux_alloc(es)
            xt = self.sb("xt0", [128, NC * T], BF16, es)
            with contextlib.ExitStack() as es2:
                self.transpose_stream(stream, xt, es2)
                p.barrier()
            vaug = self.sb("vaug", [128, NT * 1040], BF16, es)
            wv = self.sb("wv", [128, NC * D], BF16, es)
            bvr = self.sb("bvr", [1, D], BF16, es)
            ps_a = self.ps("ps_a", [128, 512], F32, es)
            ps_s = self.ps("ps_s", [128, 2048], F32, es)
            ps_o = self.ps("ps_o", [128, 1024], F32, es)
            ps_tb = self.ps("ps_tb", [128, 1024], BF16, es)
            wvr = self.inp["L0_attn_Wv"].rearrange("(c p) o -> p c o", p=128)
            p.dma("pool", "wv", lambda e: e.dma_start(
                out=wv[:].rearrange("p (c o) -> p c o", o=D), in_=wvr),
                w=[wv.name])
            p.dma("pool", "bv", lambda e: e.dma_start(
                out=bvr[:], in_=self.inp["L0_attn_bv"].rearrange("(a o) -> a o", a=1)),
                w=[bvr.name])
            vg = vaug[:].rearrange("p (k h) -> p k h", h=65)
            for kt in range(NT):
                p.pool(lambda e, kt=kt: e.memset(vg[:, 16*kt:16*kt+16, 64:65], 1.0),
                       w=[(vaug.name, ("ones", kt))])
                for half in range(2):
                    for kc in range(NC):
                        lhs = xt[:].rearrange("p (c q) -> p c q", q=T)[:, kc, 128*kt:128*kt+128]
                        rhs = wv[:, D*kc + 512*half: D*kc + 512*half + 512]
                        p.pe(lambda e, lhs=lhs, rhs=rhs, kc=kc:
                             e.matmul(ps_a[:], lhs, rhs, start=(kc == 0), stop=False),
                             r=[(xt.name, ("tt", kt)), wv.name], w=[(ps_a.name, 0)])
                    p.pe(lambda e, half=half:
                         e.matmul(ps_a[:], self.t["c_ones"][0:1, :],
                                  bvr[0:1, 512*half:512*half+512],
                                  start=False, stop=True),
                         r=["c_ones", bvr.name], w=[(ps_a.name, 0)])
                    dst = vg[:, 16*kt + 8*half:16*kt + 8*half + 8, 0:64]
                    src = ps_a[:].rearrange("p (h d) -> p h d", d=64)
                    if (2 * kt + half) % 2 == 0:
                        p.dve(lambda e, dst=dst, src=src: e.tensor_copy(dst, src),
                              r=[(ps_a.name, 0)], w=[(vaug.name, ("v", kt, half))])
                    else:
                        p.act(lambda e, dst=dst, src=src: e.activation(dst, src, AF.Copy),
                              r=[(ps_a.name, 0)], w=[(vaug.name, ("v", kt, half))])
            qT = self.sb("qT", [64, 2 * T], BF16, es)
            kT = self.sb("kT", [64, 2 * T], BF16, es)
            wqk = self.sb("wqk", [128, 2 * NC * 128], BF16, es)
            bq = self.sb("bq", [128, 16], F32, es)
            for h in range(16):
                p.dma("sp", f"bq{h % 2}", lambda e, h=h: e.dma_start(
                    out=bq[:, h:h+1],
                    in_=self.inp["L0_attn_bqk"].rearrange("(a p) -> p a", p=128)[:, h:h+1]),
                    w=[(bq.name, h)])
            pT = self.sb("pT", [128, 2 * 1024], BF16, es)
            ssum = self.sb("ssum", [1, 512], F32, es)
            oT = self.sb("oT", [64, 512], BF16, es)
            rcol = self.sb("rcol", [128, 4], F32, es)
            wqkr = self.inp["L0_attn_Wqk"].rearrange("(c p) m -> p c m", p=128)
            for h in range(16):
                hs = h % 2
                wsl = wqk[:].rearrange("p (s c m) -> p s c m", s=2, m=128)[:, hs]
                p.dma("pool", f"wqk{hs}", lambda e, wsl=wsl, h=h: e.dma_start(
                    out=wsl, in_=wqkr[:, :, 128*h:128*h+128]),
                    w=[(wqk.name, hs)])
                for qc in range(4):
                    for kc in range(NC):
                        lhs = wsl[:, kc, :]
                        rhs = xt[:].rearrange("p (c q) -> p c q", q=T)[:, kc, 512*qc:512*qc+512]
                        p.pe(lambda e, lhs=lhs, rhs=rhs, kc=kc:
                             e.matmul(ps_a[:], lhs, rhs, start=(kc == 0), stop=(kc == NC - 1)),
                             r=[(wqk.name, hs)] + [(xt.name, ("tt", tt))
                                                   for tt in range(4*qc, 4*qc+4)],
                             w=[(ps_a.name, 0)])
                    dstq = qT[:, T*hs + 512*qc: T*hs + 512*qc + 512]
                    p.dve(lambda e, dstq=dstq, h=h: e.tensor_scalar(
                        dstq, ps_a[0:64, :], bq[0:64, h:h+1], None, ALU.add),
                          r=[(ps_a.name, 0), (bq.name, h)], w=[(qT.name, (hs, qc))])
                    dstk = kT[:, T*hs + 512*qc: T*hs + 512*qc + 512]
                    p.act(lambda e, dstk=dstk, h=h: e.activation(
                        dstk, ps_a[64:128, :], AF.Identity, bias=bq[64:128, h:h+1]),
                          r=[(ps_a.name, 0), (bq.name, h)], w=[(kT.name, (hs, qc))])
                qk_keys = [(qT.name, (hs, qcc)) for qcc in range(4)] + \
                          [(kT.name, (hs, qcc)) for qcc in range(4)]
                pend0 = [None]
                for qh in range(2):
                    def emit_S(kt, qh=qh):
                        ss = kt % 2
                        for j in range(2):
                            lhs = kT[:, T*hs + 128*kt: T*hs + 128*kt + 128]
                            rhs = qT[:, T*hs + 1024*qh + 512*j: T*hs + 1024*qh + 512*j + 512]
                            p.pe(lambda e, lhs=lhs, rhs=rhs, ss=ss, j=j:
                                 e.matmul(ps_s[:, 1024*ss+512*j:1024*ss+512*j+512],
                                          lhs, rhs, start=True, stop=True),
                                 r=qk_keys, w=[(ps_s.name, (ss, j))])
                    emit_S(0)
                    if pend0[0] is not None:
                        pend0[0]()
                        pend0[0] = None
                    for kt in range(NT):
                        ss = kt % 2
                        p.act(lambda e, ss=ss:
                              e.activation(pT[:, 1024*ss:1024*ss+1024],
                                           ps_s[:, 1024*ss:1024*ss+1024],
                                           AF.Exp, scale=1.0 / 65536.0),
                              r=[(ps_s.name, (ss, 0)), (ps_s.name, (ss, 1))],
                              w=[(pT.name, ss)])
                        if kt + 1 < NT:
                            emit_S(kt + 1)
                        for j in range(2):
                            lhs = vaug[:, 1040*kt + 65*h: 1040*kt + 65*h + 65]
                            rhs = pT[:, 1024*ss + 512*j: 1024*ss + 512*j + 512]
                            p.pe(lambda e, lhs=lhs, rhs=rhs, j=j, kt=kt:
                                 e.matmul(ps_o[0:65, 512*j:512*j+512], lhs, rhs,
                                          start=(kt == 0), stop=(kt == NT - 1)),
                                 r=[(vaug.name, None), (pT.name, ss)],
                                 w=[(ps_o.name, j)])
                    def ep0(qh=qh):
                        for j in range(2):
                            qc = 2 * qh + j
                            p.dve(lambda e, j=j: e.tensor_scalar(
                                ssum[:], ps_o[64:65, 512*j:512*j+512], 32.0, None, ALU.mult),
                                  r=[(ps_o.name, j)], w=[ssum.name])
                            p.dve(lambda e, j=j: e.tensor_copy(
                                oT[:], ps_o[0:64, 512*j:512*j+512]),
                                  r=[(ps_o.name, j)], w=[oT.name])
                            for k in range(4):
                                p.pe(lambda e, k=k: e.transpose(
                                    ps_a[:, k:k+1], ssum[0:1, 128*k:128*k+128],
                                    self.t["c_onesf"][0:1, 0:1]),
                                     r=[ssum.name, "c_onesf"], w=[(ps_a.name, 0)])
                                p.pe(lambda e, k=k: e.transpose(
                                    ps_tb[:, 64*k:64*k+64], oT[:, 128*k:128*k+128],
                                    self.t["c_ident"][0:64, 0:64]),
                                     r=[oT.name, "c_ident"], w=[(ps_tb.name, k)])
                            p.dve(lambda e: e.reciprocal(rcol[:], ps_a[:, 0:4]),
                                  r=[(ps_a.name, 0)], w=[rcol.name])
                            for k in range(4):
                                tt = 4 * qc + k
                                out_sl = stream[:, 1024*tt + 64*h: 1024*tt + 64*h + 64]
                                p.dve(lambda e, k=k, out_sl=out_sl:
                                      e.scalar_tensor_tensor(
                                          out_sl, ps_tb[:, 64*k:64*k+64],
                                          rcol[:, k:k+1], out_sl, ALU.mult, ALU.add),
                                      r=[(ps_tb.name, k), rcol.name,
                                         (stream.name, ("tile", tt))],
                                      w=[(stream.name, ("tile", tt))])
                    if qh == 0:
                        pend0[0] = ep0
                    else:
                        ep0()
                if aux_step is not None:
                    aux_step(h)
            scr = self.ln_scratch("l0a", es)
            for t0 in range(0, NT, 2):
                specs = [(stream[:, D*t:D*t+D], stream[:, D*t:D*t+D],
                          [(stream.name, ("tile", t))],
                          [(stream.name, ("tile", t))]) for t in (t0, t0 + 1)]
                self.ln_pair(specs, scr)
            p.barrier()

    def quantize_weight_to_T(self, wname, hbm, R, C, wT, es):
        p = self.p
        u = self.uid()
        nrc, nic = R // 128, C // 128
        nbuf = 2 if C <= 1024 else 1
        wchunk = self.sb(f"wch{u}", [128, nbuf * C], F32, es)
        junk = self.sb(f"wjk{u}", [128, C], BF16, es)
        macc = self.sb(f"macc{u}", [128, nrc], F32, es)
        m1 = self.sb(f"m1{u}", [128, 1], F32, es)
        thr = self.sb(f"thr{u}", [128, 3], F32, es)
        nb2 = 2 if C <= 1024 else 1
        ge = self.sb(f"ge{u}", [128, nb2 * C], BF16, es)
        sg = self.sb(f"sg{u}", [128, nb2 * C], BF16, es)
        ps_m = self.ps(f"ps_m{u}", [128, 512], F32, es)
        ps_t = self.ps(f"ps_t{u}", [128, 2048], BF16, es)
        hbr = hbm.rearrange("(r p) c -> p r c", p=128)
        for rc in range(nrc):
            s = rc % nbuf
            wsl = wchunk[:, C*s:C*s+C]
            p.dma("sp", f"wl{u}_{s}", lambda e, wsl=wsl, rc=rc: e.dma_start(
                out=wsl, in_=hbr[:, rc, :]), w=[(wchunk.name, s)])
            p.act(lambda e, wsl=wsl, rc=rc: e.activation(
                junk[:], wsl, AF.Abs, accum_out=macc[:, rc:rc+1]),
                r=[(wchunk.name, s)], w=[(macc.name, rc), junk.name])
        p.dve(lambda e: e.reduce_sum(m1[:], macc[:, 0:nrc], axis=AX.X),
              r=[(macc.name, rc) for rc in range(nrc)], w=[m1.name])
        msb = self.sb(f"msb{u}", [1, 1], F32, es)
        p.pe(lambda e: e.matmul(ps_m[0:1, 0:1], self.t["c_onesf"][:, 0:1], m1[:],
                                start=True, stop=True),
             r=[m1.name, "c_onesf"], w=[(ps_m.name, "s")])
        p.dve(lambda e: e.tensor_scalar(msb[:], ps_m[0:1, 0:1],
                                        1.0 / (R * C), 1e-5, ALU.mult, ALU.max),
              r=[(ps_m.name, "s")], w=[msb.name])
        p.pe(lambda e: e.matmul(ps_m[:, 2:3], self.t["c_onesfr"][0:1, :],
                                msb[0:1, 0:1], start=True, stop=True),
             r=[msb.name, "c_onesfr"], w=[(ps_m.name, "bc")])
        p.dve(lambda e: e.tensor_scalar(thr[:, 0:1], ps_m[:, 2:3], 0.5, None,
                                        ALU.mult),
              r=[(ps_m.name, "bc")], w=[(thr.name, 0)])
        p.dve(lambda e: e.tensor_copy(thr[:, 1:2], ps_m[:, 2:3]),
              r=[(ps_m.name, "bc")], w=[(thr.name, 1)])
        p.dve(lambda e: e.tensor_scalar(thr[:, 2:3], ps_m[:, 2:3], -0.5, None,
                                        ALU.mult),
              r=[(ps_m.name, "bc")], w=[(thr.name, 2)])
        for rc in range(nrc):
            s = rc % nbuf
            s2 = rc % nb2
            wsl = wchunk[:, C*s:C*s+C]
            gesl = ge[:, C*s2:C*s2+C]
            sgsl = sg[:, C*s2:C*s2+C]
            wint = gesl
            p.dma("sp", f"wl{u}_{s}", lambda e, wsl=wsl, rc=rc: e.dma_start(
                out=wsl, in_=hbr[:, rc, :]), w=[(wchunk.name, s)])
            p.dve(lambda e, wsl=wsl, gesl=gesl: e.tensor_single_scalar(
                gesl, wsl, thr[:, 0:1], ALU.is_ge),
                  r=[(wchunk.name, s), (thr.name, 0)], w=[(ge.name, s2)])
            p.dve(lambda e, wsl=wsl, sgsl=sgsl: e.tensor_single_scalar(
                sgsl, wsl, thr[:, 2:3], ALU.is_le),
                  r=[(wchunk.name, s), (thr.name, 2)], w=[(sg.name, s2)])
            p.dve(lambda e, wint=wint, gesl=gesl, sgsl=sgsl: e.tensor_sub(
                wint, gesl, sgsl),
                  r=[(sg.name, s2)], w=[(ge.name, s2)])
            for g in range(nic // 16 if nic >= 16 else 1):
                nb = min(16, nic)
                for icb in range(nb):
                    ic = g * 16 + icb
                    p.pe(lambda e, wint=wint, ic=ic, icb=icb: e.transpose(
                        ps_t[:, 128*icb:128*icb+128], wint[:, 128*ic:128*ic+128],
                        self.t["c_ident"][:]),
                         r=[(ge.name, s2), "c_ident"], w=[(ps_t.name, icb)])
                dst = wT[:].rearrange("p (ic r) -> p ic r", r=R)[:, 16*g:16*g+nb,
                                                                128*rc:128*rc+128]
                eng = p.dve if rc % 2 == 0 else p.act
                if rc % 2 == 0:
                    p.dve(lambda e, dst=dst, nb=nb: e.tensor_copy(
                        dst, ps_t[:, 0:128*nb].rearrange("p (ic r) -> p ic r", r=128)),
                          r=[(ps_t.name, icb) for icb in range(nb)],
                          w=[(wT.name, ("rc", rc, g))])
                else:
                    p.act(lambda e, dst=dst, nb=nb: e.activation(
                        dst, ps_t[:, 0:128*nb].rearrange("p (ic r) -> p ic r", r=128),
                        AF.Copy),
                          r=[(ps_t.name, icb) for icb in range(nb)],
                          w=[(wT.name, ("rc", rc, g))])
        return thr

    def quantize_act_tile(self, xte, xcol, keys_r, rbf, m_out, scr, cols, mkey,
                          nsplit=1, nwide=None, split_keys=None, tail_pool=False):
        p = self.p
        u = self.uid()
        mr = scr["mr"]
        w = cols // nsplit
        x_full = xte[:, xcol:xcol+cols]
        p.dve(lambda e: e.reduce_max(mr[:], x_full, axis=AX.X,
                                     apply_absolute_value=True),
              r=keys_r, w=[(mr.name, (u, 0))])
        p.dve(lambda e: e.tensor_scalar_max(m_out, mr[:], 1e-5),
              r=[(mr.name, (u, 0))], w=[mkey])
        p.dve(lambda e: e.reciprocal(mr[:], m_out),
              r=[mkey], w=[(mr.name, (u, 1))])
        p.dve(lambda e: e.tensor_scalar(mr[:], mr[:], 127.0, None, ALU.mult),
              r=[(mr.name, (u, 1))], w=[(mr.name, (u, 2))])
        umr = u
        half = self.t["c_half"]

        def split_stages(i):
            s = i % 2
            tt = scr["t"][:, w*s:w*s+w]
            dd = scr["d"][:, w*s:w*s+w]
            vv = scr["v"][:, w*s:w*s+w]
            gt = scr["gt"][:, w*s:w*s+w]
            tk, dk, vk, gk = ("t", s), ("d", s), ("v", s), ("g", s)
            x_sl = xte[:, xcol + w*i: xcol + w*i + w]
            kr_i = split_keys[i] if split_keys is not None else keys_r
            ui = self.uid()
            st = [lambda: p.dve(lambda e: e.tensor_scalar(
                      tt, x_sl, mr[:, 0:1], None, ALU.mult),
                      r=kr_i + [(mr.name, (umr, 2))], w=[("qs", tk)]),
                  lambda: p.act(lambda e: e.activation(dd, tt, AF.Copy),
                      r=[("qs", tk)], w=[("qs", dk)]),
                  lambda: p.dve(lambda e: e.scalar_tensor_tensor(
                      vv, dd, -1.0, tt, ALU.mult, ALU.add),
                      r=[("qs", tk), ("qs", dk)], w=[("qs", vk)])]
            for k in (4.0, 2.0, 1.0):
                st.append(lambda: p.act(lambda e: e.activation(
                    gt, vv, AF.Sign, bias=half[:, 0:1]),
                    r=[("qs", vk), "c_half"], w=[("qs", gk)]))
                st.append(lambda k=k: p.dve(lambda e: e.scalar_tensor_tensor(
                    vv, gt, -k, vv, ALU.mult, ALU.add),
                    r=[("qs", gk)], w=[("qs", vk)]))
            st.append(lambda: p.dve(lambda e: e.tensor_single_scalar(
                gt, vv, -0.5, ALU.is_lt),
                r=[("qs", vk)], w=[("qs", gk)]))
            if tail_pool:
                st.append(lambda: p.pool(lambda e: e.tensor_sub(tt, tt, vv),
                    r=[("qs", vk)], w=[("qs", tk)]))
                st.append(lambda: p.pool(lambda e: e.tensor_sub(
                    rbf[:, w*i:w*i+w], tt, gt),
                    r=[("qs", tk), ("qs", gk)], w=[("rbf_out", (ui, i))]))
            else:
                st.append(lambda: p.dve(lambda e: e.scalar_tensor_tensor(
                    tt, vv, -1.0, tt, ALU.mult, ALU.add),
                    r=[("qs", vk)], w=[("qs", tk)]))
                st.append(lambda: p.dve(lambda e: e.tensor_sub(
                    rbf[:, w*i:w*i+w], tt, gt),
                    r=[("qs", tk), ("qs", gk)], w=[("rbf_out", (ui, i))]))
            return st

        for i0 in range(0, nsplit, 2):
            pair = [split_stages(i) for i in range(i0, min(i0 + 2, nsplit))]
            for stg in range(len(pair[0])):
                for sp in pair:
                    sp[stg]()
        return ("rbf_out", None)

    def l0_part1_start(self, x1sp, rt, mraw, es):
        p = self.p
        x1r = x1sp.rearrange("(t p) d -> p t d", p=128)
        x1rb = self.sb("x1rb", [128, 2 * D], F32, es)
        scr = {"mr": self.sb("q1mr", [128, 1], F32, es),
               "t": self.sb("q1t", [128, D], F32, es),
               "d": self.sb("q1d", [128, D], I32, es),
               "gt": self.sb("q1gt", [128, D], BF16, es),
               "v": self.sb("q1v", [128, D], F32, es)}
        rtile = self.sb("q1r", [128, D], BF16, es)
        ps_t = self.ps("q1ps", [128, 1024], BF16, es)

        def load(t):
            if t >= NT:
                return
            s = t % 2
            xsl = x1rb[:, D*s:D*s+D]
            p.dma("pool", f"x1rb{s}", lambda e, xsl=xsl, t=t: e.dma_start(
                out=xsl, in_=x1r[:, t, :]),
                  r=[("x1sp_r", t // 4)], w=[(x1rb.name, s)])

        def tile(t):
            s = t % 2
            self.quantize_act_tile(x1rb, D*s, [(x1rb.name, s)], rtile,
                                   mraw[:, t:t+1], scr, D,
                                   (mraw.name, t), nsplit=2)
            for c in range(NC):
                p.pe(lambda e, c=c: e.transpose(
                    ps_t[:, 128*c:128*c+128], rtile[:, 128*c:128*c+128],
                    self.t["c_ident"][:]),
                     r=[("rbf_out", None), "c_ident"], w=[(ps_t.name, c)])
            dst = rt[:].rearrange("p (c q) -> p c q", q=T)[:, :, 128*t:128*t+128]
            psrc = ps_t[:, 0:1024].rearrange("p (c q) -> p c q", q=128)
            rks = [(ps_t.name, c) for c in range(NC)]
            if t % 2 == 0:
                p.dve(lambda e, dst=dst: e.tensor_copy(dst, psrc),
                      r=rks, w=[(rt.name, ("tt", t))])
            else:
                p.act(lambda e, dst=dst: e.activation(dst, psrc, AF.Copy),
                      r=rks, w=[(rt.name, ("tt", t))])
        return load, tile

    def make_l0_wquants(self, wgT, wuT_sp, wdT_sp, es):
        INNER = 4096
        qg = WQuant(self, "wg", self.inp["L0_ffn_w_gate"], INNER, D, es,
                    wT=wgT, q="sp")
        qu = WQuant(self, "wu", self.inp["L0_ffn_w_up"], INNER, D, es,
                    sink=wuT_sp, q="pool")
        qd = WQuant(self, "wd", self.inp["L0_ffn_w_down"], D, INNER, es,
                    sink=wdT_sp, csplit=1024, q="act")
        return qg, qu, qd

    def emit_l0_ffn_phaseA(self, x1sp, rt, mraw, qs, mwg, mwu, mwd, es,
                           means_done=False):
        qg, qu, qd = qs
        if not means_done:
            with contextlib.ExitStack() as es_m:
                for q in qs:
                    q.alloc_mean(es_m)
                for i in range(32):
                    qg.mean_chunk(i)
                    qu.mean_chunk(i)
                    qd.mean_chunk(i)
            self.p.barrier()
        ps_m = self.ps("ps_mq", [128, 512], F32, es)
        for q in qs:
            q.alloc_comp(es, ps_m=ps_m)
        p1_load, p1 = self.l0_part1_start(x1sp, rt, mraw, es)
        qg.calc_thr(mwg[:, 0:1], mwg.name)
        qu.calc_thr(mwu[:, 0:1], mwu.name)
        qd.calc_thr(mwd[:, 0:1], mwd.name)
        for q in qs:
            q.comp_load(0)
            q.comp_load(1)
        p1_load(0)
        for t in range(NT):
            p1_load(t + 1)
            p1(t)
            for q in qs:
                q.comp_calc(2*t)
                q.comp_load(2*t + 2)
                q.comp_calc(2*t + 1)
                q.comp_load(2*t + 3)

    def emit_l0_ffn_part2(self, rt, mraw, rh_sp, scd, wgT, wuT, wuT_sp,
                          mwg, mwu):
        p = self.p
        INNER = 4096
        with contextlib.ExitStack() as es:
            for rc in range(32):
                dst = wuT[:].rearrange("p (ic r) -> p ic r", r=INNER)[
                    :, :, 128*rc:128*rc+128]
                wsrc = wuT_sp[rc].rearrange("p (ic r) -> p ic r", r=128)
                p.dma("sp", f"wult{rc % 2}", lambda e, dst=dst, wsrc=wsrc:
                      e.dma_start(out=dst, in_=wsrc),
                      r=[("wu_sink", rc)], w=[(wuT.name, ("ld", rc))])
            h = self.sb("hbuf", [128, INNER], F32, es)
            rh = self.sb("rhbuf", [128, INNER], BF16, es)
            scg = self.sb("scg", [128, 2], F32, es)
            scr = {"mr": self.sb("q2mr", [128, 1], F32, es),
                   "t": self.sb("q2t", [128, INNER // 4], F32, es),
                   "d": self.sb("q2d", [128, INNER // 4], I32, es),
                   "gt": self.sb("q2gt", [128, INNER // 4], BF16, es),
                   "v": self.sb("q2v", [128, INNER // 4], F32, es)}
            mh = self.sb("mh", [128, 1], F32, es)
            ps_g = self.ps("ps_g", [128, 2048], F32, es)
            ps_u = self.ps("ps_u", [128, 2048], F32, es)
            for t in range(NT):
                p.dve(lambda e, t=t: e.tensor_scalar(scg[:, 0:1], mraw[:, t:t+1],
                                                     mwg[:, 0:1], 1.0 / 127.0,
                                                     ALU.mult, ALU.mult),
                      r=[(mraw.name, t), mwg.name], w=[(scg.name, ("g", t))])
                p.dve(lambda e, t=t: e.tensor_scalar(scg[:, 1:2], mraw[:, t:t+1],
                                                     mwu[:, 0:1], 1.0 / 127.0,
                                                     ALU.mult, ALU.mult),
                      r=[(mraw.name, t), mwu.name], w=[(scg.name, ("u", t))])
                for oc in range(8):
                    s = oc % 4
                    gps = ps_g[:, 512*s:512*s+512]
                    ups = ps_u[:, 512*s:512*s+512]
                    for kc in range(NC):
                        lhs = rt[:].rearrange("p (c q) -> p c q", q=T)[:, kc, 128*t:128*t+128]
                        rhsg = wgT[:].rearrange("p (c o) -> p c o", o=INNER)[:, kc, 512*oc:512*oc+512]
                        rhsu = wuT[:].rearrange("p (c o) -> p c o", o=INNER)[:, kc, 512*oc:512*oc+512]
                        p.pe(lambda e, gps=gps, lhs=lhs, rhsg=rhsg, kc=kc:
                             e.matmul(gps, lhs, rhsg, start=(kc == 0), stop=(kc == NC-1)),
                             r=[(rt.name, ("tt", t)), (wgT.name, None)],
                             w=[(ps_g.name, s)])
                        p.pe(lambda e, ups=ups, lhs=lhs, rhsu=rhsu, kc=kc:
                             e.matmul(ups, lhs, rhsu, start=(kc == 0), stop=(kc == NC-1)),
                             r=[(rt.name, ("tt", t)), (wuT.name, None)],
                             w=[(ps_u.name, s)])
                    hsl = h[:, 512*oc:512*oc+512]
                    p.act(lambda e, hsl=hsl, gps=gps: e.activation(
                        hsl, gps, AF.Silu, scale=scg[:, 0:1]),
                          r=[(ps_g.name, s), (scg.name, ("g", t))],
                          w=[(h.name, ("g", oc)), (h.name, ("h", oc))])
                    p.dve(lambda e, hsl=hsl, ups=ups: e.scalar_tensor_tensor(
                        hsl, ups, scg[:, 1:2], hsl, ALU.mult, ALU.mult),
                          r=[(ps_u.name, s), (scg.name, ("u", t)), (h.name, ("g", oc))],
                          w=[(h.name, ("h", oc))])
                self.quantize_act_tile(h, 0,
                                       [(h.name, ("h", oc)) for oc in range(8)],
                                       rh, mh[:, 0:1], scr, INNER,
                                       (mh.name, t), nsplit=8,
                                       split_keys=[[(h.name, ("h", i // 1))]
                                                   for i in range(8)],
                                       tail_pool=True)
                p.dve(lambda e, t=t: e.tensor_scalar(scd[:, t:t+1], mh[:, 0:1],
                                                     1.0 / 127.0, None, ALU.mult),
                      r=[(mh.name, t)], w=[(scd.name, t)])
                p.dma("sp", f"rhsp{t % 2}", lambda e, t=t: e.dma_start(
                    out=rh_sp[t, :, :], in_=rh[:]),
                      r=[("rbf_out", None)], w=[("rh_sp", t)], tag=f"rhsp{t}")

    def emit_l0_ffn_part3(self, x1_src, rh_sp, scd, stream2, wdT_sp, mwd):
        p = self.p
        INNER = 4096
        with contextlib.ExitStack() as es:
            wdT = self.sb("wdT", [128, 32 * D], BF16, es)
            for i in range(32):
                rc, pc = divmod(i, 4)
                dst = wdT[:].rearrange("p (ic r) -> p ic r", r=D)[
                    :, 8*pc:8*pc+8, 128*rc:128*rc+128]
                wsrc = wdT_sp[i].rearrange("p (ic r) -> p ic r", r=128)
                p.dma("sp", f"wdlt{i % 4}", lambda e, dst=dst, wsrc=wsrc:
                      e.dma_start(out=dst, in_=wsrc),
                      r=[("wd_sink", i)], w=[(wdT.name, ("ld", i))])
            rhl = self.sb("rhl", [128, 2 * INNER], BF16, es)
            rhT = self.sb("rhTb", [128, 2 * INNER], BF16, es)
            x1t = self.sb("x1t", [128, 2 * D], F32, es)
            scf = self.sb("scf", [128, 1], F32, es)
            ps_d = self.ps("ps_d", [128, 1024], F32, es)
            ps_t3 = self.ps("ps_t3", [128, 4096], BF16, es)
            x1r = x1_src.rearrange("(t p) d -> p t d", p=128)
            scr = self.ln_scratch("l0f", es)
            for t in range(NT):
                s = t % 2
                lsl = rhl[:, INNER*s:INNER*s+INNER]
                xsl = x1t[:, D*s:D*s+D]
                p.dma("sp", f"rhld{s}", lambda e, lsl=lsl, t=t: e.dma_start(
                    out=lsl, in_=rh_sp[t, :, :]),
                      r=[("rh_sp", t)], w=[(rhl.name, s)])
                rsl = rhT[:, INNER*s:INNER*s+INNER]
                for g2 in range(2):
                    st3 = (2 * t + g2) % 2
                    pt3 = ps_t3[:, 2048*st3:2048*st3+2048]
                    for icb in range(16):
                        ic = 16 * g2 + icb
                        p.pe(lambda e, pt3=pt3, lsl=lsl, ic=ic, icb=icb: e.transpose(
                            pt3[:, 128*icb:128*icb+128], lsl[:, 128*ic:128*ic+128],
                            self.t["c_ident"][:]),
                             r=[(rhl.name, s), "c_ident"], w=[(ps_t3.name, (st3, icb))])
                    rks3 = [(ps_t3.name, (st3, icb)) for icb in range(16)]
                    dst3 = rsl[:, 2048*g2:2048*g2+2048]
                    if g2 == 0:
                        p.dve(lambda e, dst3=dst3, pt3=pt3: e.tensor_copy(dst3, pt3),
                              r=rks3, w=[(rhT.name, (s, 0))])
                    else:
                        p.act(lambda e, dst3=dst3, pt3=pt3: e.activation(dst3, pt3, AF.Copy),
                              r=rks3, w=[(rhT.name, (s, 1))])
                p.dma("pool", f"x1ld{s}", lambda e, xsl=xsl, t=t: e.dma_start(
                    out=xsl, in_=x1r[:, t, :]),
                      r=[("x1sp_r", t // 4)], w=[(x1t.name, s)])
                p.dve(lambda e, t=t: e.tensor_scalar(scf[:], scd[:, t:t+1],
                                                     mwd[:, 0:1], None, ALU.mult),
                      r=[(scd.name, t), mwd.name], w=[scf.name])
                for nc2 in range(2):
                    dps = ps_d[:, 512*nc2:512*nc2+512]
                    for kc in range(32):
                        lhs = rsl[:, 128*kc:128*kc+128]
                        rhs = wdT[:, 1024*kc + 512*nc2: 1024*kc + 512*nc2 + 512]
                        p.pe(lambda e, dps=dps, lhs=lhs, rhs=rhs, kc=kc:
                             e.matmul(dps, lhs, rhs, start=(kc == 0), stop=(kc == 31)),
                             r=[(rhT.name, (s, 0)), (rhT.name, (s, 1))]
                             + [(wdT.name, ("ld", 4*rc2 + kc//8)) for rc2 in range(8)],
                             w=[(ps_d.name, nc2)])
                    out_sl = stream2[:, D*t + 512*nc2: D*t + 512*nc2 + 512]
                    xs2 = xsl[:, 512*nc2:512*nc2+512]
                    p.dve(lambda e, dps=dps, out_sl=out_sl, xs2=xs2:
                          e.scalar_tensor_tensor(out_sl, dps, scf[:, 0:1], xs2,
                                                 ALU.mult, ALU.add),
                          r=[(ps_d.name, nc2), scf.name, (x1t.name, s)],
                          w=[(stream2.name, ("pre", t, nc2))])
                if t % 2 == 1:
                    specs = [(stream2[:, D*t2:D*t2+D], stream2[:, D*t2:D*t2+D],
                              [(stream2.name, ("pre", t2, 0)),
                               (stream2.name, ("pre", t2, 1))],
                              [(stream2.name, ("tile", t2))])
                             for t2 in (t - 1, t)]
                    self.ln_pair(specs, scr)
            p.barrier()

    def rope_stages(self, src, dst, W, cb, scr, keys_r, keys_w, s=0):
        p = self.p
        rb, cbuf = scr["rb"], scr["cbuf"]
        o = 512 * s
        cos, sin = self.t["c_cos"], self.t["c_sin"]
        return [
            lambda: p.dve(lambda e: e.scalar_tensor_tensor(
                rb[0:32, o:o+W], src[32:64, 0:W], -1.0,
                sin[32:64, cb:cb+W], ALU.mult, ALU.mult),
                r=keys_r + ["c_sin"], w=[(rb.name, (s, 0))]),
            lambda: p.dve(lambda e: e.scalar_tensor_tensor(
                rb[32:64, o:o+W], src[0:32, 0:W], 1.0,
                sin[0:32, cb:cb+W], ALU.mult, ALU.mult),
                r=keys_r + ["c_sin"], w=[(rb.name, (s, 1))]),
            lambda: p.pool(lambda e: e.tensor_mul(cbuf[0:64, o:o+W], src[0:64, 0:W],
                                                  cos[0:64, cb:cb+W]),
                r=keys_r + ["c_cos"], w=[(cbuf.name, s)]),
            lambda: p.dve(lambda e: e.tensor_add(dst, cbuf[0:64, o:o+W],
                                                 rb[0:64, o:o+W]),
                r=[(rb.name, (s, 0)), (rb.name, (s, 1)), (cbuf.name, s)],
                w=keys_w)]

    def rope(self, src, dst, W, cb, scr, keys_r, keys_w, s=0):
        for f in self.rope_stages(src, dst, W, cb, scr, keys_r, keys_w, s):
            f()

    def emit_l1_attn(self, stream):
        p = self.p
        with contextlib.ExitStack() as es:
            cos = self.load_const_named("c_cos", es)
            sin = self.load_const_named("c_sin", es)
            tri = self.load_const_named("c_tri", es)
            with contextlib.ExitStack() as eslo:
                for nm, hi in [("c_cosl", cos), ("c_sinl", sin)]:
                    lo = self.load_const_named(nm, eslo)
                    p.dve(lambda e, hi=hi, lo=lo: e.tensor_add(hi[:], hi[:], lo[:]),
                          r=[nm, hi.name.replace("_s", "")],
                          w=[hi.name.replace("_s", "")])
                p.barrier()
            xt = self.sb("xt1", [128, NC * T], BF16, es)
            with contextlib.ExitStack() as es2:
                self.transpose_stream(stream, xt, es2)
                p.barrier()
            wq = self.sb("wq1", [128, NC * D], BF16, es)
            wo = self.sb("wo1", [128, NC * D], BF16, es)
            wkv = self.sb("wkv1", [128, NC * 128], BF16, es)
            for nm, hb in [("wq1", "L1_attn_Wq"), ("wo1", "L1_attn_Wo")]:
                t = self.t[nm]
                hr = self.inp[hb].rearrange("(c p) o -> p c o", p=128)
                p.dma("pool", nm, lambda e, t=t, hr=hr: e.dma_start(
                    out=t[:].rearrange("p (c o) -> p c o", o=D), in_=hr), w=[nm])
            hrkv = self.inp["L1_attn_Wkv"].rearrange("(c p) o -> p c o", p=128)
            p.dma("pool", "wkv1", lambda e: e.dma_start(
                out=wkv[:].rearrange("p (c o) -> p c o", o=128), in_=hrkv), w=["wkv1"])
            kTf = self.sb("kTf", [64, 512], F32, es)
            kTb = self.sb("kTb", [64, T], BF16, es)
            vaug1 = self.sb("vaug1", [128, NT * 65], BF16, es)
            scr_r = {"rb": self.sb("rrb", [64, 1024], F32, es),
                     "cbuf": self.sb("rcb", [64, 1024], F32, es)}
            qTq = self.sb("qTq", [64, 16 * 512], BF16, es)
            tmpq = self.sb("tmpq", [128, 1024], F32, es)
            pTb = self.sb("pTb", [128, 2 * 512], BF16, es)
            hi_t = self.sb("hi_t", [1, 512], BF16, es)
            lo_t = self.sb("lo_t", [1, 512], BF16, es)
            rrow = self.sb("rrow", [1, 512], F32, es)
            og = self.sb("og", [64, 512], BF16, es)
            oTn = self.sb("oTn", [128, NC * 512], BF16, es)
            ps_a = self.ps("ps_a1", [128, 1024], F32, es)
            ps_s = self.ps("ps_s1", [128, 1024], F32, es)
            ps_o = self.ps("ps_o1", [128, 1024], F32, es)
            ps_b = self.ps("ps_b1", [128, 1024], F32, es)
            for qc in range(4):
                for kc in range(NC):
                    lhs = wkv[:, 128*kc:128*kc+64]
                    rhs = xt[:].rearrange("p (c q) -> p c q", q=T)[:, kc, 512*qc:512*qc+512]
                    p.pe(lambda e, lhs=lhs, rhs=rhs, kc=kc:
                         e.matmul(ps_a[0:64, 0:512], lhs, rhs, start=(kc == 0), stop=(kc == NC-1)),
                         r=["wkv1"] + [(xt.name, ("tt", tt)) for tt in range(4*qc, 4*qc+4)],
                         w=[(ps_a.name, 0)])
                p.dve(lambda e, qc=qc: e.tensor_copy(kTf[:, 0:512], ps_a[0:64, 0:512]),
                      r=[(ps_a.name, 0)], w=[(kTf.name, None)])
                self.rope(kTf, kTb[0:64, 512*qc:512*qc+512], 512, 512*qc, scr_r,
                          [(kTf.name, None)], [(kTb.name, ("kc", qc))])
            for kt in range(NT):
                sb2 = kt % 2
                pb = ps_b[:, 512*sb2:512*sb2+64]
                for kc in range(NC):
                    lhs = xt[:].rearrange("p (c q) -> p c q", q=T)[:, kc, 128*kt:128*kt+128]
                    rhs = wkv[:, 128*kc+64:128*kc+128]
                    p.pe(lambda e, pb=pb, lhs=lhs, rhs=rhs, kc=kc:
                         e.matmul(pb, lhs, rhs, start=(kc == 0), stop=(kc == NC-1)),
                         r=["wkv1", (xt.name, ("tt", kt))], w=[(ps_b.name, sb2)])
                p.dve(lambda e, pb=pb, kt=kt: e.tensor_copy(vaug1[:, 65*kt:65*kt+64],
                                                            pb),
                      r=[(ps_b.name, sb2)], w=[(vaug1.name, ("v", kt))])
                p.pool(lambda e, kt=kt: e.memset(vaug1[:, 65*kt+64:65*kt+65], 1.0),
                       w=[(vaug1.name, ("o", kt))])
            scr = self.ln_scratch("l1a", es)
            for qc in range(4):
                for oc in range(NC):
                    sa = oc % 2
                    pa = ps_a[:, 512*sa:512*sa+512]
                    for kc in range(NC):
                        lhs = wq[:, 1024*kc + 128*oc: 1024*kc + 128*oc + 128]
                        rhs = xt[:].rearrange("p (c q) -> p c q", q=T)[:, kc, 512*qc:512*qc+512]
                        p.pe(lambda e, pa=pa, lhs=lhs, rhs=rhs, kc=kc:
                             e.matmul(pa, lhs, rhs, start=(kc == 0), stop=(kc == NC-1)),
                             r=["wq1"] + [(xt.name, ("tt", tt)) for tt in range(4*qc, 4*qc+4)],
                             w=[(ps_a.name, sa)])
                    chains = []
                    for hh in range(2):
                        h = 2 * oc + hh
                        tsl = tmpq[:, 512*hh:512*hh+512]
                        def cp(hh=hh, sa=sa, tsl=tsl):
                            p.dve(lambda e, hh=hh: e.tensor_copy(
                                tsl[0:64, :], ps_a[64*hh:64*hh+64, 512*sa:512*sa+512]),
                                  r=[(ps_a.name, sa)], w=[(tmpq.name, hh)])
                        chains.append([cp] + self.rope_stages(
                            tsl, qTq[0:64, 512*h:512*h+512], 512, 512*qc,
                            scr_r, [(tmpq.name, hh)], [(qTq.name, (qc, h))], s=hh))
                    for stg in range(5):
                        chains[0][stg]()
                        chains[1][stg]()
                def make_ep(h, qc=qc):
                    hp = h % 2
                    def ep():
                        p.dve(lambda e, hp=hp: e.reciprocal(
                            rrow[:], ps_o[64:65, 512*hp:512*hp+512]),
                              r=[(ps_o.name, hp)], w=[rrow.name])
                        p.act(lambda e: e.activation(hi_t[:], rrow[:], AF.Copy),
                              r=[rrow.name], w=[hi_t.name])
                        p.dve(lambda e: e.tensor_sub(lo_t[:], rrow[:], hi_t[:]),
                              r=[rrow.name, hi_t.name], w=[lo_t.name])
                        p.pe(lambda e, hp=hp: e.matmul(ps_b[:, 512*hp:512*hp+512],
                                                       self.t["c_ones"][0:1, :], hi_t[:],
                                                       start=True, stop=False),
                             r=[hi_t.name, "c_ones"], w=[(ps_b.name, hp)])
                        p.pe(lambda e, hp=hp: e.matmul(ps_b[:, 512*hp:512*hp+512],
                                                       self.t["c_ones"][0:1, :], lo_t[:],
                                                       start=False, stop=True),
                             r=[lo_t.name, "c_ones"], w=[(ps_b.name, hp)])
                        p.dve(lambda e, hp=hp: e.tensor_copy(og[:], ps_o[0:64, 512*hp:512*hp+512]),
                              r=[(ps_o.name, hp)], w=[og.name])
                        dst = oTn[64*(h % 2):64*(h % 2)+64, 512*(h // 2):512*(h // 2)+512]
                        p.dve(lambda e, dst=dst, hp=hp: e.tensor_mul(dst, og[:],
                                                                     ps_b[0:64, 512*hp:512*hp+512]),
                              r=[og.name, (ps_b.name, hp)], w=[(oTn.name, h)])
                    return ep
                pend = [None]
                for h in range(16):
                    nkt = 4 * qc + 4
                    def emit_S1(kt, h=h):
                        ss = kt % 2
                        psl = ps_s[:, 512*ss:512*ss+512]
                        p.pe(lambda e, psl=psl, kt=kt, h=h:
                             e.matmul(psl, kTb[0:64, 128*kt:128*kt+128],
                                      qTq[0:64, 512*h:512*h+512], start=True, stop=True),
                             r=[(kTb.name, None), (qTq.name, (qc, h))],
                             w=[(ps_s.name, ss)])
                    emit_S1(0)
                    for kt in range(nkt):
                        ss = kt % 2
                        psl = ps_s[:, 512*ss:512*ss+512]
                        ptl = pTb[:, 512*ss:512*ss+512]
                        p.act(lambda e, ptl=ptl, psl=psl: e.activation(
                            ptl, psl, AF.Exp, scale=0.125),
                              r=[(ps_s.name, ss)], w=[(pTb.name, ss)])
                        if kt + 1 < nkt:
                            emit_S1(kt + 1)
                        b = kt - 4 * qc
                        if b >= 0:
                            if b > 0:
                                p.pool(lambda e, ptl=ptl, b=b: e.memset(
                                    ptl[:, 0:128*b], 0.0),
                                       r=[], w=[(pTb.name, ss)])
                            p.pool(lambda e, ptl=ptl, b=b: e.tensor_mul(
                                ptl[:, 128*b:128*b+128], ptl[:, 128*b:128*b+128],
                                tri[:]),
                                   r=["c_tri"], w=[(pTb.name, ss)])
                        hp = h % 2
                        p.pe(lambda e, ptl=ptl, kt=kt, nkt=nkt, hp=hp:
                             e.matmul(ps_o[0:65, 512*hp:512*hp+512],
                                      vaug1[:, 65*kt:65*kt+65], ptl,
                                      start=(kt == 0), stop=(kt == nkt - 1)),
                             r=[(vaug1.name, None), (pTb.name, ss)],
                             w=[(ps_o.name, hp)])
                        if kt == 1 and pend[0] is not None:
                            pend[0]()
                            pend[0] = None
                    pend[0] = make_ep(h)
                if pend[0] is not None:
                    pend[0]()
                    pend[0] = None
                for sub in range(4):
                    tt = 4 * qc + sub
                    for half in range(2):
                        sa = (2 * sub + half) % 2
                        pa = ps_a[:, 512*sa:512*sa+512]
                        for kc in range(NC):
                            lhs = oTn[:, 512*kc + 128*sub: 512*kc + 128*sub + 128]
                            rhs = wo[:, 1024*kc + 512*half: 1024*kc + 512*half + 512]
                            p.pe(lambda e, pa=pa, lhs=lhs, rhs=rhs, kc=kc:
                                 e.matmul(pa, lhs, rhs, start=(kc == 0), stop=(kc == NC-1)),
                                 r=["wo1"] + [(oTn.name, hh) for hh in range(16)],
                                 w=[(ps_a.name, sa)])
                        out_sl = stream[:, D*tt + 512*half: D*tt + 512*half + 512]
                        p.dve(lambda e, pa=pa, out_sl=out_sl: e.tensor_add(out_sl, pa, out_sl),
                              r=[(ps_a.name, sa), (stream.name, ("tile", tt))],
                              w=[(stream.name, ("tile", tt))])
                    if sub % 2 == 1:
                        specs = [(stream[:, D*t2:D*t2+D], stream[:, D*t2:D*t2+D],
                                  [(stream.name, ("tile", t2))],
                                  [(stream.name, ("tile", t2))])
                                 for t2 in (tt - 1, tt)]
                        self.ln_pair(specs, scr)
            p.barrier()

    def prep_wT_stages(self, nm, hbm, es, half=0):
        p = self.p
        u = self.uid()
        sp = self.dram_tmp(f"sp_{nm}", [NC, 128, D], BF16)
        nat32 = self.sb(f"nat32{u}", [128, 2 * D], F32, es)
        nat = self.sb(f"nat{u}", [128, 2 * D], BF16, es)
        stg = self.sb(f"stg{u}", [128, 2 * D], BF16, es)
        ps_t = self.t["ps_prep"]
        ho = 1024 * half
        hbr = hbm.rearrange("(r p) c -> p r c", p=128)

        def load(rc):
            if rc >= NC:
                return
            s = rc % 2
            n3 = nat32[:, D*s:D*s+D]
            p.dma("sp", f"pw{u}{s}", lambda e, n3=n3, rc=rc: e.dma_start(
                out=n3, in_=hbr[:, rc, :]), w=[(nat32.name, s)])

        def mk(rc):
            def f():
                if rc == 0:
                    load(0)
                load(rc + 1)
                s = rc % 2
                nsl = nat[:, D*s:D*s+D]
                n3 = nat32[:, D*s:D*s+D]
                if (rc + half) % 2 == 0:
                    p.act(lambda e, nsl=nsl, n3=n3: e.activation(nsl, n3, AF.Copy),
                          r=[(nat32.name, s)], w=[(nat.name, s)])
                else:
                    p.dve(lambda e, nsl=nsl, n3=n3: e.tensor_copy(nsl, n3),
                          r=[(nat32.name, s)], w=[(nat.name, s)])
                for jc in range(NC):
                    p.pe(lambda e, nsl=nsl, jc=jc: e.transpose(
                        ps_t[:, ho + 128*jc: ho + 128*jc + 128],
                        nsl[:, 128*jc:128*jc+128], self.t["c_ident"][:]),
                         r=[(nat.name, s), "c_ident"], w=[(ps_t.name, (half, jc))])
                ssl = stg[:, D*s:D*s+D]
                rks = [(ps_t.name, (half, jc)) for jc in range(NC)]
                if (rc + half) % 2 == 0:
                    p.dve(lambda e, ssl=ssl: e.tensor_copy(ssl, ps_t[:, ho:ho+1024]),
                          r=rks, w=[(stg.name, s)])
                else:
                    p.act(lambda e, ssl=ssl: e.activation(ssl, ps_t[:, ho:ho+1024],
                                                          AF.Copy),
                          r=rks, w=[(stg.name, s)])
                p.dma("sp", f"pws{u}{s}", lambda e, ssl=ssl, rc=rc: e.dma_start(
                    out=sp.rearrange("c p o -> c p o")[:, :, 128*rc:128*rc+128]
                        .rearrange("c p o -> p c o"),
                    in_=ssl[:].rearrange("p (c o) -> p c o", o=128)),
                      r=[(stg.name, s)], w=[(f"sp_{nm}", rc)])
            return f
        return sp, [mk(rc) for rc in range(NC)]

    def branch(self, qT, kT, qkeys, Vf, Wf, UT, bmat, carry, ci, pP, hT, scr1, qc,
               ps, alpha_col=None, fA=None, bcA=None, totT=None):
        p = self.p
        u = self.uid()
        ps_x, ps_y, ps_u, ps_l = ps
        eg, egb, rw, lg = scr1["eg"], scr1["egb"], scr1["rw"], scr1["lg"]
        cum, tmp = scr1["cum"], scr1["tmp"]
        rb1 = scr1["rb1"]; rb2e = scr1["rb2e"]; hi_t = scr1["hi"]; lo_t = scr1["lo"]
        for oc in range(NC):
            s = oc % 2
            hsl = hT[:, 512*s:512*s+512]
            for kc in range(NC):
                lhs = scr1["rw1T"][:, D*kc + 128*oc: D*kc + 128*oc + 128]
                rhs = qT[:, 512*kc:512*kc+512]
                p.pe(lambda e, lhs=lhs, rhs=rhs, kc=kc:
                     e.matmul(ps_l[:, 0:512], lhs, rhs, start=(kc == 0), stop=(kc == NC-1)),
                     r=qkeys + [("wslot", None)], w=[(ps_l.name, 0)])
            p.act(lambda e, hsl=hsl, oc=oc: e.activation(hsl, ps_l[:, 0:512],
                                                         AF.Gelu, bias=rb1[:, oc:oc+1]),
                  r=[(ps_l.name, 0), "rb1"], w=[(hT.name, s)])
            lhs2 = scr1["rw2T"][:, 8*oc:8*oc+8]
            p.pe(lambda e, lhs2=lhs2, hsl=hsl, oc=oc:
                 e.matmul(ps_l[0:8, 512:1024], lhs2, hsl,
                          start=(oc == 0), stop=(oc == NC-1)),
                 r=[(hT.name, s), ("rw2T", None)], w=[(ps_l.name, 1)])
        p.act(lambda e: e.activation(lg[:], ps_l[0:8, 512:1024], AF.Identity,
                                     bias=rb2e[0:8, 0:1]),
              r=[(ps_l.name, 1), "rb2e"], w=[lg.name])
        p.act(lambda e: e.activation(eg[:], lg[:], AF.Exp),
              r=[lg.name], w=[eg.name])
        p.dve(lambda e: e.tensor_copy(egb[:], eg[:]),
              r=[eg.name], w=[egb.name])
        p.pe(lambda e: e.matmul(ps_l[0:1, 512:1024], self.t["c_onesf"][0:8, 0:1], eg[:],
                                start=True, stop=True),
             r=[eg.name, "c_onesf"], w=[(ps_l.name, 1)])
        p.dve(lambda e: e.reciprocal(rw[:], ps_l[0:1, 512:1024]),
              r=[(ps_l.name, 1)], w=[rw.name])
        p.dve(lambda e: e.tensor_copy(hi_t[:], rw[:]), r=[rw.name], w=[hi_t.name])
        p.dve(lambda e: e.tensor_sub(lo_t[:], rw[:], hi_t[:]),
              r=[rw.name, hi_t.name], w=[lo_t.name])
        if alpha_col is not None:
            p.dve(lambda e: e.tensor_scalar(hi_t[:], hi_t[:], alpha_col[0:1, 0:1],
                                            None, ALU.mult),
                  r=[hi_t.name, "alpha"], w=[hi_t.name])
            p.dve(lambda e: e.tensor_scalar(lo_t[:], lo_t[:], alpha_col[0:1, 0:1],
                                            None, ALU.mult),
                  r=[lo_t.name, "alpha"], w=[lo_t.name])
        p.pe(lambda e: e.matmul(ps_l[:, 512:1024], self.t["c_ones"][0:1, :], hi_t[:],
                                start=True, stop=False),
             r=[hi_t.name, "c_ones"], w=[(ps_l.name, 1)])
        p.pe(lambda e: e.matmul(ps_l[:, 512:1024], self.t["c_ones"][0:1, :], lo_t[:],
                                start=False, stop=True),
             r=[lo_t.name, "c_ones"], w=[(ps_l.name, 1)])
        bca = bcA
        p.act(lambda e: e.activation(bca[:], ps_l[:, 512:1024], AF.Copy),
              r=[(ps_l.name, 1)], w=[bca.name])
        for krc in range(4):
            for dc in range(NC):
                lhsv = Vf[:, 512*dc + 128*krc: 512*dc + 128*krc + 128]
                lhsw = Wf[:, 512*dc + 128*krc: 512*dc + 128*krc + 128]
                rhsq = qT[:, 512*dc:512*dc+512]
                rhsk = kT[:, 512*dc:512*dc+512]
                p.pe(lambda e, lhsv=lhsv, rhsq=rhsq, dc=dc:
                     e.matmul(ps_x[:], lhsv, rhsq, start=(dc == 0), stop=(dc == NC-1)),
                     r=qkeys + [(Vf.name, None)], w=[(ps_x.name, 0)])
                p.pe(lambda e, lhsw=lhsw, rhsk=rhsk, dc=dc:
                     e.matmul(ps_y[:], lhsw, rhsk, start=(dc == 0), stop=(dc == NC-1)),
                     r=[(kT.name, None), (Wf.name, None)], w=[(ps_y.name, 0)])
            p.dve(lambda e, krc=krc: e.tensor_tensor_scan(
                cum[:], ps_y[:], scr1["rc_n"][:], carry[:, 4*ci + krc: 4*ci + krc + 1],
                ALU.add, ALU.bypass),
                  r=[(ps_y.name, 0), (carry.name, (ci, krc)), "rc_n"], w=[cum.name])
            p.dve(lambda e, krc=krc: e.tensor_copy(
                carry[:, 4*ci + krc: 4*ci + krc + 1], cum[:, 511:512]),
                  r=[cum.name], w=[(carry.name, (ci, krc))])
            p.pe(lambda e, krc=krc: e.matmul(
                ps_l[:, 512:1024], self.t["c_emat"][0:8, 128*krc:128*krc+128], egb[:],
                start=True, stop=True),
                 r=[egb.name, "c_emat"], w=[(ps_l.name, 1)])
            p.dve(lambda e: e.tensor_tensor(tmp[:], ps_x[:], cum[:], ALU.mult),
                  r=[(ps_x.name, 0), cum.name], w=[tmp.name])
            p.dve(lambda e, krc=krc: e.tensor_tensor(
                pP[:, 512*krc:512*krc+512], tmp[:], ps_l[:, 512:1024], ALU.mult),
                  r=[tmp.name, (ps_l.name, 1)], w=[(pP.name, krc)])
        for dc in range(NC):
            s = dc % 2
            psu = ps_u[:, 512*s:512*s+512]
            for krc in range(4):
                lhs = UT[:, 1024*krc + 128*dc: 1024*krc + 128*dc + 128]
                p.pe(lambda e, psu=psu, lhs=lhs, krc=krc:
                     e.matmul(psu, lhs, pP[:, 512*krc:512*krc+512],
                              start=(krc == 0), stop=False),
                     r=[(pP.name, krc), (UT.name, None)], w=[(ps_u.name, s)])
            p.pe(lambda e, psu=psu, dc=dc: e.matmul(
                psu, bmat[0:8, 128*dc:128*dc+128], egb[:], start=False, stop=True),
                 r=[egb.name, (bmat.name, None)], w=[(ps_u.name, s)])
            if alpha_col is None:
                p.dve(lambda e, psu=psu, dc=dc: e.tensor_tensor(
                    fA[:, 512*dc:512*dc+512], psu, bca[:], ALU.mult),
                      r=[(ps_u.name, s), bca.name], w=[(fA.name, dc)])
            else:
                p.dve(lambda e, psu=psu: e.scalar_tensor_tensor(
                    tmp[:], psu, 1.0, bca[:], ALU.mult, ALU.mult),
                      r=[(ps_u.name, s), bca.name], w=[tmp.name])
                p.dve(lambda e, dc=dc: e.tensor_add(tmp[:], tmp[:], fA[:, 512*dc:512*dc+512]),
                      r=[tmp.name, (fA.name, dc)], w=[(tmp.name, "b")])
                rsl = scr1["rc_n"]
                p.dve(lambda e, dc=dc, rsl=rsl: e.tensor_tensor(
                    totT[:, 512*dc:512*dc+512], tmp[:], rsl[:], ALU.mult),
                      r=[(tmp.name, "b"), "rc_n"], w=[(totT.name, dc)])

    def emit_l1_ffn(self, stream):
        p = self.p
        with contextlib.ExitStack() as es:
            rc_n = self.sb("rc_n", [128, 512], F32, es)
            flats = {}
            for nm, hb in [("Vf", "L1_ffn_V_fwd"), ("Wf", "L1_ffn_W_fwd"),
                           ("Vi", "L1_ffn_V_inv"), ("Wie", "L1_ffn_W_inv_exp")]:
                t = self.sb(nm, [128, NC * 512], BF16, es)
                flats[nm] = t
            for nm in ["UfT", "UiT"]:
                flats[nm] = self.sb(nm, [128, 4 * D], BF16, es)
            rw2T = self.sb("rw2T", [128, NC * 8], BF16, es)
            self.t["rw2T"] = rw2T
            with contextlib.ExitStack() as esp:
                ps_prep = self.ps("ps_prep", [128, 2048], BF16, esp)
                self.t["ps_prep"] = ps_prep
                stg32 = self.sb("fstg32", [128, 2 * 512], F32, esp)
                cnt = 0
                for nm, hb in [("Vf", "L1_ffn_V_fwd"), ("Wf", "L1_ffn_W_fwd"),
                               ("Vi", "L1_ffn_V_inv"), ("Wie", "L1_ffn_W_inv_exp")]:
                    t = flats[nm]
                    hr = self.inp[hb].rearrange("k (c p) r -> p c k r", p=128)
                    for c in range(NC):
                        s2 = cnt % 2
                        ssl = stg32[:, 512*s2:512*s2+512]
                        p.dma("sp", f"flt{s2}", lambda e, ssl=ssl, hr=hr, c=c:
                              e.dma_start(out=ssl.rearrange("p (k r) -> p k r", r=64),
                                          in_=hr[:, c]),
                              w=[(stg32.name, s2)])
                        dst = t[:, 512*c:512*c+512]
                        if cnt % 2 == 0:
                            p.dve(lambda e, dst=dst, ssl=ssl: e.tensor_copy(dst, ssl),
                                  r=[(stg32.name, s2)], w=[(nm, c)])
                        else:
                            p.act(lambda e, dst=dst, ssl=ssl: e.activation(
                                dst, ssl, AF.Copy),
                                  r=[(stg32.name, s2)], w=[(nm, c)])
                        cnt += 1
                sps = {}
                for pair in [[("wqT", "L1_ffn_W_Q"), ("wkT", "L1_ffn_W_K")],
                             [("wiT", "L1_ffn_W_inv"), ("r1T", "L1_ffn_router_w1")],
                             [("woT", "L1_ffn_W_O")]]:
                    streams = []
                    for half, (nm, hb) in enumerate(pair):
                        sp2, stages = self.prep_wT_stages(nm, self.inp[hb],
                                                          esp, half)
                        sps[nm] = sp2
                        streams.append(stages)
                    for rc in range(NC):
                        for st in streams:
                            st[rc]()
                for nm, hb in [("UfT", "L1_ffn_U_fwd"), ("UiT", "L1_ffn_U_inv")]:
                    t = flats[nm]
                    tmpu = self.sb(nm + "_n", [128, 512], BF16, esp)
                    hr = self.inp[hb].rearrange("k (c p) r -> p c k r", p=128)
                    for c in range(NC):
                        p.dma("pool", nm + "n", lambda e, c=c, hr=hr, tmpu=tmpu: e.dma_start(
                            out=tmpu[:].rearrange("p (k r) -> p k r", r=64), in_=hr[:, c]),
                              w=[(tmpu.name, None)])
                        for b4 in range(4):
                            p.pe(lambda e, b4=b4, tmpu=tmpu: e.transpose(
                                ps_prep[:, 128*b4:128*b4+128], tmpu[:, 128*b4:128*b4+128],
                                self.t["c_ident"][:]),
                                 r=[(tmpu.name, None), "c_ident"],
                                 w=[(ps_prep.name, (0, b4))])
                        dst = t[:].rearrange("p (kc d) -> p kc d", d=D)[:, :, 128*c:128*c+128]
                        p.dve(lambda e, dst=dst: e.tensor_copy(
                            dst, ps_prep[:, 0:512].rearrange("p (kc d) -> p kc d", d=128)),
                              r=[(ps_prep.name, (0, b4)) for b4 in range(4)],
                              w=[(t.name, ("c", c))])
                rw2n = self.sb("rw2n", [8, D], BF16, esp)
                p.dma("pool", "rw2", lambda e: e.dma_start(out=rw2n[:],
                                                           in_=self.inp["L1_ffn_router_w2"]),
                      w=[rw2n.name])
                for c in range(NC):
                    p.pe(lambda e, c=c: e.transpose(ps_prep[:, 0:8], rw2n[0:8, 128*c:128*c+128],
                                                    self.t["c_ident"][0:8, 0:8]),
                         r=[rw2n.name, "c_ident"], w=[(ps_prep.name, (0, 0))])
                    p.dve(lambda e, c=c: e.tensor_copy(rw2T[:, 8*c:8*c+8], ps_prep[:, 0:8]),
                          r=[(ps_prep.name, (0, 0))], w=[("rw2T", c)])
                p.barrier()
            bf = self.sb("bfwd", [8, D], BF16, es)
            bi = self.sb("binv", [8, D], BF16, es)
            p.dma("pool", "bfwd", lambda e: e.dma_start(out=bf[:], in_=self.inp["L1_ffn_b_fwd"]),
                  w=[bf.name])
            p.dma("pool", "binv", lambda e: e.dma_start(out=bi[:], in_=self.inp["L1_ffn_b_inv"]),
                  w=[bi.name])
            rb1 = self.sb("rb1", [128, NC], F32, es)
            for c in range(NC):
                p.dma("sp", "rb1", lambda e, c=c: e.dma_start(
                    out=rb1[:, c:c+1],
                    in_=self.inp["L1_ffn_router_b1"].rearrange("(a p) -> p a", p=128)[:, c:c+1]),
                      w=[("rb1", c)])
            self.t["rb1"] = rb1
            rb2e = self.sb("rb2e", [8, 2], F32, es)
            p.dma("sp", "rb2", lambda e: e.dma_start(
                out=rb2e[0:8, 0:1], in_=self.inp["L1_ffn_router_b2"].rearrange("(a k) -> k a", a=1)),
                  w=[("rb2e", 0)])
            p.dma("sp", "rb2", lambda e: e.dma_start(
                out=rb2e[0:8, 1:2], in_=self.inp["L1_ffn_expert_bias"].rearrange("(a k) -> k a", a=1)),
                  w=[("rb2e", 1)])
            p.dve(lambda e: e.tensor_add(rb2e[0:8, 0:1], rb2e[0:8, 0:1], rb2e[0:8, 1:2]),
                  r=[("rb2e", 0), ("rb2e", 1)], w=["rb2e"])
            alpha_sb = self.sb("alpha_sb", [1, 1], F32, es)
            p.dma("sp", "alph", lambda e: e.dma_start(
                out=alpha_sb[:], in_=self.inp["L1_ffn_alpha_bi"]), w=["alpha"])
            xtc = self.sb("xtc", [128, NC * 512], BF16, es)
            xbshared = self.sb("xbshared", [128, 2 * D], BF16, es)
            pP = xbshared
            wslot = self.sb("wslot", [128, NC * D], BF16, es)
            qTc = self.sb("qTc", [128, NC * 512], BF16, es)
            kTc = self.sb("kTc", [128, NC * 512], BF16, es)
            iqT = self.sb("iqT", [128, NC * 512], BF16, es)
            ikT = self.sb("ikT", [128, NC * 512], BF16, es)
            hT = self.sb("hT1", [128, 2 * 512], BF16, es)
            fA = self.sb("fA", [128, NC * 512], BF16, es)
            totT = fA
            carry = self.sb("carry", [128, 8], F32, es)
            p.dve(lambda e: e.memset(carry[:], 0.0), w=[carry.name])
            scr1 = {"eg": self.sb("eg", [8, 512], F32, es),
                    "egb": self.sb("egb", [8, 512], BF16, es),
                    "rw": self.sb("rw", [1, 512], F32, es),
                    "lg": self.sb("lg", [8, 512], F32, es),
                    "cum": self.sb("cum", [128, 512], F32, es),
                    "tmp": self.sb("tmp1", [128, 512], F32, es),
                    "hi": self.sb("hi1", [1, 512], BF16, es),
                    "lo": self.sb("lo1", [1, 512], BF16, es),
                    "rb1": rb1, "rb2e": rb2e, "rc_n": rc_n,
                    "rw1T": wslot, "rw2T": rw2T}
            bcA = self.sb("bcA", [128, 512], F32, es)
            alpha_col = alpha_sb
            ps_trf = self.ps("ps_trf", [128, 2048], BF16, es)
            ps_x = self.ps("ps_x", [128, 512], F32, es)
            ps_y = self.ps("ps_y", [128, 512], F32, es)
            ps_u = self.ps("ps_u2", [128, 1024], F32, es)
            ps_l = self.ps("ps_l", [128, 1024], F32, es)
            scr = {"sx": self.sb("l1f_sx", [128, 2], F32, es),
                   "sq": self.sb("l1f_sq", [128, 2], F32, es),
                   "mt": self.sb("l1f_mt", [128, 2], F32, es),
                   "vt": self.sb("l1f_vt", [128, 2], F32, es),
                   "junk": hT}

            def load_w(nm, dep):
                spw = sps[nm]
                for c in range(NC):
                    dstw = wslot[:, D*c:D*c+D]
                    p.dma("sp", f"wsl{c % 2}", lambda e, spw=spw, dstw=dstw, c=c:
                          e.dma_start(out=dstw, in_=spw[c]),
                          w=[("wslot", ("ld", c))])

            def proj(dst, dkey, rhs_t, rhs_keys):
                for oc in range(NC):
                    sl = oc % 2
                    for kc in range(NC):
                        lhs = wslot[:, D*kc + 128*oc: D*kc + 128*oc + 128]
                        rhs = rhs_t[:, 512*kc:512*kc+512]
                        p.pe(lambda e, lhs=lhs, rhs=rhs, kc=kc, sl=sl:
                             e.matmul(ps_l[:, 512*sl:512*sl+512], lhs, rhs,
                                      start=(kc == 0), stop=(kc == NC-1)),
                             r=rhs_keys + [("wslot", ("ld", kc))], w=[(ps_l.name, sl)])
                    p.dve(lambda e, oc=oc, sl=sl: e.tensor_copy(
                        dst[:, 512*oc:512*oc+512], ps_l[:, 512*sl:512*sl+512]),
                          r=[(ps_l.name, sl)], w=[(dkey, oc)])

            for qc in range(4):
                with contextlib.ExitStack() as est:
                    self.transpose_stream(stream, xtc, est, tiles=range(4*qc, 4*qc+4),
                                          xt_cols=512, colbase=-512*qc, ptr_ext=ps_trf,
                                          xb_ext=xbshared)
                xkeys = [(xtc.name, ("tt", tt)) for tt in range(4*qc, 4*qc+4)]
                load_w("wqT", None)
                proj(qTc, "qTc", xtc, xkeys)
                load_w("wkT", None)
                proj(kTc, "kTc", xtc, xkeys)
                load_w("wiT", None)
                proj(iqT, "iqT", qTc, [("qTc", oc) for oc in range(NC)])
                proj(ikT, "ikT", kTc, [("kTc", oc) for oc in range(NC)])
                load_w("r1T", None)
                p.dma("sp", "rcn", lambda e, qc=qc: e.dma_start(
                    out=rc_n[:], in_=self.inp["c_recipn"][:, 512*qc:512*qc+512]),
                      w=["rc_n"])
                qkeysA = [("qTc", oc) for oc in range(NC)]
                self.branch(qTc, kTc, qkeysA, flats["Vf"], flats["Wf"], flats["UfT"],
                            bf, carry, 0, pP, hT, scr1, qc,
                            (ps_x, ps_y, ps_u, ps_l),
                            alpha_col=None, fA=fA, bcA=bcA)
                qkeysB = [("iqT", oc) for oc in range(NC)]
                self.branch(iqT, ikT, qkeysB, flats["Wie"], flats["Vi"], flats["UiT"],
                            bi, carry, 1, pP, hT, scr1, qc,
                            (ps_x, ps_y, ps_u, ps_l),
                            alpha_col=alpha_col, fA=fA, bcA=bcA, totT=totT)
                load_w("woT", None)
                for sub in range(4):
                    tt = 4 * qc + sub
                    for half in range(2):
                        sl = (2 * sub + half) % 2
                        pl = ps_l[:, 512*sl:512*sl+512]
                        for kc in range(NC):
                            lhs = totT[:, 512*kc + 128*sub: 512*kc + 128*sub + 128]
                            rhs = wslot[:, D*kc + 512*half: D*kc + 512*half + 512]
                            p.pe(lambda e, pl=pl, lhs=lhs, rhs=rhs, kc=kc:
                                 e.matmul(pl, lhs, rhs,
                                          start=(kc == 0), stop=(kc == NC-1)),
                                 r=[("fA", kc2) for kc2 in range(NC)]
                                 + [("wslot", ("ld", kc))],
                                 w=[(ps_l.name, sl)])
                        out_sl = stream[:, D*tt + 512*half: D*tt + 512*half + 512]
                        p.dve(lambda e, pl=pl, out_sl=out_sl: e.tensor_add(
                            out_sl, pl, out_sl),
                              r=[(ps_l.name, sl), (stream.name, ("tile", tt))],
                              w=[(stream.name, ("tile", tt))])
                    if sub % 2 == 1:
                        specs = [(stream[:, D*t2:D*t2+D], stream[:, D*t2:D*t2+D],
                                  [(stream.name, ("tile", t2))],
                                  [(stream.name, ("tile", t2))])
                                 for t2 in (tt - 1, tt)]
                        self.ln_pair(specs, scr)
            p.barrier()


from concourse.bass_utils import run_bass_kernel_spmd

L1FFN_SHAPES = {"W_Q": [1024, 1024], "W_K": [1024, 1024], "W_O": [1024, 1024],
                "W_inv": [1024, 1024], "V_fwd": [8, 1024, 64], "W_fwd": [8, 1024, 64],
                "U_fwd": [8, 1024, 64], "b_fwd": [8, 1024], "V_inv": [8, 1024, 64],
                "W_inv_exp": [8, 1024, 64], "U_inv": [8, 1024, 64], "b_inv": [8, 1024],
                "router_w1": [1024, 1024], "router_b1": [1024], "router_w2": [8, 1024],
                "router_b2": [8], "alpha_bi": [1, 1], "expert_bias": [8]}


def _build_fused():
    nc = bass.Bass("TRN2", target_bir_lowering=False, debug=False)
    core = Core(nc)
    x = core.dram_in("x", [T, D])
    core.dram_in("L0_attn_Wqk", [1024, 2048])
    core.dram_in("L0_attn_bqk", [2048])
    core.dram_in("L0_attn_Wv", [1024, 1024])
    core.dram_in("L0_attn_bv", [1024])
    core.dram_in("L0_ffn_w_gate", [4096, 1024])
    core.dram_in("L0_ffn_w_up", [4096, 1024])
    core.dram_in("L0_ffn_w_down", [1024, 4096])
    core.dram_in("L1_attn_Wq", [1024, 1024])
    core.dram_in("L1_attn_Wkv", [1024, 128])
    core.dram_in("L1_attn_Wo", [1024, 1024])
    for n2, shp in L1FFN_SHAPES.items():
        core.dram_in("L1_ffn_" + n2, shp)
    y = core.dram_out("y", [T, D])
    core.load_constants()
    x1sp = core.dram_tmp("x1sp", [T, D], F32)
    rh_sp = core.dram_tmp("rh_sp", [NT, 128, 4096], BF16)
    wuT_sp = core.dram_tmp("wuT_sp", [32, 128, 1024], BF16)
    wdT_sp = core.dram_tmp("wdT_sp", [32, 128, 1024], BF16)
    scd0 = core.sb("scd_l", [128, NT], F32)
    mwd = core.sb("mwd", [128, 1], F32)
    qs = core.make_l0_wquants(None, wuT_sp, wdT_sp, None)
    with contextlib.ExitStack() as es_r:
        with contextlib.ExitStack() as es1:
            stream1 = core.sb("stream1", [128, NT * D], F32, es1)
            core.load_stream(x, stream1)

            def aux_alloc(es_attn):
                mjunk = core.sb("wmjks", [128, 1024], BF16, es_attn)
                for q in qs:
                    q.alloc_mean(es_attn, junk_ext=mjunk)

            def aux_step(h):
                for q in qs:
                    q.mean_chunk(2*h)
                    q.mean_chunk(2*h+1)
            core.emit_l0_attn(stream1, aux_alloc, aux_step)
            rt = core.sb("rt", [128, NC * T], BF16, es_r, side="right")
            mraw = core.sb("mraw", [128, NT], F32, es_r, side="right")
            core.store_stream(stream1, x1sp, wreg="x1sp_r")
        core.p.barrier()
        with contextlib.ExitStack() as es_f:
            wgT = core.sb("wgT", [128, 8 * 4096], BF16, es_f)
            mwg = core.sb("mwg", [128, 1], F32, es_f)
            mwu = core.sb("mwu", [128, 1], F32, es_f)
            qs[0].wT = wgT
            with contextlib.ExitStack() as es_A:
                core.emit_l0_ffn_phaseA(x1sp, rt, mraw, qs, mwg, mwu, mwd,
                                        es_A, means_done=True)
            core.p.barrier()
            wuT = core.sb("wuT", [128, 8 * 4096], BF16, es_f)
            core.emit_l0_ffn_part2(rt, mraw, rh_sp, scd0, wgT, wuT, wuT_sp,
                                   mwg, mwu)
        core.p.barrier()
    stream2 = core.sb("stream2", [128, NT * D], F32)
    core.emit_l0_ffn_part3(x1sp, rh_sp, scd0, stream2, wdT_sp, mwd)
    core.emit_l1_attn(stream2)
    core.emit_l1_ffn(stream2)
    core.store_stream(stream2, y)
    core.p.barrier()
    core.p.finalize()
    return nc


def kernel(**inputs):
    f = np.float32
    consts = host_constants()
    xfull = np.ascontiguousarray(np.asarray(inputs["L0_attn_x"], f))
    n = 8
    shared = {}
    for k in ["L0_attn_Wqk", "L0_attn_bqk", "L0_attn_Wv", "L0_attn_bv",
              "L0_ffn_w_gate", "L0_ffn_w_up", "L0_ffn_w_down",
              "L1_attn_Wq", "L1_attn_Wkv", "L1_attn_Wo"]:
        shared[k] = np.ascontiguousarray(np.asarray(inputs[k], f))
    for nme in L1FFN_SHAPES:
        v = np.asarray(inputs["L1_ffn_" + nme], f)
        if nme == "alpha_bi":
            v = v.reshape(1, 1)
        shared["L1_ffn_" + nme] = np.ascontiguousarray(v)
    shared.update(consts)
    in_maps = [{"x": np.ascontiguousarray(xfull[c]), **shared} for c in range(n)]
    nc = _build_fused()
    res = run_bass_kernel_spmd(nc, in_maps, core_ids=list(range(n)))
    return np.stack([np.asarray(res.results[c]["y"], f) for c in range(n)], axis=0)



# version 27
# speedup vs baseline: 1.0101x; 1.0101x over previous
import contextlib

CE = ("pe", "act", "dve", "pool", "sp")
DQ = ("sp", "pool", "act")


def _conflict(tag_a, tag_b):
    return tag_a is None or tag_b is None or tag_a == tag_b


class _Region:
    __slots__ = ("w", "r")
    def __init__(self):
        self.w = {}
        self.r = {}


class Prog:
    def __init__(self, nc):
        self.nc = nc
        self.ops = {e: [] for e in CE}
        self.marks = {e: set() for e in CE}
        self.ch_cnt = {}
        self.regions = {}
        self.wm = {e: {} for e in CE}

    def _reg(self, name):
        return self.regions.setdefault(name, _Region())

    def _covered(self, eng, tok):
        kind, key, val = tok
        return self.wm[eng].get(("c", key) if kind == "c" else ("d", key), -1) >= val

    def _note(self, eng, tok):
        kind, key, val = tok
        k = ("c", key) if kind == "c" else ("d", key)
        self.wm[eng][k] = max(self.wm[eng].get(k, -1), val)
        if kind == "c":
            self.marks[key].add(val)

    def op(self, eng, emit, r=(), w=(), ch=None, tag=None):
        def norm(lst):
            return [((it, None) if isinstance(it, str) else tuple(it)) for it in lst]
        r, w = norm(r), norm(w)
        deps = []
        for name, t in r:
            reg = self._reg(name)
            for t2, tok in reg.w.items():
                if _conflict(t, t2):
                    deps.append(tok)
        for name, t in w:
            reg = self._reg(name)
            for t2, tok in reg.w.items():
                if _conflict(t, t2):
                    deps.append(tok)
            for t2, toks in reg.r.items():
                if _conflict(t, t2):
                    deps.extend(toks)
        if ch is not None and ch in self.ch_cnt:
            deps.append(("d", ch, self.ch_cnt[ch]))
        waits = []
        for tok in deps:
            kind, key, val = tok
            if kind == "c" and key == eng == "pe":
                continue
            if self._covered(eng, tok):
                continue
            waits.append(tok)
            self._note(eng, tok)
        idx = len(self.ops[eng])
        if ch is not None:
            self.ch_cnt[ch] = self.ch_cnt.get(ch, 0) + 16
            tok = ("d", ch, self.ch_cnt[ch])
        else:
            tok = ("c", eng, idx)
        self.ops[eng].append({"emit": emit, "waits": waits, "ch": ch, "tag": tag})
        for name, t in w:
            reg = self._reg(name)
            if t is None:
                reg.w = {}
                reg.r = {}
            else:
                reg.w.pop(t, None)
                reg.r.pop(t, None)
            reg.w[t] = tok
        for name, t in r:
            self._reg(name).r.setdefault(t, []).append(tok)
        return tok

    def pe(self, emit, r=(), w=(), tag=None):
        return self.op("pe", emit, r, w, tag=tag)
    def act(self, emit, r=(), w=(), tag=None):
        return self.op("act", emit, r, w, tag=tag)
    def dve(self, emit, r=(), w=(), tag=None):
        return self.op("dve", emit, r, w, tag=tag)
    def pool(self, emit, r=(), w=(), tag=None):
        return self.op("pool", emit, r, w, tag=tag)
    def dma(self, q, ch, emit, r=(), w=(), tag=None):
        assert q in DQ
        return self.op(q, emit, r, w, ch=ch, tag=tag)

    def barrier(self):
        toks = []
        for e in CE:
            for i in range(len(self.ops[e]) - 1, -1, -1):
                o = self.ops[e][i]
                if o["emit"] is not None and o["ch"] is None:
                    toks.append(("c", e, i))
                    break
        for chn, c in self.ch_cnt.items():
            toks.append(("d", chn, c))
        for e in CE:
            waits = []
            for tok in toks:
                kind, key, val = tok
                if kind == "c" and key == e:
                    continue
                if self._covered(e, tok):
                    continue
                waits.append(tok)
                self._note(e, tok)
            if waits:
                self.ops[e].append({"emit": None, "waits": waits, "ch": None,
                                    "tag": "barrier"})

    def finalize(self):
        nc = self.nc
        with contextlib.ExitStack() as es:
            sems = {}
            for s in CE:
                if self.marks[s]:
                    sems[s] = es.enter_context(nc.semaphore("sem_" + s))
            chsems = {c: es.enter_context(nc.semaphore("sch_" + c))
                      for c in self.ch_cnt}
            rank = {}
            for e in CE:
                ms = sorted(self.marks[e])
                rank[e] = {m: i + 1 for i, m in enumerate(ms)}

            block = es.enter_context(nc.Block())
            meths = {"sp": block.sync, "pe": block.tensor, "act": block.scalar,
                     "dve": block.vector, "pool": block.gpsimd}
            for ename in CE:
                def play(eng, ename=ename):
                    for i, o in enumerate(self.ops[ename]):
                        for kind, key, val in o["waits"]:
                            if kind == "d":
                                eng.wait_ge(chsems[key], val)
                            else:
                                eng.wait_ge(sems[key], rank[key][val])
                        if o["emit"] is None:
                            continue
                        instr = o["emit"](eng)
                        if o["ch"] is not None:
                            instr.then_inc(chsems[o["ch"]], 16)
                        elif i in self.marks[ename]:
                            instr.then_inc(sems[ename], 1)
                meths[ename](play)

    def stats(self):
        n = {e: len(self.ops[e]) for e in CE}
        waits = sum(len(o["waits"]) for e in CE for o in self.ops[e])
        return (f"ops={n} waits={waits} "
                f"marks={ {e: len(m) for e, m in self.marks.items()} } "
                f"nch={len(self.ch_cnt)}")


"""Megakernel emitters for mk_seq_postln_d2 (d_model 1024, T 2048/core).

Per-core layout conventions:
- residual stream: token-major fp32 SBUF [128, 16*1024]; tile t at cols
  [1024*t, 1024*t+1024); stream[p, 1024*t+d] = x[128*t+p, d].
- feature-major activations X^T: bf16 [128, 8*2048]; chunk c at cols
  [2048*c, ...): xt[p, 2048*c+tok] = x[tok, 128*c+p].
- matmul: out = lhsT.T @ rhs; contraction on partitions.
"""
import contextlib
import numpy as np
import ml_dtypes
import concourse.bass as bass
import concourse.mybir as mybir

F32 = mybir.dt.float32
I32 = mybir.dt.int32
BF16 = mybir.dt.bfloat16
AF = mybir.ActivationFunctionType
ALU = mybir.AluOpType
AX = mybir.AxisListType

T = 2048
NT = 16
D = 1024
NC = 8


def host_constants():
    c = {}
    c["c_ident"] = np.eye(128, dtype=ml_dtypes.bfloat16)
    c["c_ones"] = np.ones((2, 128), dtype=ml_dtypes.bfloat16)
    c["c_onesf"] = np.ones((128, 2), dtype=np.float32)
    a4 = np.arange(0, 64, 2, dtype=np.float32) / np.int32(64)
    inv4 = (1.0 / (10000.0 ** a4.astype(np.float64))).astype(np.float32)
    freqs4 = (np.arange(T, dtype=np.float32)[:, None] * inv4[None, :]).astype(np.float32)
    pos = np.concatenate([freqs4, freqs4], axis=1).astype(np.float64)
    cosT = np.ascontiguousarray(np.cos(pos).T)
    sinT = np.ascontiguousarray(np.sin(pos).T)
    sinT = np.concatenate([sinT[32:64], sinT[0:32]], axis=0)
    c["c_cos"] = cosT.astype(np.float32)
    c["c_cosl"] = (cosT - c["c_cos"].astype(np.float64)).astype(np.float32)
    c["c_sin"] = sinT.astype(np.float32)
    c["c_sinl"] = (sinT - c["c_sin"].astype(np.float64)).astype(np.float32)
    tri = (np.arange(128)[:, None] <= np.arange(128)[None, :])
    c["c_tri"] = tri.astype(ml_dtypes.bfloat16)
    c["c_recipn"] = np.tile(1.0 / np.arange(1, T + 1, dtype=np.float32)[None, :],
                            (128, 1))
    c["c_onesfr"] = np.ones((1, 128), dtype=np.float32)
    em = np.zeros((8, 4 * 128), dtype=ml_dtypes.bfloat16)
    for krc in range(4):
        for j in range(128):
            em[2 * krc + j // 64, 128 * krc + j] = 1
    c["c_emat"] = em
    return c


CONST_SPECS = [("c_ident", [128, 128], BF16), ("c_ones", [2, 128], BF16),
               ("c_emat", [8, 512], BF16),
               ("c_onesf", [128, 2], F32), ("c_onesfr", [1, 128], F32),
               ("c_cos", [64, T], F32), ("c_cosl", [64, T], F32),
               ("c_sin", [64, T], F32), ("c_sinl", [64, T], F32),
               ("c_tri", [128, 128], BF16),
               ("c_recipn", [128, T], F32)]



class WQuant:
    def __init__(self, core, wname, hbm, R, C, es, wT=None, sink=None,
                 csplit=None, q="sp"):
        self.core, self.p, self.u = core, core.p, core.uid()
        self.q = q
        cw = csplit or C
        self.R, self.C, self.cw = R, C, cw
        self.nrc, self.npc = R // 128, C // cw
        self.nch = self.nrc * self.npc
        self.nic = cw // 128
        assert self.nic <= 16
        self.wT, self.sink = wT, sink
        self.sinkname = wname + "_sink"
        u = self.u
        sb = core.sb
        self.macc = sb(f"macc{u}", [128, self.nch], F32, es)
        self.m1 = sb(f"m1{u}", [128, 1], F32, es)
        self.thr = sb(f"thr{u}", [128, 3], F32, es)
        self.msb = sb(f"msb{u}", [1, 1], F32, es)
        self.hbr = hbm.rearrange("(r p) c -> p r c", p=128)

    def alloc_mean(self, es, junk_ext=None):
        u, cw, sb = self.u, self.cw, self.core.sb
        self.mwch = sb(f"wmch{u}", [128, 2 * cw], F32, es)
        self.mjunk = junk_ext if junk_ext is not None else sb(
            f"wmjk{u}", [128, cw], BF16, es)

    def alloc_comp(self, es, ps_m=None):
        u, cw, sb, ps = self.u, self.cw, self.core.sb, self.core.ps
        self.wchunk = sb(f"wch{u}", [128, 2 * cw], F32, es)
        self.ge = sb(f"ge{u}", [128, 2 * cw], BF16, es)
        self.sg = sb(f"sg{u}", [128, 2 * cw], BF16, es)
        if self.sink is not None:
            self.stg = sb(f"wst{u}", [128, 2 * cw], BF16, es)
        self.ps_m = ps_m if ps_m is not None else ps(f"ps_wm{u}", [128, 512], F32, es)
        self.ps_t = ps(f"ps_w{u}", [128, cw], BF16, es)

    def _src(self, i):
        rc, pc = divmod(i, self.npc)
        return self.hbr[:, rc, self.cw * pc: self.cw * pc + self.cw]

    def mean_chunk(self, i):
        p, u = self.p, self.u
        s = i % 2
        wsl = self.mwch[:, self.cw * s: self.cw * s + self.cw]
        src = self._src(i)
        p.dma(self.q, f"wm{u}_{s}", lambda e, wsl=wsl, src=src: e.dma_start(
            out=wsl, in_=src), w=[(self.mwch.name, s)])
        p.act(lambda e, wsl=wsl, i=i: e.activation(
            self.mjunk[:], wsl, AF.Abs, accum_out=self.macc[:, i:i+1]),
              r=[(self.mwch.name, s)], w=[(self.macc.name, i), self.mjunk.name])

    def calc_thr(self, m_out=None, mkey=None):
        p, core = self.p, self.core
        ps_m = self.ps_m
        p.dve(lambda e: e.reduce_sum(self.m1[:], self.macc[:, 0:self.nch], axis=AX.X),
              r=[(self.macc.name, i) for i in range(self.nch)], w=[self.m1.name])
        p.pe(lambda e: e.matmul(ps_m[0:1, 0:1], core.t["c_onesf"][:, 0:1], self.m1[:],
                                start=True, stop=True),
             r=[self.m1.name, "c_onesf"], w=[(ps_m.name, "s")])
        p.dve(lambda e: e.tensor_scalar(self.msb[:], ps_m[0:1, 0:1],
                                        1.0 / (self.R * self.C), 1e-5,
                                        ALU.mult, ALU.max),
              r=[(ps_m.name, "s")], w=[self.msb.name])
        p.pe(lambda e: e.matmul(ps_m[:, 2:3], core.t["c_onesfr"][0:1, :],
                                self.msb[0:1, 0:1], start=True, stop=True),
             r=[self.msb.name, "c_onesfr"], w=[(ps_m.name, "bc")])
        p.dve(lambda e: e.tensor_scalar(self.thr[:, 0:1], ps_m[:, 2:3], 0.5, None,
                                        ALU.mult),
              r=[(ps_m.name, "bc")], w=[(self.thr.name, 0)])
        p.dve(lambda e: e.tensor_copy(self.thr[:, 1:2], ps_m[:, 2:3]),
              r=[(ps_m.name, "bc")], w=[(self.thr.name, 1)])
        p.dve(lambda e: e.tensor_scalar(self.thr[:, 2:3], ps_m[:, 2:3], -0.5, None,
                                        ALU.mult),
              r=[(ps_m.name, "bc")], w=[(self.thr.name, 2)])
        if m_out is not None:
            p.dve(lambda e: e.tensor_copy(m_out, self.thr[:, 1:2]),
                  r=[(self.thr.name, 1)], w=[mkey])

    def comp_load(self, i):
        if i >= self.nch:
            return
        p, u = self.p, self.u
        s = i % 2
        cw = self.cw
        wsl = self.wchunk[:, cw * s: cw * s + cw]
        src = self._src(i)
        p.dma(self.q, f"wl{u}_{s}", lambda e, wsl=wsl, src=src: e.dma_start(
            out=wsl, in_=src), w=[(self.wchunk.name, s)])

    def comp_calc(self, i):
        p, u = self.p, self.u
        s = i % 2
        cw, nic = self.cw, self.nic
        wsl = self.wchunk[:, cw * s: cw * s + cw]
        gesl = self.ge[:, cw * s: cw * s + cw]
        sgsl = self.sg[:, cw * s: cw * s + cw]
        p.dve(lambda e, wsl=wsl, gesl=gesl: e.tensor_single_scalar(
            gesl, wsl, self.thr[:, 0:1], ALU.is_ge),
              r=[(self.wchunk.name, s), (self.thr.name, 0)], w=[(self.ge.name, s)])
        p.dve(lambda e, wsl=wsl, sgsl=sgsl: e.tensor_single_scalar(
            sgsl, wsl, self.thr[:, 2:3], ALU.is_le),
              r=[(self.wchunk.name, s), (self.thr.name, 2)], w=[(self.sg.name, s)])
        p.dve(lambda e, gesl=gesl, sgsl=sgsl: e.tensor_sub(gesl, gesl, sgsl),
              r=[(self.sg.name, s)], w=[(self.ge.name, s)])
        for j in range(nic):
            p.pe(lambda e, gesl=gesl, j=j, s=s: e.transpose(
                self.ps_t[:, 128*j:128*j+128], gesl[:, 128*j:128*j+128],
                self.core.t["c_ident"][:]),
                 r=[(self.ge.name, s), "c_ident"], w=[(self.ps_t.name, j)])
        rks = [(self.ps_t.name, j) for j in range(nic)]
        rc, pc = divmod(i, self.npc)
        psrc = self.ps_t[:, 0:128*nic].rearrange("p (ic r) -> p ic r", r=128)
        if self.sink is None:
            dst = self.wT[:].rearrange("p (ic r) -> p ic r", r=self.R)[
                :, nic*pc:nic*pc+nic, 128*rc:128*rc+128]
            if i % 2 == 0:
                p.dve(lambda e, dst=dst: e.tensor_copy(dst, psrc),
                      r=rks, w=[(self.wT.name, ("rc", i))])
            else:
                p.act(lambda e, dst=dst: e.activation(dst, psrc, AF.Copy),
                      r=rks, w=[(self.wT.name, ("rc", i))])
        else:
            stg_sl = self.stg[:, cw*s:cw*s+cw]
            if i % 2 == 0:
                p.dve(lambda e, stg_sl=stg_sl: e.tensor_copy(stg_sl, psrc),
                      r=rks, w=[(self.stg.name, s)])
            else:
                p.act(lambda e, stg_sl=stg_sl: e.activation(stg_sl, psrc, AF.Copy),
                      r=rks, w=[(self.stg.name, s)])
            p.dma(self.q, f"ws{u}", lambda e, stg_sl=stg_sl, i=i: e.dma_start(
                out=self.sink[i], in_=stg_sl),
                  r=[(self.stg.name, s)], w=[(self.sinkname, i)])


class Core:
    def __init__(self, nc):
        self.nc = nc
        self.p = Prog(nc)
        self.es = contextlib.ExitStack()
        self.t = {}
        self.inp = {}
        self.out = {}
        self._uid = 0

    def dram_in(self, name, shape, dtype=F32):
        self.inp[name] = self.nc.dram_tensor(name, shape, dtype,
                                             kind="ExternalInput").ap()
        return self.inp[name]

    def dram_out(self, name, shape, dtype=F32):
        self.out[name] = self.nc.dram_tensor(name, shape, dtype,
                                             kind="ExternalOutput").ap()
        return self.out[name]

    def dram_tmp(self, name, shape, dtype):
        return self.nc.dram_tensor(name, shape, dtype, kind="Internal").ap()

    def sb(self, name, shape, dtype, es=None, side=None):
        kw = {"side": side} if side else {}
        h = (es or self.es).enter_context(self.nc.sbuf_tensor(name, shape, dtype, **kw))
        self.t[name] = h
        return h

    def ps(self, name, shape, dtype=F32, es=None):
        h = (es or self.es).enter_context(self.nc.psum_tensor(name, shape, dtype))
        self.t[name] = h
        return h

    def uid(self):
        self._uid += 1
        return self._uid

    def load_constants(self, names=("c_ident", "c_ones", "c_onesf", "c_onesfr", "c_emat")):
        p = self.p
        for name, shape, dt in CONST_SPECS:
            if name not in names:
                self.dram_in(name, shape, dt)
                continue
            ap = self.dram_in(name, shape, dt)
            t = self.sb(name + "_s", shape, dt)
            self.t[name] = t
            p.dma("sp", "const", lambda e, t=t, ap=ap: e.dma_start(out=t[:], in_=ap),
                  w=[name])
        eps = self.sb("c_eps", [128, 1], F32)
        p.dve(lambda e: e.memset(eps[:], 1e-5), w=["c_eps"])
        half = self.sb("c_half", [128, 1], F32)
        p.dve(lambda e: e.memset(half[:], 0.5), w=["c_half"])

    def load_const_named(self, name, es):
        spec = {n: (s, d) for n, s, d in CONST_SPECS}[name]
        t = self.sb(name + "_s", spec[0], spec[1], es)
        self.t[name] = t
        self.p.dma("sp", "const2", lambda e: e.dma_start(out=t[:], in_=self.inp[name]),
                   w=[name])
        return t

    def load_stream(self, x_hbm, stream):
        p = self.p
        xr = x_hbm.rearrange("(t p) d -> p t d", p=128)
        for g in range(4):
            dst = stream[:].rearrange("p (t d) -> p t d", d=D)[:, 4*g:4*g+4, :]
            p.dma("sp", f"ldst{g % 2}",
                  lambda e, dst=dst, g=g: e.dma_start(out=dst, in_=xr[:, 4*g:4*g+4, :]),
                  w=[(stream.name, ("tile", tt)) for tt in range(4*g, 4*g+4)])

    def store_stream(self, stream, y_hbm, wreg=None):
        p = self.p
        yr = y_hbm.rearrange("(t p) d -> p t d", p=128)
        for g in range(4):
            src = stream[:].rearrange("p (t d) -> p t d", d=D)[:, 4*g:4*g+4, :]
            p.dma("sp", f"stst{g % 2}",
                  lambda e, src=src, g=g: e.dma_start(out=yr[:, 4*g:4*g+4, :], in_=src),
                  r=[(stream.name, ("tile", tt)) for tt in range(4*g, 4*g+4)],
                  w=[] if wreg is None else [(wreg, g)])

    def transpose_stream(self, stream, xt, es, src_bf16=None, tiles=range(NT),
                         xt_cols=None, colbase=0, ptr_ext=None, xb_ext=None,
                         cast_eng="dve"):
        p = self.p
        u = self.uid()
        ptr = ptr_ext if ptr_ext is not None else self.ps(f"ps_tr{u}", [128, 2048], BF16, es)
        xb = xb_ext
        xbtag_none = xb_ext is not None
        if src_bf16 is None and xb is None:
            xb = self.sb(f"xb{u}", [128, 2 * D], BF16, es)
        for t in tiles:
            s = t % 2
            if src_bf16 is None:
                xbs = xb[:, D*s:D*s+D]
                ceng = p.pool if cast_eng == "pool" else p.dve
                ceng(lambda e, xbs=xbs, t=t: e.tensor_copy(xbs, stream[:, D*t:D*t+D]),
                     r=[(stream.name, ("tile", t))],
                     w=[(xb.name, None if xbtag_none else s)])
                src_name = xb.name
                src_tag = None if xbtag_none else s
            else:
                xbs = src_bf16[:, D*t:D*t+D]
                src_name = src_bf16.name
                src_tag = ("tile", t)
            for c in range(NC):
                p.pe(lambda e, xbs=xbs, c=c, s=s:
                     e.transpose(ptr[:, 1024*s+128*c:1024*s+128*c+128],
                                 xbs[:, 128*c:128*c+128], self.t["c_ident"][:]),
                     r=[(src_name, src_tag), "c_ident"], w=[(ptr.name, (s, c))])
            qw = xt_cols if xt_cols is not None else T
            dst = xt[:].rearrange("p (c q) -> p c q", q=qw)[:, :, colbase+128*t:colbase+128*t+128]
            src = ptr[:, 1024*s:1024*s+1024].rearrange("p (c q) -> p c q", q=128)
            rks = [(ptr.name, (s, c)) for c in range(NC)]
            if t % 2 == 0:
                p.dve(lambda e, dst=dst, src=src: e.tensor_copy(dst, src),
                      r=rks, w=[(xt.name, ("tt", t))])
            else:
                p.act(lambda e, dst=dst, src=src: e.activation(dst, src, AF.Copy),
                      r=rks, w=[(xt.name, ("tt", t))])

    def ln_stages(self, r_sl, out_sl, keys_r, keys_w, scr, s=0):
        p = self.p
        u = self.uid()
        sx, sq, mt, vt, junk = scr["sx"], scr["sq"], scr["mt"], scr["vt"], scr["junk"]
        sxs, sqs = sx[:, s:s+1], sq[:, s:s+1]
        mts, vts = mt[:, s:s+1], vt[:, s:s+1]
        return [
            lambda: p.dve(lambda e: e.reduce_sum(sxs, r_sl, axis=AX.X),
                          r=keys_r, w=[(sx.name, (u, s))]),
            lambda: p.act(lambda e: e.activation(junk[:], r_sl, AF.Square,
                                                 accum_out=sqs),
                          r=keys_r, w=[(sq.name, (u, s)), junk.name]),
            lambda: p.dve(lambda e: e.tensor_scalar(mts, sxs, 1.0 / D, None,
                                                    ALU.mult),
                          r=[(sx.name, (u, s))], w=[(mt.name, (u, s))]),
            lambda: p.dve(lambda e: e.scalar_tensor_tensor(vts, sqs, 1.0 / D, sqs,
                                                           ALU.mult, ALU.bypass),
                          r=[(sq.name, (u, s))], w=[(vt.name, (u, s, 0))]),
            lambda: p.dve(lambda e: e.scalar_tensor_tensor(vts, mts, mts, vts,
                                                           ALU.mult, ALU.subtract),
                          r=[(mt.name, (u, s)), (vt.name, (u, s, 0))],
                          w=[(vt.name, (u, s, 1))]),
            lambda: p.act(lambda e: e.activation(vts, vts, AF.Ln, scale=-1.0,
                                                 bias=self.t["c_eps"][:, 0:1]),
                          r=[(vt.name, (u, s, 1)), "c_eps"], w=[(vt.name, (u, s, 2))]),
            lambda: p.act(lambda e: e.activation(vts, vts, AF.Exp, scale=-0.5),
                          r=[(vt.name, (u, s, 2))], w=[(vt.name, (u, s, 3))]),
            lambda: p.dve(lambda e: e.tensor_scalar(out_sl, r_sl, mts, vts,
                                                    ALU.subtract, ALU.mult),
                          r=keys_r + [(mt.name, (u, s)), (vt.name, (u, s, 3))],
                          w=keys_w)]

    def ln_tile(self, r_sl, out_sl, keys_r, keys_w, scr, s=0):
        for f in self.ln_stages(r_sl, out_sl, keys_r, keys_w, scr, s):
            f()

    def ln_pair(self, specs, scr):
        chains = [self.ln_stages(r, o, kr, kw, scr, s=i)
                  for i, (r, o, kr, kw) in enumerate(specs)]
        for stg in range(8):
            for ch in chains:
                ch[stg]()

    def ln_scratch(self, pref, es):
        return {"sx": self.sb(pref + "_sx", [128, 2], F32, es),
                "sq": self.sb(pref + "_sq", [128, 2], F32, es),
                "mt": self.sb(pref + "_mt", [128, 2], F32, es),
                "vt": self.sb(pref + "_vt", [128, 2], F32, es),
                "junk": self.sb(pref + "_junk", [128, D], BF16, es)}

    def emit_l0_attn(self, stream, aux_alloc=None, aux_step=None):
        p = self.p
        with contextlib.ExitStack() as es:
            if aux_alloc is not None:
                aux_alloc(es)
            xt = self.sb("xt0", [128, NC * T], BF16, es)
            with contextlib.ExitStack() as es2:
                self.transpose_stream(stream, xt, es2)
                p.barrier()
            vaug = self.sb("vaug", [128, NT * 1040], BF16, es)
            wv = self.sb("wv", [128, NC * D], BF16, es)
            bvr = self.sb("bvr", [1, D], BF16, es)
            ps_a = self.ps("ps_a", [128, 512], F32, es)
            ps_s = self.ps("ps_s", [128, 2048], F32, es)
            ps_o = self.ps("ps_o", [128, 1024], F32, es)
            ps_tb = self.ps("ps_tb", [128, 1024], BF16, es)
            wvr = self.inp["L0_attn_Wv"].rearrange("(c p) o -> p c o", p=128)
            p.dma("pool", "wv", lambda e: e.dma_start(
                out=wv[:].rearrange("p (c o) -> p c o", o=D), in_=wvr),
                w=[wv.name])
            p.dma("pool", "bv", lambda e: e.dma_start(
                out=bvr[:], in_=self.inp["L0_attn_bv"].rearrange("(a o) -> a o", a=1)),
                w=[bvr.name])
            vg = vaug[:].rearrange("p (k h) -> p k h", h=65)
            for kt in range(NT):
                p.pool(lambda e, kt=kt: e.memset(vg[:, 16*kt:16*kt+16, 64:65], 1.0),
                       w=[(vaug.name, ("ones", kt))])
                for half in range(2):
                    for kc in range(NC):
                        lhs = xt[:].rearrange("p (c q) -> p c q", q=T)[:, kc, 128*kt:128*kt+128]
                        rhs = wv[:, D*kc + 512*half: D*kc + 512*half + 512]
                        p.pe(lambda e, lhs=lhs, rhs=rhs, kc=kc:
                             e.matmul(ps_a[:], lhs, rhs, start=(kc == 0), stop=False),
                             r=[(xt.name, ("tt", kt)), wv.name], w=[(ps_a.name, 0)])
                    p.pe(lambda e, half=half:
                         e.matmul(ps_a[:], self.t["c_ones"][0:1, :],
                                  bvr[0:1, 512*half:512*half+512],
                                  start=False, stop=True),
                         r=["c_ones", bvr.name], w=[(ps_a.name, 0)])
                    dst = vg[:, 16*kt + 8*half:16*kt + 8*half + 8, 0:64]
                    src = ps_a[:].rearrange("p (h d) -> p h d", d=64)
                    if (2 * kt + half) % 2 == 0:
                        p.dve(lambda e, dst=dst, src=src: e.tensor_copy(dst, src),
                              r=[(ps_a.name, 0)], w=[(vaug.name, ("v", kt, half))])
                    else:
                        p.act(lambda e, dst=dst, src=src: e.activation(dst, src, AF.Copy),
                              r=[(ps_a.name, 0)], w=[(vaug.name, ("v", kt, half))])
            qT = self.sb("qT", [64, 2 * T], BF16, es)
            kT = self.sb("kT", [64, 2 * T], BF16, es)
            wqk = self.sb("wqk", [128, 2 * NC * 128], BF16, es)
            bq = self.sb("bq", [128, 16], F32, es)
            for h in range(16):
                p.dma("sp", f"bq{h % 2}", lambda e, h=h: e.dma_start(
                    out=bq[:, h:h+1],
                    in_=self.inp["L0_attn_bqk"].rearrange("(a p) -> p a", p=128)[:, h:h+1]),
                    w=[(bq.name, h)])
            pT = self.sb("pT", [128, 2 * 1024], BF16, es)
            ssum = self.sb("ssum", [1, 512], F32, es)
            oT = self.sb("oT", [64, 512], BF16, es)
            rcol = self.sb("rcol", [128, 4], F32, es)
            wqkr = self.inp["L0_attn_Wqk"].rearrange("(c p) m -> p c m", p=128)
            for h in range(16):
                hs = h % 2
                wsl = wqk[:].rearrange("p (s c m) -> p s c m", s=2, m=128)[:, hs]
                p.dma("pool", f"wqk{hs}", lambda e, wsl=wsl, h=h: e.dma_start(
                    out=wsl, in_=wqkr[:, :, 128*h:128*h+128]),
                    w=[(wqk.name, hs)])
                for qc in range(4):
                    for kc in range(NC):
                        lhs = wsl[:, kc, :]
                        rhs = xt[:].rearrange("p (c q) -> p c q", q=T)[:, kc, 512*qc:512*qc+512]
                        p.pe(lambda e, lhs=lhs, rhs=rhs, kc=kc:
                             e.matmul(ps_a[:], lhs, rhs, start=(kc == 0), stop=(kc == NC - 1)),
                             r=[(wqk.name, hs)] + [(xt.name, ("tt", tt))
                                                   for tt in range(4*qc, 4*qc+4)],
                             w=[(ps_a.name, 0)])
                    dstq = qT[:, T*hs + 512*qc: T*hs + 512*qc + 512]
                    p.dve(lambda e, dstq=dstq, h=h: e.tensor_scalar(
                        dstq, ps_a[0:64, :], bq[0:64, h:h+1], None, ALU.add),
                          r=[(ps_a.name, 0), (bq.name, h)], w=[(qT.name, (hs, qc))])
                    dstk = kT[:, T*hs + 512*qc: T*hs + 512*qc + 512]
                    p.act(lambda e, dstk=dstk, h=h: e.activation(
                        dstk, ps_a[64:128, :], AF.Identity, bias=bq[64:128, h:h+1]),
                          r=[(ps_a.name, 0), (bq.name, h)], w=[(kT.name, (hs, qc))])
                qk_keys = [(qT.name, (hs, qcc)) for qcc in range(4)] + \
                          [(kT.name, (hs, qcc)) for qcc in range(4)]
                pend0 = [None]
                for qh in range(2):
                    def emit_S(kt, qh=qh):
                        ss = kt % 2
                        for j in range(2):
                            lhs = kT[:, T*hs + 128*kt: T*hs + 128*kt + 128]
                            rhs = qT[:, T*hs + 1024*qh + 512*j: T*hs + 1024*qh + 512*j + 512]
                            p.pe(lambda e, lhs=lhs, rhs=rhs, ss=ss, j=j:
                                 e.matmul(ps_s[:, 1024*ss+512*j:1024*ss+512*j+512],
                                          lhs, rhs, start=True, stop=True),
                                 r=qk_keys, w=[(ps_s.name, (ss, j))])
                    emit_S(0)
                    if pend0[0] is not None:
                        pend0[0]()
                        pend0[0] = None
                    for kt in range(NT):
                        ss = kt % 2
                        p.act(lambda e, ss=ss:
                              e.activation(pT[:, 1024*ss:1024*ss+1024],
                                           ps_s[:, 1024*ss:1024*ss+1024],
                                           AF.Exp, scale=1.0 / 65536.0),
                              r=[(ps_s.name, (ss, 0)), (ps_s.name, (ss, 1))],
                              w=[(pT.name, ss)])
                        if kt + 1 < NT:
                            emit_S(kt + 1)
                        for j in range(2):
                            lhs = vaug[:, 1040*kt + 65*h: 1040*kt + 65*h + 65]
                            rhs = pT[:, 1024*ss + 512*j: 1024*ss + 512*j + 512]
                            p.pe(lambda e, lhs=lhs, rhs=rhs, j=j, kt=kt:
                                 e.matmul(ps_o[0:65, 512*j:512*j+512], lhs, rhs,
                                          start=(kt == 0), stop=(kt == NT - 1)),
                                 r=[(vaug.name, None), (pT.name, ss)],
                                 w=[(ps_o.name, j)])
                    def ep0(qh=qh):
                        for j in range(2):
                            qc = 2 * qh + j
                            p.dve(lambda e, j=j: e.tensor_scalar(
                                ssum[:], ps_o[64:65, 512*j:512*j+512], 32.0, None, ALU.mult),
                                  r=[(ps_o.name, j)], w=[ssum.name])
                            p.dve(lambda e, j=j: e.tensor_copy(
                                oT[:], ps_o[0:64, 512*j:512*j+512]),
                                  r=[(ps_o.name, j)], w=[oT.name])
                            for k in range(4):
                                p.pe(lambda e, k=k: e.transpose(
                                    ps_a[:, k:k+1], ssum[0:1, 128*k:128*k+128],
                                    self.t["c_onesf"][0:1, 0:1]),
                                     r=[ssum.name, "c_onesf"], w=[(ps_a.name, 0)])
                                p.pe(lambda e, k=k: e.transpose(
                                    ps_tb[:, 64*k:64*k+64], oT[:, 128*k:128*k+128],
                                    self.t["c_ident"][0:64, 0:64]),
                                     r=[oT.name, "c_ident"], w=[(ps_tb.name, k)])
                            p.dve(lambda e: e.reciprocal(rcol[:], ps_a[:, 0:4]),
                                  r=[(ps_a.name, 0)], w=[rcol.name])
                            for k in range(4):
                                tt = 4 * qc + k
                                out_sl = stream[:, 1024*tt + 64*h: 1024*tt + 64*h + 64]
                                p.dve(lambda e, k=k, out_sl=out_sl:
                                      e.scalar_tensor_tensor(
                                          out_sl, ps_tb[:, 64*k:64*k+64],
                                          rcol[:, k:k+1], out_sl, ALU.mult, ALU.add),
                                      r=[(ps_tb.name, k), rcol.name,
                                         (stream.name, ("tile", tt))],
                                      w=[(stream.name, ("tile", tt))])
                    if qh == 0:
                        pend0[0] = ep0
                    else:
                        ep0()
                if aux_step is not None:
                    aux_step(h)
            scr = self.ln_scratch("l0a", es)
            for t0 in range(0, NT, 2):
                specs = [(stream[:, D*t:D*t+D], stream[:, D*t:D*t+D],
                          [(stream.name, ("tile", t))],
                          [(stream.name, ("tile", t))]) for t in (t0, t0 + 1)]
                self.ln_pair(specs, scr)
            p.barrier()

    def quantize_weight_to_T(self, wname, hbm, R, C, wT, es):
        p = self.p
        u = self.uid()
        nrc, nic = R // 128, C // 128
        nbuf = 2 if C <= 1024 else 1
        wchunk = self.sb(f"wch{u}", [128, nbuf * C], F32, es)
        junk = self.sb(f"wjk{u}", [128, C], BF16, es)
        macc = self.sb(f"macc{u}", [128, nrc], F32, es)
        m1 = self.sb(f"m1{u}", [128, 1], F32, es)
        thr = self.sb(f"thr{u}", [128, 3], F32, es)
        nb2 = 2 if C <= 1024 else 1
        ge = self.sb(f"ge{u}", [128, nb2 * C], BF16, es)
        sg = self.sb(f"sg{u}", [128, nb2 * C], BF16, es)
        ps_m = self.ps(f"ps_m{u}", [128, 512], F32, es)
        ps_t = self.ps(f"ps_t{u}", [128, 2048], BF16, es)
        hbr = hbm.rearrange("(r p) c -> p r c", p=128)
        for rc in range(nrc):
            s = rc % nbuf
            wsl = wchunk[:, C*s:C*s+C]
            p.dma("sp", f"wl{u}_{s}", lambda e, wsl=wsl, rc=rc: e.dma_start(
                out=wsl, in_=hbr[:, rc, :]), w=[(wchunk.name, s)])
            p.act(lambda e, wsl=wsl, rc=rc: e.activation(
                junk[:], wsl, AF.Abs, accum_out=macc[:, rc:rc+1]),
                r=[(wchunk.name, s)], w=[(macc.name, rc), junk.name])
        p.dve(lambda e: e.reduce_sum(m1[:], macc[:, 0:nrc], axis=AX.X),
              r=[(macc.name, rc) for rc in range(nrc)], w=[m1.name])
        msb = self.sb(f"msb{u}", [1, 1], F32, es)
        p.pe(lambda e: e.matmul(ps_m[0:1, 0:1], self.t["c_onesf"][:, 0:1], m1[:],
                                start=True, stop=True),
             r=[m1.name, "c_onesf"], w=[(ps_m.name, "s")])
        p.dve(lambda e: e.tensor_scalar(msb[:], ps_m[0:1, 0:1],
                                        1.0 / (R * C), 1e-5, ALU.mult, ALU.max),
              r=[(ps_m.name, "s")], w=[msb.name])
        p.pe(lambda e: e.matmul(ps_m[:, 2:3], self.t["c_onesfr"][0:1, :],
                                msb[0:1, 0:1], start=True, stop=True),
             r=[msb.name, "c_onesfr"], w=[(ps_m.name, "bc")])
        p.dve(lambda e: e.tensor_scalar(thr[:, 0:1], ps_m[:, 2:3], 0.5, None,
                                        ALU.mult),
              r=[(ps_m.name, "bc")], w=[(thr.name, 0)])
        p.dve(lambda e: e.tensor_copy(thr[:, 1:2], ps_m[:, 2:3]),
              r=[(ps_m.name, "bc")], w=[(thr.name, 1)])
        p.dve(lambda e: e.tensor_scalar(thr[:, 2:3], ps_m[:, 2:3], -0.5, None,
                                        ALU.mult),
              r=[(ps_m.name, "bc")], w=[(thr.name, 2)])
        for rc in range(nrc):
            s = rc % nbuf
            s2 = rc % nb2
            wsl = wchunk[:, C*s:C*s+C]
            gesl = ge[:, C*s2:C*s2+C]
            sgsl = sg[:, C*s2:C*s2+C]
            wint = gesl
            p.dma("sp", f"wl{u}_{s}", lambda e, wsl=wsl, rc=rc: e.dma_start(
                out=wsl, in_=hbr[:, rc, :]), w=[(wchunk.name, s)])
            p.dve(lambda e, wsl=wsl, gesl=gesl: e.tensor_single_scalar(
                gesl, wsl, thr[:, 0:1], ALU.is_ge),
                  r=[(wchunk.name, s), (thr.name, 0)], w=[(ge.name, s2)])
            p.dve(lambda e, wsl=wsl, sgsl=sgsl: e.tensor_single_scalar(
                sgsl, wsl, thr[:, 2:3], ALU.is_le),
                  r=[(wchunk.name, s), (thr.name, 2)], w=[(sg.name, s2)])
            p.dve(lambda e, wint=wint, gesl=gesl, sgsl=sgsl: e.tensor_sub(
                wint, gesl, sgsl),
                  r=[(sg.name, s2)], w=[(ge.name, s2)])
            for g in range(nic // 16 if nic >= 16 else 1):
                nb = min(16, nic)
                for icb in range(nb):
                    ic = g * 16 + icb
                    p.pe(lambda e, wint=wint, ic=ic, icb=icb: e.transpose(
                        ps_t[:, 128*icb:128*icb+128], wint[:, 128*ic:128*ic+128],
                        self.t["c_ident"][:]),
                         r=[(ge.name, s2), "c_ident"], w=[(ps_t.name, icb)])
                dst = wT[:].rearrange("p (ic r) -> p ic r", r=R)[:, 16*g:16*g+nb,
                                                                128*rc:128*rc+128]
                eng = p.dve if rc % 2 == 0 else p.act
                if rc % 2 == 0:
                    p.dve(lambda e, dst=dst, nb=nb: e.tensor_copy(
                        dst, ps_t[:, 0:128*nb].rearrange("p (ic r) -> p ic r", r=128)),
                          r=[(ps_t.name, icb) for icb in range(nb)],
                          w=[(wT.name, ("rc", rc, g))])
                else:
                    p.act(lambda e, dst=dst, nb=nb: e.activation(
                        dst, ps_t[:, 0:128*nb].rearrange("p (ic r) -> p ic r", r=128),
                        AF.Copy),
                          r=[(ps_t.name, icb) for icb in range(nb)],
                          w=[(wT.name, ("rc", rc, g))])
        return thr

    def quantize_act_tile(self, xte, xcol, keys_r, rbf, m_out, scr, cols, mkey,
                          nsplit=1, nwide=None, split_keys=None, tail_pool=False):
        p = self.p
        u = self.uid()
        mr = scr["mr"]
        w = cols // nsplit
        x_full = xte[:, xcol:xcol+cols]
        p.dve(lambda e: e.reduce_max(mr[:], x_full, axis=AX.X,
                                     apply_absolute_value=True),
              r=keys_r, w=[(mr.name, (u, 0))])
        p.dve(lambda e: e.tensor_scalar_max(m_out, mr[:], 1e-5),
              r=[(mr.name, (u, 0))], w=[mkey])
        p.dve(lambda e: e.reciprocal(mr[:], m_out),
              r=[mkey], w=[(mr.name, (u, 1))])
        p.dve(lambda e: e.tensor_scalar(mr[:], mr[:], 127.0, None, ALU.mult),
              r=[(mr.name, (u, 1))], w=[(mr.name, (u, 2))])
        umr = u
        half = self.t["c_half"]

        def split_stages(i):
            s = i % 2
            tt = scr["t"][:, w*s:w*s+w]
            dd = scr["d"][:, w*s:w*s+w]
            vv = scr["v"][:, w*s:w*s+w]
            gt = scr["gt"][:, w*s:w*s+w]
            tk, dk, vk, gk = ("t", s), ("d", s), ("v", s), ("g", s)
            x_sl = xte[:, xcol + w*i: xcol + w*i + w]
            kr_i = split_keys[i] if split_keys is not None else keys_r
            ui = self.uid()
            st = [lambda: p.dve(lambda e: e.tensor_scalar(
                      tt, x_sl, mr[:, 0:1], None, ALU.mult),
                      r=kr_i + [(mr.name, (umr, 2))], w=[("qs", tk)]),
                  lambda: p.act(lambda e: e.activation(dd, tt, AF.Copy),
                      r=[("qs", tk)], w=[("qs", dk)]),
                  lambda: p.dve(lambda e: e.scalar_tensor_tensor(
                      vv, dd, -1.0, tt, ALU.mult, ALU.add),
                      r=[("qs", tk), ("qs", dk)], w=[("qs", vk)])]
            for k in (4.0, 2.0, 1.0):
                st.append(lambda: p.act(lambda e: e.activation(
                    gt, vv, AF.Sign, bias=half[:, 0:1]),
                    r=[("qs", vk), "c_half"], w=[("qs", gk)]))
                st.append(lambda k=k: p.dve(lambda e: e.scalar_tensor_tensor(
                    vv, gt, -k, vv, ALU.mult, ALU.add),
                    r=[("qs", gk)], w=[("qs", vk)]))
            st.append(lambda: p.dve(lambda e: e.tensor_single_scalar(
                gt, vv, -0.5, ALU.is_lt),
                r=[("qs", vk)], w=[("qs", gk)]))
            if tail_pool:
                st.append(lambda: p.pool(lambda e: e.tensor_sub(tt, tt, vv),
                    r=[("qs", vk)], w=[("qs", tk)]))
                st.append(lambda: p.pool(lambda e: e.tensor_sub(
                    rbf[:, w*i:w*i+w], tt, gt),
                    r=[("qs", tk), ("qs", gk)], w=[("rbf_out", (ui, i))]))
            else:
                st.append(lambda: p.dve(lambda e: e.scalar_tensor_tensor(
                    tt, vv, -1.0, tt, ALU.mult, ALU.add),
                    r=[("qs", vk)], w=[("qs", tk)]))
                st.append(lambda: p.dve(lambda e: e.tensor_sub(
                    rbf[:, w*i:w*i+w], tt, gt),
                    r=[("qs", tk), ("qs", gk)], w=[("rbf_out", (ui, i))]))
            return st

        for i0 in range(0, nsplit, 2):
            pair = [split_stages(i) for i in range(i0, min(i0 + 2, nsplit))]
            for stg in range(len(pair[0])):
                for sp in pair:
                    sp[stg]()
        return ("rbf_out", None)

    def l0_part1_start(self, x1sp, rt, mraw, es):
        p = self.p
        x1r = x1sp.rearrange("(t p) d -> p t d", p=128)
        x1rb = self.sb("x1rb", [128, 2 * D], F32, es)
        scr = {"mr": self.sb("q1mr", [128, 1], F32, es),
               "t": self.sb("q1t", [128, D], F32, es),
               "d": self.sb("q1d", [128, D], I32, es),
               "gt": self.sb("q1gt", [128, D], BF16, es),
               "v": self.sb("q1v", [128, D], F32, es)}
        rtile = self.sb("q1r", [128, D], BF16, es)
        ps_t = self.ps("q1ps", [128, 1024], BF16, es)

        def load(t):
            if t >= NT:
                return
            s = t % 2
            xsl = x1rb[:, D*s:D*s+D]
            p.dma("pool", f"x1rb{s}", lambda e, xsl=xsl, t=t: e.dma_start(
                out=xsl, in_=x1r[:, t, :]),
                  r=[("x1sp_r", t // 4)], w=[(x1rb.name, s)])

        def tile(t):
            s = t % 2
            self.quantize_act_tile(x1rb, D*s, [(x1rb.name, s)], rtile,
                                   mraw[:, t:t+1], scr, D,
                                   (mraw.name, t), nsplit=2)
            for c in range(NC):
                p.pe(lambda e, c=c: e.transpose(
                    ps_t[:, 128*c:128*c+128], rtile[:, 128*c:128*c+128],
                    self.t["c_ident"][:]),
                     r=[("rbf_out", None), "c_ident"], w=[(ps_t.name, c)])
            dst = rt[:].rearrange("p (c q) -> p c q", q=T)[:, :, 128*t:128*t+128]
            psrc = ps_t[:, 0:1024].rearrange("p (c q) -> p c q", q=128)
            rks = [(ps_t.name, c) for c in range(NC)]
            if t % 2 == 0:
                p.dve(lambda e, dst=dst: e.tensor_copy(dst, psrc),
                      r=rks, w=[(rt.name, ("tt", t))])
            else:
                p.act(lambda e, dst=dst: e.activation(dst, psrc, AF.Copy),
                      r=rks, w=[(rt.name, ("tt", t))])
        return load, tile

    def make_l0_wquants(self, wgT, wuT_sp, wdT_sp, es):
        INNER = 4096
        qg = WQuant(self, "wg", self.inp["L0_ffn_w_gate"], INNER, D, es,
                    wT=wgT, q="sp")
        qu = WQuant(self, "wu", self.inp["L0_ffn_w_up"], INNER, D, es,
                    sink=wuT_sp, q="pool")
        qd = WQuant(self, "wd", self.inp["L0_ffn_w_down"], D, INNER, es,
                    sink=wdT_sp, csplit=1024, q="act")
        return qg, qu, qd

    def emit_l0_ffn_phaseA(self, x1sp, rt, mraw, qs, mwg, mwu, mwd, es,
                           means_done=False):
        qg, qu, qd = qs
        if not means_done:
            with contextlib.ExitStack() as es_m:
                for q in qs:
                    q.alloc_mean(es_m)
                for i in range(32):
                    qg.mean_chunk(i)
                    qu.mean_chunk(i)
                    qd.mean_chunk(i)
            self.p.barrier()
        ps_m = self.ps("ps_mq", [128, 512], F32, es)
        for q in qs:
            q.alloc_comp(es, ps_m=ps_m)
        p1_load, p1 = self.l0_part1_start(x1sp, rt, mraw, es)
        qg.calc_thr(mwg[:, 0:1], mwg.name)
        qu.calc_thr(mwu[:, 0:1], mwu.name)
        qd.calc_thr(mwd[:, 0:1], mwd.name)
        for q in qs:
            q.comp_load(0)
            q.comp_load(1)
        p1_load(0)
        for t in range(NT):
            p1_load(t + 1)
            p1(t)
            for q in qs:
                q.comp_calc(2*t)
                q.comp_load(2*t + 2)
                q.comp_calc(2*t + 1)
                q.comp_load(2*t + 3)

    def emit_l0_ffn_part2(self, rt, mraw, rh_sp, scd, wgT, wuT, wuT_sp,
                          mwg, mwu):
        p = self.p
        INNER = 4096
        with contextlib.ExitStack() as es:
            for rc in range(32):
                dst = wuT[:].rearrange("p (ic r) -> p ic r", r=INNER)[
                    :, :, 128*rc:128*rc+128]
                wsrc = wuT_sp[rc].rearrange("p (ic r) -> p ic r", r=128)
                p.dma("sp", f"wult{rc % 2}", lambda e, dst=dst, wsrc=wsrc:
                      e.dma_start(out=dst, in_=wsrc),
                      r=[("wu_sink", rc)], w=[(wuT.name, ("ld", rc))])
            h = self.sb("hbuf", [128, INNER], F32, es)
            rh = self.sb("rhbuf", [128, INNER], BF16, es)
            scg = self.sb("scg", [128, 2], F32, es)
            scr = {"mr": self.sb("q2mr", [128, 1], F32, es),
                   "t": self.sb("q2t", [128, INNER // 4], F32, es),
                   "d": self.sb("q2d", [128, INNER // 4], I32, es),
                   "gt": self.sb("q2gt", [128, INNER // 4], BF16, es),
                   "v": self.sb("q2v", [128, INNER // 4], F32, es)}
            mh = self.sb("mh", [128, 1], F32, es)
            ps_g = self.ps("ps_g", [128, 2048], F32, es)
            ps_u = self.ps("ps_u", [128, 2048], F32, es)
            for t in range(NT):
                p.dve(lambda e, t=t: e.tensor_scalar(scg[:, 0:1], mraw[:, t:t+1],
                                                     mwg[:, 0:1], 1.0 / 127.0,
                                                     ALU.mult, ALU.mult),
                      r=[(mraw.name, t), mwg.name], w=[(scg.name, ("g", t))])
                p.dve(lambda e, t=t: e.tensor_scalar(scg[:, 1:2], mraw[:, t:t+1],
                                                     mwu[:, 0:1], 1.0 / 127.0,
                                                     ALU.mult, ALU.mult),
                      r=[(mraw.name, t), mwu.name], w=[(scg.name, ("u", t))])
                for oc in range(8):
                    s = oc % 4
                    gps = ps_g[:, 512*s:512*s+512]
                    ups = ps_u[:, 512*s:512*s+512]
                    for kc in range(NC):
                        lhs = rt[:].rearrange("p (c q) -> p c q", q=T)[:, kc, 128*t:128*t+128]
                        rhsg = wgT[:].rearrange("p (c o) -> p c o", o=INNER)[:, kc, 512*oc:512*oc+512]
                        rhsu = wuT[:].rearrange("p (c o) -> p c o", o=INNER)[:, kc, 512*oc:512*oc+512]
                        p.pe(lambda e, gps=gps, lhs=lhs, rhsg=rhsg, kc=kc:
                             e.matmul(gps, lhs, rhsg, start=(kc == 0), stop=(kc == NC-1)),
                             r=[(rt.name, ("tt", t)), (wgT.name, None)],
                             w=[(ps_g.name, s)])
                        p.pe(lambda e, ups=ups, lhs=lhs, rhsu=rhsu, kc=kc:
                             e.matmul(ups, lhs, rhsu, start=(kc == 0), stop=(kc == NC-1)),
                             r=[(rt.name, ("tt", t)), (wuT.name, None)],
                             w=[(ps_u.name, s)])
                    hsl = h[:, 512*oc:512*oc+512]
                    p.act(lambda e, hsl=hsl, gps=gps: e.activation(
                        hsl, gps, AF.Silu, scale=scg[:, 0:1]),
                          r=[(ps_g.name, s), (scg.name, ("g", t))],
                          w=[(h.name, ("g", oc)), (h.name, ("h", oc))])
                    p.dve(lambda e, hsl=hsl, ups=ups: e.scalar_tensor_tensor(
                        hsl, ups, scg[:, 1:2], hsl, ALU.mult, ALU.mult),
                          r=[(ps_u.name, s), (scg.name, ("u", t)), (h.name, ("g", oc))],
                          w=[(h.name, ("h", oc))])
                self.quantize_act_tile(h, 0,
                                       [(h.name, ("h", oc)) for oc in range(8)],
                                       rh, mh[:, 0:1], scr, INNER,
                                       (mh.name, t), nsplit=8,
                                       split_keys=[[(h.name, ("h", i // 1))]
                                                   for i in range(8)])
                p.dve(lambda e, t=t: e.tensor_scalar(scd[:, t:t+1], mh[:, 0:1],
                                                     1.0 / 127.0, None, ALU.mult),
                      r=[(mh.name, t)], w=[(scd.name, t)])
                p.dma("sp", f"rhsp{t % 2}", lambda e, t=t: e.dma_start(
                    out=rh_sp[t, :, :], in_=rh[:]),
                      r=[("rbf_out", None)], w=[("rh_sp", t)], tag=f"rhsp{t}")

    def emit_l0_ffn_part3(self, x1_src, rh_sp, scd, stream2, wdT_sp, mwd):
        p = self.p
        INNER = 4096
        with contextlib.ExitStack() as es:
            wdT = self.sb("wdT", [128, 32 * D], BF16, es)
            for i in range(32):
                rc, pc = divmod(i, 4)
                dst = wdT[:].rearrange("p (ic r) -> p ic r", r=D)[
                    :, 8*pc:8*pc+8, 128*rc:128*rc+128]
                wsrc = wdT_sp[i].rearrange("p (ic r) -> p ic r", r=128)
                p.dma("sp", f"wdlt{i % 4}", lambda e, dst=dst, wsrc=wsrc:
                      e.dma_start(out=dst, in_=wsrc),
                      r=[("wd_sink", i)], w=[(wdT.name, ("ld", i))])
            rhl = self.sb("rhl", [128, 2 * INNER], BF16, es)
            rhT = self.sb("rhTb", [128, 2 * INNER], BF16, es)
            x1t = self.sb("x1t", [128, 2 * D], F32, es)
            scf = self.sb("scf", [128, 1], F32, es)
            ps_d = self.ps("ps_d", [128, 1024], F32, es)
            ps_t3 = self.ps("ps_t3", [128, 4096], BF16, es)
            x1r = x1_src.rearrange("(t p) d -> p t d", p=128)
            scr = self.ln_scratch("l0f", es)
            for t in range(NT):
                s = t % 2
                lsl = rhl[:, INNER*s:INNER*s+INNER]
                xsl = x1t[:, D*s:D*s+D]
                p.dma("sp", f"rhld{s}", lambda e, lsl=lsl, t=t: e.dma_start(
                    out=lsl, in_=rh_sp[t, :, :]),
                      r=[("rh_sp", t)], w=[(rhl.name, s)])
                rsl = rhT[:, INNER*s:INNER*s+INNER]
                for g2 in range(2):
                    st3 = (2 * t + g2) % 2
                    pt3 = ps_t3[:, 2048*st3:2048*st3+2048]
                    for icb in range(16):
                        ic = 16 * g2 + icb
                        p.pe(lambda e, pt3=pt3, lsl=lsl, ic=ic, icb=icb: e.transpose(
                            pt3[:, 128*icb:128*icb+128], lsl[:, 128*ic:128*ic+128],
                            self.t["c_ident"][:]),
                             r=[(rhl.name, s), "c_ident"], w=[(ps_t3.name, (st3, icb))])
                    rks3 = [(ps_t3.name, (st3, icb)) for icb in range(16)]
                    dst3 = rsl[:, 2048*g2:2048*g2+2048]
                    if g2 == 0:
                        p.dve(lambda e, dst3=dst3, pt3=pt3: e.tensor_copy(dst3, pt3),
                              r=rks3, w=[(rhT.name, (s, 0))])
                    else:
                        p.act(lambda e, dst3=dst3, pt3=pt3: e.activation(dst3, pt3, AF.Copy),
                              r=rks3, w=[(rhT.name, (s, 1))])
                p.dma("pool", f"x1ld{s}", lambda e, xsl=xsl, t=t: e.dma_start(
                    out=xsl, in_=x1r[:, t, :]),
                      r=[("x1sp_r", t // 4)], w=[(x1t.name, s)])
                p.dve(lambda e, t=t: e.tensor_scalar(scf[:], scd[:, t:t+1],
                                                     mwd[:, 0:1], None, ALU.mult),
                      r=[(scd.name, t), mwd.name], w=[scf.name])
                for nc2 in range(2):
                    dps = ps_d[:, 512*nc2:512*nc2+512]
                    for kc in range(32):
                        lhs = rsl[:, 128*kc:128*kc+128]
                        rhs = wdT[:, 1024*kc + 512*nc2: 1024*kc + 512*nc2 + 512]
                        p.pe(lambda e, dps=dps, lhs=lhs, rhs=rhs, kc=kc:
                             e.matmul(dps, lhs, rhs, start=(kc == 0), stop=(kc == 31)),
                             r=[(rhT.name, (s, 0)), (rhT.name, (s, 1))]
                             + [(wdT.name, ("ld", 4*rc2 + kc//8)) for rc2 in range(8)],
                             w=[(ps_d.name, nc2)])
                    out_sl = stream2[:, D*t + 512*nc2: D*t + 512*nc2 + 512]
                    xs2 = xsl[:, 512*nc2:512*nc2+512]
                    p.dve(lambda e, dps=dps, out_sl=out_sl, xs2=xs2:
                          e.scalar_tensor_tensor(out_sl, dps, scf[:, 0:1], xs2,
                                                 ALU.mult, ALU.add),
                          r=[(ps_d.name, nc2), scf.name, (x1t.name, s)],
                          w=[(stream2.name, ("pre", t, nc2))])
                if t % 2 == 1:
                    specs = [(stream2[:, D*t2:D*t2+D], stream2[:, D*t2:D*t2+D],
                              [(stream2.name, ("pre", t2, 0)),
                               (stream2.name, ("pre", t2, 1))],
                              [(stream2.name, ("tile", t2))])
                             for t2 in (t - 1, t)]
                    self.ln_pair(specs, scr)
            p.barrier()

    def rope_stages(self, src, dst, W, cb, scr, keys_r, keys_w, s=0):
        p = self.p
        rb, cbuf = scr["rb"], scr["cbuf"]
        o = 512 * s
        cos, sin = self.t["c_cos"], self.t["c_sin"]
        return [
            lambda: p.dve(lambda e: e.scalar_tensor_tensor(
                rb[0:32, o:o+W], src[32:64, 0:W], -1.0,
                sin[32:64, cb:cb+W], ALU.mult, ALU.mult),
                r=keys_r + ["c_sin"], w=[(rb.name, (s, 0))]),
            lambda: p.dve(lambda e: e.scalar_tensor_tensor(
                rb[32:64, o:o+W], src[0:32, 0:W], 1.0,
                sin[0:32, cb:cb+W], ALU.mult, ALU.mult),
                r=keys_r + ["c_sin"], w=[(rb.name, (s, 1))]),
            lambda: p.pool(lambda e: e.tensor_mul(cbuf[0:64, o:o+W], src[0:64, 0:W],
                                                  cos[0:64, cb:cb+W]),
                r=keys_r + ["c_cos"], w=[(cbuf.name, s)]),
            lambda: p.dve(lambda e: e.tensor_add(dst, cbuf[0:64, o:o+W],
                                                 rb[0:64, o:o+W]),
                r=[(rb.name, (s, 0)), (rb.name, (s, 1)), (cbuf.name, s)],
                w=keys_w)]

    def rope(self, src, dst, W, cb, scr, keys_r, keys_w, s=0):
        for f in self.rope_stages(src, dst, W, cb, scr, keys_r, keys_w, s):
            f()

    def emit_l1_attn(self, stream):
        p = self.p
        with contextlib.ExitStack() as es:
            cos = self.load_const_named("c_cos", es)
            sin = self.load_const_named("c_sin", es)
            tri = self.load_const_named("c_tri", es)
            with contextlib.ExitStack() as eslo:
                for nm, hi in [("c_cosl", cos), ("c_sinl", sin)]:
                    lo = self.load_const_named(nm, eslo)
                    p.dve(lambda e, hi=hi, lo=lo: e.tensor_add(hi[:], hi[:], lo[:]),
                          r=[nm, hi.name.replace("_s", "")],
                          w=[hi.name.replace("_s", "")])
                p.barrier()
            xt = self.sb("xt1", [128, NC * T], BF16, es)
            with contextlib.ExitStack() as es2:
                self.transpose_stream(stream, xt, es2)
                p.barrier()
            wq = self.sb("wq1", [128, NC * D], BF16, es)
            wo = self.sb("wo1", [128, NC * D], BF16, es)
            wkv = self.sb("wkv1", [128, NC * 128], BF16, es)
            for nm, hb in [("wq1", "L1_attn_Wq"), ("wo1", "L1_attn_Wo")]:
                t = self.t[nm]
                hr = self.inp[hb].rearrange("(c p) o -> p c o", p=128)
                p.dma("pool", nm, lambda e, t=t, hr=hr: e.dma_start(
                    out=t[:].rearrange("p (c o) -> p c o", o=D), in_=hr), w=[nm])
            hrkv = self.inp["L1_attn_Wkv"].rearrange("(c p) o -> p c o", p=128)
            p.dma("pool", "wkv1", lambda e: e.dma_start(
                out=wkv[:].rearrange("p (c o) -> p c o", o=128), in_=hrkv), w=["wkv1"])
            kTf = self.sb("kTf", [64, 512], F32, es)
            kTb = self.sb("kTb", [64, T], BF16, es)
            vaug1 = self.sb("vaug1", [128, NT * 65], BF16, es)
            scr_r = {"rb": self.sb("rrb", [64, 1024], F32, es),
                     "cbuf": self.sb("rcb", [64, 1024], F32, es)}
            qTq = self.sb("qTq", [64, 16 * 512], BF16, es)
            tmpq = self.sb("tmpq", [128, 1024], F32, es)
            pTb = self.sb("pTb", [128, 2 * 512], BF16, es)
            hi_t = self.sb("hi_t", [1, 512], BF16, es)
            lo_t = self.sb("lo_t", [1, 512], BF16, es)
            rrow = self.sb("rrow", [1, 512], F32, es)
            og = self.sb("og", [64, 512], BF16, es)
            oTn = self.sb("oTn", [128, NC * 512], BF16, es)
            ps_a = self.ps("ps_a1", [128, 1024], F32, es)
            ps_s = self.ps("ps_s1", [128, 1024], F32, es)
            ps_o = self.ps("ps_o1", [128, 1024], F32, es)
            ps_b = self.ps("ps_b1", [128, 1024], F32, es)
            for qc in range(4):
                for kc in range(NC):
                    lhs = wkv[:, 128*kc:128*kc+64]
                    rhs = xt[:].rearrange("p (c q) -> p c q", q=T)[:, kc, 512*qc:512*qc+512]
                    p.pe(lambda e, lhs=lhs, rhs=rhs, kc=kc:
                         e.matmul(ps_a[0:64, 0:512], lhs, rhs, start=(kc == 0), stop=(kc == NC-1)),
                         r=["wkv1"] + [(xt.name, ("tt", tt)) for tt in range(4*qc, 4*qc+4)],
                         w=[(ps_a.name, 0)])
                p.dve(lambda e, qc=qc: e.tensor_copy(kTf[:, 0:512], ps_a[0:64, 0:512]),
                      r=[(ps_a.name, 0)], w=[(kTf.name, None)])
                self.rope(kTf, kTb[0:64, 512*qc:512*qc+512], 512, 512*qc, scr_r,
                          [(kTf.name, None)], [(kTb.name, ("kc", qc))])
            for kt in range(NT):
                sb2 = kt % 2
                pb = ps_b[:, 512*sb2:512*sb2+64]
                for kc in range(NC):
                    lhs = xt[:].rearrange("p (c q) -> p c q", q=T)[:, kc, 128*kt:128*kt+128]
                    rhs = wkv[:, 128*kc+64:128*kc+128]
                    p.pe(lambda e, pb=pb, lhs=lhs, rhs=rhs, kc=kc:
                         e.matmul(pb, lhs, rhs, start=(kc == 0), stop=(kc == NC-1)),
                         r=["wkv1", (xt.name, ("tt", kt))], w=[(ps_b.name, sb2)])
                p.dve(lambda e, pb=pb, kt=kt: e.tensor_copy(vaug1[:, 65*kt:65*kt+64],
                                                            pb),
                      r=[(ps_b.name, sb2)], w=[(vaug1.name, ("v", kt))])
                p.pool(lambda e, kt=kt: e.memset(vaug1[:, 65*kt+64:65*kt+65], 1.0),
                       w=[(vaug1.name, ("o", kt))])
            scr = self.ln_scratch("l1a", es)
            for qc in range(4):
                for oc in range(NC):
                    sa = oc % 2
                    pa = ps_a[:, 512*sa:512*sa+512]
                    for kc in range(NC):
                        lhs = wq[:, 1024*kc + 128*oc: 1024*kc + 128*oc + 128]
                        rhs = xt[:].rearrange("p (c q) -> p c q", q=T)[:, kc, 512*qc:512*qc+512]
                        p.pe(lambda e, pa=pa, lhs=lhs, rhs=rhs, kc=kc:
                             e.matmul(pa, lhs, rhs, start=(kc == 0), stop=(kc == NC-1)),
                             r=["wq1"] + [(xt.name, ("tt", tt)) for tt in range(4*qc, 4*qc+4)],
                             w=[(ps_a.name, sa)])
                    chains = []
                    for hh in range(2):
                        h = 2 * oc + hh
                        tsl = tmpq[:, 512*hh:512*hh+512]
                        def cp(hh=hh, sa=sa, tsl=tsl):
                            p.dve(lambda e, hh=hh: e.tensor_copy(
                                tsl[0:64, :], ps_a[64*hh:64*hh+64, 512*sa:512*sa+512]),
                                  r=[(ps_a.name, sa)], w=[(tmpq.name, hh)])
                        chains.append([cp] + self.rope_stages(
                            tsl, qTq[0:64, 512*h:512*h+512], 512, 512*qc,
                            scr_r, [(tmpq.name, hh)], [(qTq.name, (qc, h))], s=hh))
                    for stg in range(5):
                        chains[0][stg]()
                        chains[1][stg]()
                def make_ep(h, qc=qc):
                    hp = h % 2
                    def ep():
                        p.dve(lambda e, hp=hp: e.reciprocal(
                            rrow[:], ps_o[64:65, 512*hp:512*hp+512]),
                              r=[(ps_o.name, hp)], w=[rrow.name])
                        p.act(lambda e: e.activation(hi_t[:], rrow[:], AF.Copy),
                              r=[rrow.name], w=[hi_t.name])
                        p.dve(lambda e: e.tensor_sub(lo_t[:], rrow[:], hi_t[:]),
                              r=[rrow.name, hi_t.name], w=[lo_t.name])
                        p.pe(lambda e, hp=hp: e.matmul(ps_b[:, 512*hp:512*hp+512],
                                                       self.t["c_ones"][0:1, :], hi_t[:],
                                                       start=True, stop=False),
                             r=[hi_t.name, "c_ones"], w=[(ps_b.name, hp)])
                        p.pe(lambda e, hp=hp: e.matmul(ps_b[:, 512*hp:512*hp+512],
                                                       self.t["c_ones"][0:1, :], lo_t[:],
                                                       start=False, stop=True),
                             r=[lo_t.name, "c_ones"], w=[(ps_b.name, hp)])
                        p.dve(lambda e, hp=hp: e.tensor_copy(og[:], ps_o[0:64, 512*hp:512*hp+512]),
                              r=[(ps_o.name, hp)], w=[og.name])
                        dst = oTn[64*(h % 2):64*(h % 2)+64, 512*(h // 2):512*(h // 2)+512]
                        p.dve(lambda e, dst=dst, hp=hp: e.tensor_mul(dst, og[:],
                                                                     ps_b[0:64, 512*hp:512*hp+512]),
                              r=[og.name, (ps_b.name, hp)], w=[(oTn.name, h)])
                    return ep
                pend = [None]
                for h in range(16):
                    nkt = 4 * qc + 4
                    def emit_S1(kt, h=h):
                        ss = kt % 2
                        psl = ps_s[:, 512*ss:512*ss+512]
                        p.pe(lambda e, psl=psl, kt=kt, h=h:
                             e.matmul(psl, kTb[0:64, 128*kt:128*kt+128],
                                      qTq[0:64, 512*h:512*h+512], start=True, stop=True),
                             r=[(kTb.name, None), (qTq.name, (qc, h))],
                             w=[(ps_s.name, ss)])
                    emit_S1(0)
                    for kt in range(nkt):
                        ss = kt % 2
                        psl = ps_s[:, 512*ss:512*ss+512]
                        ptl = pTb[:, 512*ss:512*ss+512]
                        p.act(lambda e, ptl=ptl, psl=psl: e.activation(
                            ptl, psl, AF.Exp, scale=0.125),
                              r=[(ps_s.name, ss)], w=[(pTb.name, ss)])
                        if kt + 1 < nkt:
                            emit_S1(kt + 1)
                        b = kt - 4 * qc
                        if b >= 0:
                            if b > 0:
                                p.pool(lambda e, ptl=ptl, b=b: e.memset(
                                    ptl[:, 0:128*b], 0.0),
                                       r=[], w=[(pTb.name, ss)])
                            p.pool(lambda e, ptl=ptl, b=b: e.tensor_mul(
                                ptl[:, 128*b:128*b+128], ptl[:, 128*b:128*b+128],
                                tri[:]),
                                   r=["c_tri"], w=[(pTb.name, ss)])
                        hp = h % 2
                        p.pe(lambda e, ptl=ptl, kt=kt, nkt=nkt, hp=hp:
                             e.matmul(ps_o[0:65, 512*hp:512*hp+512],
                                      vaug1[:, 65*kt:65*kt+65], ptl,
                                      start=(kt == 0), stop=(kt == nkt - 1)),
                             r=[(vaug1.name, None), (pTb.name, ss)],
                             w=[(ps_o.name, hp)])
                        if kt == 1 and pend[0] is not None:
                            pend[0]()
                            pend[0] = None
                    pend[0] = make_ep(h)
                if pend[0] is not None:
                    pend[0]()
                    pend[0] = None
                for sub in range(4):
                    tt = 4 * qc + sub
                    for half in range(2):
                        sa = (2 * sub + half) % 2
                        pa = ps_a[:, 512*sa:512*sa+512]
                        for kc in range(NC):
                            lhs = oTn[:, 512*kc + 128*sub: 512*kc + 128*sub + 128]
                            rhs = wo[:, 1024*kc + 512*half: 1024*kc + 512*half + 512]
                            p.pe(lambda e, pa=pa, lhs=lhs, rhs=rhs, kc=kc:
                                 e.matmul(pa, lhs, rhs, start=(kc == 0), stop=(kc == NC-1)),
                                 r=["wo1"] + [(oTn.name, hh) for hh in range(16)],
                                 w=[(ps_a.name, sa)])
                        out_sl = stream[:, D*tt + 512*half: D*tt + 512*half + 512]
                        p.dve(lambda e, pa=pa, out_sl=out_sl: e.tensor_add(out_sl, pa, out_sl),
                              r=[(ps_a.name, sa), (stream.name, ("tile", tt))],
                              w=[(stream.name, ("tile", tt))])
                    if sub % 2 == 1:
                        specs = [(stream[:, D*t2:D*t2+D], stream[:, D*t2:D*t2+D],
                                  [(stream.name, ("tile", t2))],
                                  [(stream.name, ("tile", t2))])
                                 for t2 in (tt - 1, tt)]
                        self.ln_pair(specs, scr)
            p.barrier()

    def prep_wT_stages(self, nm, hbm, es, half=0):
        p = self.p
        u = self.uid()
        sp = self.dram_tmp(f"sp_{nm}", [NC, 128, D], BF16)
        nat32 = self.sb(f"nat32{u}", [128, 2 * D], F32, es)
        nat = self.sb(f"nat{u}", [128, 2 * D], BF16, es)
        stg = self.sb(f"stg{u}", [128, 2 * D], BF16, es)
        ps_t = self.t["ps_prep"]
        ho = 1024 * half
        hbr = hbm.rearrange("(r p) c -> p r c", p=128)

        def load(rc):
            if rc >= NC:
                return
            s = rc % 2
            n3 = nat32[:, D*s:D*s+D]
            p.dma("sp", f"pw{u}{s}", lambda e, n3=n3, rc=rc: e.dma_start(
                out=n3, in_=hbr[:, rc, :]), w=[(nat32.name, s)])

        def mk(rc):
            def f():
                if rc == 0:
                    load(0)
                load(rc + 1)
                s = rc % 2
                nsl = nat[:, D*s:D*s+D]
                n3 = nat32[:, D*s:D*s+D]
                if (rc + half) % 2 == 0:
                    p.act(lambda e, nsl=nsl, n3=n3: e.activation(nsl, n3, AF.Copy),
                          r=[(nat32.name, s)], w=[(nat.name, s)])
                else:
                    p.dve(lambda e, nsl=nsl, n3=n3: e.tensor_copy(nsl, n3),
                          r=[(nat32.name, s)], w=[(nat.name, s)])
                for jc in range(NC):
                    p.pe(lambda e, nsl=nsl, jc=jc: e.transpose(
                        ps_t[:, ho + 128*jc: ho + 128*jc + 128],
                        nsl[:, 128*jc:128*jc+128], self.t["c_ident"][:]),
                         r=[(nat.name, s), "c_ident"], w=[(ps_t.name, (half, jc))])
                ssl = stg[:, D*s:D*s+D]
                rks = [(ps_t.name, (half, jc)) for jc in range(NC)]
                if (rc + half) % 2 == 0:
                    p.dve(lambda e, ssl=ssl: e.tensor_copy(ssl, ps_t[:, ho:ho+1024]),
                          r=rks, w=[(stg.name, s)])
                else:
                    p.act(lambda e, ssl=ssl: e.activation(ssl, ps_t[:, ho:ho+1024],
                                                          AF.Copy),
                          r=rks, w=[(stg.name, s)])
                p.dma("sp", f"pws{u}{s}", lambda e, ssl=ssl, rc=rc: e.dma_start(
                    out=sp.rearrange("c p o -> c p o")[:, :, 128*rc:128*rc+128]
                        .rearrange("c p o -> p c o"),
                    in_=ssl[:].rearrange("p (c o) -> p c o", o=128)),
                      r=[(stg.name, s)], w=[(f"sp_{nm}", rc)])
            return f
        return sp, [mk(rc) for rc in range(NC)]

    def branch(self, qT, kT, qkeys, Vf, Wf, UT, bmat, carry, ci, pP, hT, scr1, qc,
               ps, alpha_col=None, fA=None, bcA=None, totT=None):
        p = self.p
        u = self.uid()
        ps_x, ps_y, ps_u, ps_l = ps
        eg, egb, rw, lg = scr1["eg"], scr1["egb"], scr1["rw"], scr1["lg"]
        cum, tmp = scr1["cum"], scr1["tmp"]
        rb1 = scr1["rb1"]; rb2e = scr1["rb2e"]; hi_t = scr1["hi"]; lo_t = scr1["lo"]
        for oc in range(NC):
            s = oc % 2
            hsl = hT[:, 512*s:512*s+512]
            for kc in range(NC):
                lhs = scr1["rw1T"][:, D*kc + 128*oc: D*kc + 128*oc + 128]
                rhs = qT[:, 512*kc:512*kc+512]
                p.pe(lambda e, lhs=lhs, rhs=rhs, kc=kc:
                     e.matmul(ps_l[:, 0:512], lhs, rhs, start=(kc == 0), stop=(kc == NC-1)),
                     r=qkeys + [("wslot", None)], w=[(ps_l.name, 0)])
            p.act(lambda e, hsl=hsl, oc=oc: e.activation(hsl, ps_l[:, 0:512],
                                                         AF.Gelu, bias=rb1[:, oc:oc+1]),
                  r=[(ps_l.name, 0), "rb1"], w=[(hT.name, s)])
            lhs2 = scr1["rw2T"][:, 8*oc:8*oc+8]
            p.pe(lambda e, lhs2=lhs2, hsl=hsl, oc=oc:
                 e.matmul(ps_l[0:8, 512:1024], lhs2, hsl,
                          start=(oc == 0), stop=(oc == NC-1)),
                 r=[(hT.name, s), ("rw2T", None)], w=[(ps_l.name, 1)])
        p.act(lambda e: e.activation(lg[:], ps_l[0:8, 512:1024], AF.Identity,
                                     bias=rb2e[0:8, 0:1]),
              r=[(ps_l.name, 1), "rb2e"], w=[lg.name])
        p.act(lambda e: e.activation(eg[:], lg[:], AF.Exp),
              r=[lg.name], w=[eg.name])
        p.dve(lambda e: e.tensor_copy(egb[:], eg[:]),
              r=[eg.name], w=[egb.name])
        p.pe(lambda e: e.matmul(ps_l[0:1, 512:1024], self.t["c_onesf"][0:8, 0:1], eg[:],
                                start=True, stop=True),
             r=[eg.name, "c_onesf"], w=[(ps_l.name, 1)])
        p.dve(lambda e: e.reciprocal(rw[:], ps_l[0:1, 512:1024]),
              r=[(ps_l.name, 1)], w=[rw.name])
        p.dve(lambda e: e.tensor_copy(hi_t[:], rw[:]), r=[rw.name], w=[hi_t.name])
        p.dve(lambda e: e.tensor_sub(lo_t[:], rw[:], hi_t[:]),
              r=[rw.name, hi_t.name], w=[lo_t.name])
        if alpha_col is not None:
            p.dve(lambda e: e.tensor_scalar(hi_t[:], hi_t[:], alpha_col[0:1, 0:1],
                                            None, ALU.mult),
                  r=[hi_t.name, "alpha"], w=[hi_t.name])
            p.dve(lambda e: e.tensor_scalar(lo_t[:], lo_t[:], alpha_col[0:1, 0:1],
                                            None, ALU.mult),
                  r=[lo_t.name, "alpha"], w=[lo_t.name])
        p.pe(lambda e: e.matmul(ps_l[:, 512:1024], self.t["c_ones"][0:1, :], hi_t[:],
                                start=True, stop=False),
             r=[hi_t.name, "c_ones"], w=[(ps_l.name, 1)])
        p.pe(lambda e: e.matmul(ps_l[:, 512:1024], self.t["c_ones"][0:1, :], lo_t[:],
                                start=False, stop=True),
             r=[lo_t.name, "c_ones"], w=[(ps_l.name, 1)])
        bca = bcA
        p.act(lambda e: e.activation(bca[:], ps_l[:, 512:1024], AF.Copy),
              r=[(ps_l.name, 1)], w=[bca.name])
        for krc in range(4):
            for dc in range(NC):
                lhsv = Vf[:, 512*dc + 128*krc: 512*dc + 128*krc + 128]
                lhsw = Wf[:, 512*dc + 128*krc: 512*dc + 128*krc + 128]
                rhsq = qT[:, 512*dc:512*dc+512]
                rhsk = kT[:, 512*dc:512*dc+512]
                p.pe(lambda e, lhsv=lhsv, rhsq=rhsq, dc=dc:
                     e.matmul(ps_x[:], lhsv, rhsq, start=(dc == 0), stop=(dc == NC-1)),
                     r=qkeys + [(Vf.name, None)], w=[(ps_x.name, 0)])
                p.pe(lambda e, lhsw=lhsw, rhsk=rhsk, dc=dc:
                     e.matmul(ps_y[:], lhsw, rhsk, start=(dc == 0), stop=(dc == NC-1)),
                     r=[(kT.name, None), (Wf.name, None)], w=[(ps_y.name, 0)])
            p.dve(lambda e, krc=krc: e.tensor_tensor_scan(
                cum[:], ps_y[:], scr1["rc_n"][:], carry[:, 4*ci + krc: 4*ci + krc + 1],
                ALU.add, ALU.bypass),
                  r=[(ps_y.name, 0), (carry.name, (ci, krc)), "rc_n"], w=[cum.name])
            p.dve(lambda e, krc=krc: e.tensor_copy(
                carry[:, 4*ci + krc: 4*ci + krc + 1], cum[:, 511:512]),
                  r=[cum.name], w=[(carry.name, (ci, krc))])
            p.pe(lambda e, krc=krc: e.matmul(
                ps_l[:, 512:1024], self.t["c_emat"][0:8, 128*krc:128*krc+128], egb[:],
                start=True, stop=True),
                 r=[egb.name, "c_emat"], w=[(ps_l.name, 1)])
            p.dve(lambda e: e.tensor_tensor(tmp[:], ps_x[:], cum[:], ALU.mult),
                  r=[(ps_x.name, 0), cum.name], w=[tmp.name])
            p.dve(lambda e, krc=krc: e.tensor_tensor(
                pP[:, 512*krc:512*krc+512], tmp[:], ps_l[:, 512:1024], ALU.mult),
                  r=[tmp.name, (ps_l.name, 1)], w=[(pP.name, krc)])
        for dc in range(NC):
            s = dc % 2
            psu = ps_u[:, 512*s:512*s+512]
            for krc in range(4):
                lhs = UT[:, 1024*krc + 128*dc: 1024*krc + 128*dc + 128]
                p.pe(lambda e, psu=psu, lhs=lhs, krc=krc:
                     e.matmul(psu, lhs, pP[:, 512*krc:512*krc+512],
                              start=(krc == 0), stop=False),
                     r=[(pP.name, krc), (UT.name, None)], w=[(ps_u.name, s)])
            p.pe(lambda e, psu=psu, dc=dc: e.matmul(
                psu, bmat[0:8, 128*dc:128*dc+128], egb[:], start=False, stop=True),
                 r=[egb.name, (bmat.name, None)], w=[(ps_u.name, s)])
            if alpha_col is None:
                p.dve(lambda e, psu=psu, dc=dc: e.tensor_tensor(
                    fA[:, 512*dc:512*dc+512], psu, bca[:], ALU.mult),
                      r=[(ps_u.name, s), bca.name], w=[(fA.name, dc)])
            else:
                p.dve(lambda e, psu=psu: e.scalar_tensor_tensor(
                    tmp[:], psu, 1.0, bca[:], ALU.mult, ALU.mult),
                      r=[(ps_u.name, s), bca.name], w=[tmp.name])
                p.dve(lambda e, dc=dc: e.tensor_add(tmp[:], tmp[:], fA[:, 512*dc:512*dc+512]),
                      r=[tmp.name, (fA.name, dc)], w=[(tmp.name, "b")])
                rsl = scr1["rc_n"]
                p.dve(lambda e, dc=dc, rsl=rsl: e.tensor_tensor(
                    totT[:, 512*dc:512*dc+512], tmp[:], rsl[:], ALU.mult),
                      r=[(tmp.name, "b"), "rc_n"], w=[(totT.name, dc)])

    def emit_l1_ffn(self, stream):
        p = self.p
        with contextlib.ExitStack() as es:
            rc_n = self.sb("rc_n", [128, 512], F32, es)
            flats = {}
            for nm, hb in [("Vf", "L1_ffn_V_fwd"), ("Wf", "L1_ffn_W_fwd"),
                           ("Vi", "L1_ffn_V_inv"), ("Wie", "L1_ffn_W_inv_exp")]:
                t = self.sb(nm, [128, NC * 512], BF16, es)
                flats[nm] = t
            for nm in ["UfT", "UiT"]:
                flats[nm] = self.sb(nm, [128, 4 * D], BF16, es)
            rw2T = self.sb("rw2T", [128, NC * 8], BF16, es)
            self.t["rw2T"] = rw2T
            with contextlib.ExitStack() as esp:
                ps_prep = self.ps("ps_prep", [128, 2048], BF16, esp)
                self.t["ps_prep"] = ps_prep
                stg32 = self.sb("fstg32", [128, 2 * 512], F32, esp)
                cnt = 0
                for nm, hb in [("Vf", "L1_ffn_V_fwd"), ("Wf", "L1_ffn_W_fwd"),
                               ("Vi", "L1_ffn_V_inv"), ("Wie", "L1_ffn_W_inv_exp")]:
                    t = flats[nm]
                    hr = self.inp[hb].rearrange("k (c p) r -> p c k r", p=128)
                    for c in range(NC):
                        s2 = cnt % 2
                        ssl = stg32[:, 512*s2:512*s2+512]
                        p.dma("sp", f"flt{s2}", lambda e, ssl=ssl, hr=hr, c=c:
                              e.dma_start(out=ssl.rearrange("p (k r) -> p k r", r=64),
                                          in_=hr[:, c]),
                              w=[(stg32.name, s2)])
                        dst = t[:, 512*c:512*c+512]
                        if cnt % 2 == 0:
                            p.dve(lambda e, dst=dst, ssl=ssl: e.tensor_copy(dst, ssl),
                                  r=[(stg32.name, s2)], w=[(nm, c)])
                        else:
                            p.act(lambda e, dst=dst, ssl=ssl: e.activation(
                                dst, ssl, AF.Copy),
                                  r=[(stg32.name, s2)], w=[(nm, c)])
                        cnt += 1
                sps = {}
                for pair in [[("wqT", "L1_ffn_W_Q"), ("wkT", "L1_ffn_W_K")],
                             [("wiT", "L1_ffn_W_inv"), ("r1T", "L1_ffn_router_w1")],
                             [("woT", "L1_ffn_W_O")]]:
                    streams = []
                    for half, (nm, hb) in enumerate(pair):
                        sp2, stages = self.prep_wT_stages(nm, self.inp[hb],
                                                          esp, half)
                        sps[nm] = sp2
                        streams.append(stages)
                    for rc in range(NC):
                        for st in streams:
                            st[rc]()
                for nm, hb in [("UfT", "L1_ffn_U_fwd"), ("UiT", "L1_ffn_U_inv")]:
                    t = flats[nm]
                    tmpu = self.sb(nm + "_n", [128, 512], BF16, esp)
                    hr = self.inp[hb].rearrange("k (c p) r -> p c k r", p=128)
                    for c in range(NC):
                        p.dma("pool", nm + "n", lambda e, c=c, hr=hr, tmpu=tmpu: e.dma_start(
                            out=tmpu[:].rearrange("p (k r) -> p k r", r=64), in_=hr[:, c]),
                              w=[(tmpu.name, None)])
                        for b4 in range(4):
                            p.pe(lambda e, b4=b4, tmpu=tmpu: e.transpose(
                                ps_prep[:, 128*b4:128*b4+128], tmpu[:, 128*b4:128*b4+128],
                                self.t["c_ident"][:]),
                                 r=[(tmpu.name, None), "c_ident"],
                                 w=[(ps_prep.name, (0, b4))])
                        dst = t[:].rearrange("p (kc d) -> p kc d", d=D)[:, :, 128*c:128*c+128]
                        p.dve(lambda e, dst=dst: e.tensor_copy(
                            dst, ps_prep[:, 0:512].rearrange("p (kc d) -> p kc d", d=128)),
                              r=[(ps_prep.name, (0, b4)) for b4 in range(4)],
                              w=[(t.name, ("c", c))])
                rw2n = self.sb("rw2n", [8, D], BF16, esp)
                p.dma("pool", "rw2", lambda e: e.dma_start(out=rw2n[:],
                                                           in_=self.inp["L1_ffn_router_w2"]),
                      w=[rw2n.name])
                for c in range(NC):
                    p.pe(lambda e, c=c: e.transpose(ps_prep[:, 0:8], rw2n[0:8, 128*c:128*c+128],
                                                    self.t["c_ident"][0:8, 0:8]),
                         r=[rw2n.name, "c_ident"], w=[(ps_prep.name, (0, 0))])
                    p.dve(lambda e, c=c: e.tensor_copy(rw2T[:, 8*c:8*c+8], ps_prep[:, 0:8]),
                          r=[(ps_prep.name, (0, 0))], w=[("rw2T", c)])
                p.barrier()
            bf = self.sb("bfwd", [8, D], BF16, es)
            bi = self.sb("binv", [8, D], BF16, es)
            p.dma("pool", "bfwd", lambda e: e.dma_start(out=bf[:], in_=self.inp["L1_ffn_b_fwd"]),
                  w=[bf.name])
            p.dma("pool", "binv", lambda e: e.dma_start(out=bi[:], in_=self.inp["L1_ffn_b_inv"]),
                  w=[bi.name])
            rb1 = self.sb("rb1", [128, NC], F32, es)
            for c in range(NC):
                p.dma("sp", "rb1", lambda e, c=c: e.dma_start(
                    out=rb1[:, c:c+1],
                    in_=self.inp["L1_ffn_router_b1"].rearrange("(a p) -> p a", p=128)[:, c:c+1]),
                      w=[("rb1", c)])
            self.t["rb1"] = rb1
            rb2e = self.sb("rb2e", [8, 2], F32, es)
            p.dma("sp", "rb2", lambda e: e.dma_start(
                out=rb2e[0:8, 0:1], in_=self.inp["L1_ffn_router_b2"].rearrange("(a k) -> k a", a=1)),
                  w=[("rb2e", 0)])
            p.dma("sp", "rb2", lambda e: e.dma_start(
                out=rb2e[0:8, 1:2], in_=self.inp["L1_ffn_expert_bias"].rearrange("(a k) -> k a", a=1)),
                  w=[("rb2e", 1)])
            p.dve(lambda e: e.tensor_add(rb2e[0:8, 0:1], rb2e[0:8, 0:1], rb2e[0:8, 1:2]),
                  r=[("rb2e", 0), ("rb2e", 1)], w=["rb2e"])
            alpha_sb = self.sb("alpha_sb", [1, 1], F32, es)
            p.dma("sp", "alph", lambda e: e.dma_start(
                out=alpha_sb[:], in_=self.inp["L1_ffn_alpha_bi"]), w=["alpha"])
            xtc = self.sb("xtc", [128, NC * 512], BF16, es)
            xbshared = self.sb("xbshared", [128, 2 * D], BF16, es)
            pP = xbshared
            wslot = self.sb("wslot", [128, NC * D], BF16, es)
            qTc = self.sb("qTc", [128, NC * 512], BF16, es)
            kTc = self.sb("kTc", [128, NC * 512], BF16, es)
            iqT = self.sb("iqT", [128, NC * 512], BF16, es)
            ikT = self.sb("ikT", [128, NC * 512], BF16, es)
            hT = self.sb("hT1", [128, 2 * 512], BF16, es)
            fA = self.sb("fA", [128, NC * 512], BF16, es)
            totT = fA
            carry = self.sb("carry", [128, 8], F32, es)
            p.dve(lambda e: e.memset(carry[:], 0.0), w=[carry.name])
            scr1 = {"eg": self.sb("eg", [8, 512], F32, es),
                    "egb": self.sb("egb", [8, 512], BF16, es),
                    "rw": self.sb("rw", [1, 512], F32, es),
                    "lg": self.sb("lg", [8, 512], F32, es),
                    "cum": self.sb("cum", [128, 512], F32, es),
                    "tmp": self.sb("tmp1", [128, 512], F32, es),
                    "hi": self.sb("hi1", [1, 512], BF16, es),
                    "lo": self.sb("lo1", [1, 512], BF16, es),
                    "rb1": rb1, "rb2e": rb2e, "rc_n": rc_n,
                    "rw1T": wslot, "rw2T": rw2T}
            bcA = self.sb("bcA", [128, 512], F32, es)
            alpha_col = alpha_sb
            ps_trf = self.ps("ps_trf", [128, 2048], BF16, es)
            ps_x = self.ps("ps_x", [128, 512], F32, es)
            ps_y = self.ps("ps_y", [128, 512], F32, es)
            ps_u = self.ps("ps_u2", [128, 1024], F32, es)
            ps_l = self.ps("ps_l", [128, 1024], F32, es)
            scr = {"sx": self.sb("l1f_sx", [128, 2], F32, es),
                   "sq": self.sb("l1f_sq", [128, 2], F32, es),
                   "mt": self.sb("l1f_mt", [128, 2], F32, es),
                   "vt": self.sb("l1f_vt", [128, 2], F32, es),
                   "junk": hT}

            def load_w(nm, dep):
                spw = sps[nm]
                for c in range(NC):
                    dstw = wslot[:, D*c:D*c+D]
                    p.dma("sp", f"wsl{c % 2}", lambda e, spw=spw, dstw=dstw, c=c:
                          e.dma_start(out=dstw, in_=spw[c]),
                          w=[("wslot", ("ld", c))])

            def proj(dst, dkey, rhs_t, rhs_keys):
                for oc in range(NC):
                    sl = oc % 2
                    for kc in range(NC):
                        lhs = wslot[:, D*kc + 128*oc: D*kc + 128*oc + 128]
                        rhs = rhs_t[:, 512*kc:512*kc+512]
                        p.pe(lambda e, lhs=lhs, rhs=rhs, kc=kc, sl=sl:
                             e.matmul(ps_l[:, 512*sl:512*sl+512], lhs, rhs,
                                      start=(kc == 0), stop=(kc == NC-1)),
                             r=rhs_keys + [("wslot", ("ld", kc))], w=[(ps_l.name, sl)])
                    p.dve(lambda e, oc=oc, sl=sl: e.tensor_copy(
                        dst[:, 512*oc:512*oc+512], ps_l[:, 512*sl:512*sl+512]),
                          r=[(ps_l.name, sl)], w=[(dkey, oc)])

            for qc in range(4):
                with contextlib.ExitStack() as est:
                    self.transpose_stream(stream, xtc, est, tiles=range(4*qc, 4*qc+4),
                                          xt_cols=512, colbase=-512*qc, ptr_ext=ps_trf,
                                          xb_ext=xbshared)
                xkeys = [(xtc.name, ("tt", tt)) for tt in range(4*qc, 4*qc+4)]
                load_w("wqT", None)
                proj(qTc, "qTc", xtc, xkeys)
                load_w("wkT", None)
                proj(kTc, "kTc", xtc, xkeys)
                load_w("wiT", None)
                proj(iqT, "iqT", qTc, [("qTc", oc) for oc in range(NC)])
                proj(ikT, "ikT", kTc, [("kTc", oc) for oc in range(NC)])
                load_w("r1T", None)
                p.dma("sp", "rcn", lambda e, qc=qc: e.dma_start(
                    out=rc_n[:], in_=self.inp["c_recipn"][:, 512*qc:512*qc+512]),
                      w=["rc_n"])
                qkeysA = [("qTc", oc) for oc in range(NC)]
                self.branch(qTc, kTc, qkeysA, flats["Vf"], flats["Wf"], flats["UfT"],
                            bf, carry, 0, pP, hT, scr1, qc,
                            (ps_x, ps_y, ps_u, ps_l),
                            alpha_col=None, fA=fA, bcA=bcA)
                qkeysB = [("iqT", oc) for oc in range(NC)]
                self.branch(iqT, ikT, qkeysB, flats["Wie"], flats["Vi"], flats["UiT"],
                            bi, carry, 1, pP, hT, scr1, qc,
                            (ps_x, ps_y, ps_u, ps_l),
                            alpha_col=alpha_col, fA=fA, bcA=bcA, totT=totT)
                load_w("woT", None)
                for sub in range(4):
                    tt = 4 * qc + sub
                    for half in range(2):
                        sl = (2 * sub + half) % 2
                        pl = ps_l[:, 512*sl:512*sl+512]
                        for kc in range(NC):
                            lhs = totT[:, 512*kc + 128*sub: 512*kc + 128*sub + 128]
                            rhs = wslot[:, D*kc + 512*half: D*kc + 512*half + 512]
                            p.pe(lambda e, pl=pl, lhs=lhs, rhs=rhs, kc=kc:
                                 e.matmul(pl, lhs, rhs,
                                          start=(kc == 0), stop=(kc == NC-1)),
                                 r=[("fA", kc2) for kc2 in range(NC)]
                                 + [("wslot", ("ld", kc))],
                                 w=[(ps_l.name, sl)])
                        out_sl = stream[:, D*tt + 512*half: D*tt + 512*half + 512]
                        p.dve(lambda e, pl=pl, out_sl=out_sl: e.tensor_add(
                            out_sl, pl, out_sl),
                              r=[(ps_l.name, sl), (stream.name, ("tile", tt))],
                              w=[(stream.name, ("tile", tt))])
                    if sub % 2 == 1:
                        specs = [(stream[:, D*t2:D*t2+D], stream[:, D*t2:D*t2+D],
                                  [(stream.name, ("tile", t2))],
                                  [(stream.name, ("tile", t2))])
                                 for t2 in (tt - 1, tt)]
                        self.ln_pair(specs, scr)
            p.barrier()


from concourse.bass_utils import run_bass_kernel_spmd

L1FFN_SHAPES = {"W_Q": [1024, 1024], "W_K": [1024, 1024], "W_O": [1024, 1024],
                "W_inv": [1024, 1024], "V_fwd": [8, 1024, 64], "W_fwd": [8, 1024, 64],
                "U_fwd": [8, 1024, 64], "b_fwd": [8, 1024], "V_inv": [8, 1024, 64],
                "W_inv_exp": [8, 1024, 64], "U_inv": [8, 1024, 64], "b_inv": [8, 1024],
                "router_w1": [1024, 1024], "router_b1": [1024], "router_w2": [8, 1024],
                "router_b2": [8], "alpha_bi": [1, 1], "expert_bias": [8]}


def _build_fused():
    nc = bass.Bass("TRN2", target_bir_lowering=False, debug=False)
    core = Core(nc)
    x = core.dram_in("x", [T, D])
    core.dram_in("L0_attn_Wqk", [1024, 2048])
    core.dram_in("L0_attn_bqk", [2048])
    core.dram_in("L0_attn_Wv", [1024, 1024])
    core.dram_in("L0_attn_bv", [1024])
    core.dram_in("L0_ffn_w_gate", [4096, 1024])
    core.dram_in("L0_ffn_w_up", [4096, 1024])
    core.dram_in("L0_ffn_w_down", [1024, 4096])
    core.dram_in("L1_attn_Wq", [1024, 1024])
    core.dram_in("L1_attn_Wkv", [1024, 128])
    core.dram_in("L1_attn_Wo", [1024, 1024])
    for n2, shp in L1FFN_SHAPES.items():
        core.dram_in("L1_ffn_" + n2, shp)
    y = core.dram_out("y", [T, D])
    core.load_constants()
    x1sp = core.dram_tmp("x1sp", [T, D], F32)
    rh_sp = core.dram_tmp("rh_sp", [NT, 128, 4096], BF16)
    wuT_sp = core.dram_tmp("wuT_sp", [32, 128, 1024], BF16)
    wdT_sp = core.dram_tmp("wdT_sp", [32, 128, 1024], BF16)
    scd0 = core.sb("scd_l", [128, NT], F32)
    mwd = core.sb("mwd", [128, 1], F32)
    qs = core.make_l0_wquants(None, wuT_sp, wdT_sp, None)
    with contextlib.ExitStack() as es_r:
        with contextlib.ExitStack() as es1:
            stream1 = core.sb("stream1", [128, NT * D], F32, es1)
            core.load_stream(x, stream1)

            def aux_alloc(es_attn):
                mjunk = core.sb("wmjks", [128, 1024], BF16, es_attn)
                for q in qs:
                    q.alloc_mean(es_attn, junk_ext=mjunk)

            def aux_step(h):
                for q in qs:
                    q.mean_chunk(2*h)
                    q.mean_chunk(2*h+1)
            core.emit_l0_attn(stream1, aux_alloc, aux_step)
            rt = core.sb("rt", [128, NC * T], BF16, es_r, side="right")
            mraw = core.sb("mraw", [128, NT], F32, es_r, side="right")
            core.store_stream(stream1, x1sp, wreg="x1sp_r")
        core.p.barrier()
        with contextlib.ExitStack() as es_f:
            wgT = core.sb("wgT", [128, 8 * 4096], BF16, es_f)
            mwg = core.sb("mwg", [128, 1], F32, es_f)
            mwu = core.sb("mwu", [128, 1], F32, es_f)
            qs[0].wT = wgT
            with contextlib.ExitStack() as es_A:
                core.emit_l0_ffn_phaseA(x1sp, rt, mraw, qs, mwg, mwu, mwd,
                                        es_A, means_done=True)
            core.p.barrier()
            wuT = core.sb("wuT", [128, 8 * 4096], BF16, es_f)
            core.emit_l0_ffn_part2(rt, mraw, rh_sp, scd0, wgT, wuT, wuT_sp,
                                   mwg, mwu)
        core.p.barrier()
    stream2 = core.sb("stream2", [128, NT * D], F32)
    core.emit_l0_ffn_part3(x1sp, rh_sp, scd0, stream2, wdT_sp, mwd)
    core.emit_l1_attn(stream2)
    core.emit_l1_ffn(stream2)
    core.store_stream(stream2, y)
    core.p.barrier()
    core.p.finalize()
    return nc


def kernel(**inputs):
    f = np.float32
    consts = host_constants()
    xfull = np.ascontiguousarray(np.asarray(inputs["L0_attn_x"], f))
    n = 8
    shared = {}
    for k in ["L0_attn_Wqk", "L0_attn_bqk", "L0_attn_Wv", "L0_attn_bv",
              "L0_ffn_w_gate", "L0_ffn_w_up", "L0_ffn_w_down",
              "L1_attn_Wq", "L1_attn_Wkv", "L1_attn_Wo"]:
        shared[k] = np.ascontiguousarray(np.asarray(inputs[k], f))
    for nme in L1FFN_SHAPES:
        v = np.asarray(inputs["L1_ffn_" + nme], f)
        if nme == "alpha_bi":
            v = v.reshape(1, 1)
        shared["L1_ffn_" + nme] = np.ascontiguousarray(v)
    shared.update(consts)
    in_maps = [{"x": np.ascontiguousarray(xfull[c]), **shared} for c in range(n)]
    nc = _build_fused()
    res = run_bass_kernel_spmd(nc, in_maps, core_ids=list(range(n)))
    return np.stack([np.asarray(res.results[c]["y"], f) for c in range(n)], axis=0)



# version 28 (final)
# speedup vs baseline: 1.0254x; 1.0152x over previous
import contextlib

CE = ("pe", "act", "dve", "pool", "sp")
DQ = ("sp", "pool", "act")


def _conflict(tag_a, tag_b):
    return tag_a is None or tag_b is None or tag_a == tag_b


class _Region:
    __slots__ = ("w", "r")
    def __init__(self):
        self.w = {}
        self.r = {}


class Prog:
    def __init__(self, nc):
        self.nc = nc
        self.ops = {e: [] for e in CE}
        self.marks = {e: set() for e in CE}
        self.ch_cnt = {}
        self.regions = {}
        self.wm = {e: {} for e in CE}

    def _reg(self, name):
        return self.regions.setdefault(name, _Region())

    def _covered(self, eng, tok):
        kind, key, val = tok
        return self.wm[eng].get(("c", key) if kind == "c" else ("d", key), -1) >= val

    def _note(self, eng, tok):
        kind, key, val = tok
        k = ("c", key) if kind == "c" else ("d", key)
        self.wm[eng][k] = max(self.wm[eng].get(k, -1), val)
        if kind == "c":
            self.marks[key].add(val)

    def op(self, eng, emit, r=(), w=(), ch=None, tag=None):
        def norm(lst):
            return [((it, None) if isinstance(it, str) else tuple(it)) for it in lst]
        r, w = norm(r), norm(w)
        deps = []
        for name, t in r:
            reg = self._reg(name)
            for t2, tok in reg.w.items():
                if _conflict(t, t2):
                    deps.append(tok)
        for name, t in w:
            reg = self._reg(name)
            for t2, tok in reg.w.items():
                if _conflict(t, t2):
                    deps.append(tok)
            for t2, toks in reg.r.items():
                if _conflict(t, t2):
                    deps.extend(toks)
        if ch is not None and ch in self.ch_cnt:
            deps.append(("d", ch, self.ch_cnt[ch]))
        waits = []
        for tok in deps:
            kind, key, val = tok
            if kind == "c" and key == eng == "pe":
                continue
            if self._covered(eng, tok):
                continue
            waits.append(tok)
            self._note(eng, tok)
        idx = len(self.ops[eng])
        if ch is not None:
            self.ch_cnt[ch] = self.ch_cnt.get(ch, 0) + 16
            tok = ("d", ch, self.ch_cnt[ch])
        else:
            tok = ("c", eng, idx)
        self.ops[eng].append({"emit": emit, "waits": waits, "ch": ch, "tag": tag})
        for name, t in w:
            reg = self._reg(name)
            if t is None:
                reg.w = {}
                reg.r = {}
            else:
                reg.w.pop(t, None)
                reg.r.pop(t, None)
            reg.w[t] = tok
        for name, t in r:
            self._reg(name).r.setdefault(t, []).append(tok)
        return tok

    def pe(self, emit, r=(), w=(), tag=None):
        return self.op("pe", emit, r, w, tag=tag)
    def act(self, emit, r=(), w=(), tag=None):
        return self.op("act", emit, r, w, tag=tag)
    def dve(self, emit, r=(), w=(), tag=None):
        return self.op("dve", emit, r, w, tag=tag)
    def pool(self, emit, r=(), w=(), tag=None):
        return self.op("pool", emit, r, w, tag=tag)
    def dma(self, q, ch, emit, r=(), w=(), tag=None):
        assert q in DQ
        return self.op(q, emit, r, w, ch=ch, tag=tag)

    def barrier(self):
        toks = []
        for e in CE:
            for i in range(len(self.ops[e]) - 1, -1, -1):
                o = self.ops[e][i]
                if o["emit"] is not None and o["ch"] is None:
                    toks.append(("c", e, i))
                    break
        for chn, c in self.ch_cnt.items():
            toks.append(("d", chn, c))
        for e in CE:
            waits = []
            for tok in toks:
                kind, key, val = tok
                if kind == "c" and key == e:
                    continue
                if self._covered(e, tok):
                    continue
                waits.append(tok)
                self._note(e, tok)
            if waits:
                self.ops[e].append({"emit": None, "waits": waits, "ch": None,
                                    "tag": "barrier"})

    def finalize(self):
        nc = self.nc
        with contextlib.ExitStack() as es:
            sems = {}
            for s in CE:
                if self.marks[s]:
                    sems[s] = es.enter_context(nc.semaphore("sem_" + s))
            chsems = {c: es.enter_context(nc.semaphore("sch_" + c))
                      for c in self.ch_cnt}
            rank = {}
            for e in CE:
                ms = sorted(self.marks[e])
                rank[e] = {m: i + 1 for i, m in enumerate(ms)}

            block = es.enter_context(nc.Block())
            meths = {"sp": block.sync, "pe": block.tensor, "act": block.scalar,
                     "dve": block.vector, "pool": block.gpsimd}
            for ename in CE:
                def play(eng, ename=ename):
                    for i, o in enumerate(self.ops[ename]):
                        for kind, key, val in o["waits"]:
                            if kind == "d":
                                eng.wait_ge(chsems[key], val)
                            else:
                                eng.wait_ge(sems[key], rank[key][val])
                        if o["emit"] is None:
                            continue
                        instr = o["emit"](eng)
                        if o["ch"] is not None:
                            instr.then_inc(chsems[o["ch"]], 16)
                        elif i in self.marks[ename]:
                            instr.then_inc(sems[ename], 1)
                meths[ename](play)

    def stats(self):
        n = {e: len(self.ops[e]) for e in CE}
        waits = sum(len(o["waits"]) for e in CE for o in self.ops[e])
        return (f"ops={n} waits={waits} "
                f"marks={ {e: len(m) for e, m in self.marks.items()} } "
                f"nch={len(self.ch_cnt)}")


"""Megakernel emitters for mk_seq_postln_d2 (d_model 1024, T 2048/core).

Per-core layout conventions:
- residual stream: token-major fp32 SBUF [128, 16*1024]; tile t at cols
  [1024*t, 1024*t+1024); stream[p, 1024*t+d] = x[128*t+p, d].
- feature-major activations X^T: bf16 [128, 8*2048]; chunk c at cols
  [2048*c, ...): xt[p, 2048*c+tok] = x[tok, 128*c+p].
- matmul: out = lhsT.T @ rhs; contraction on partitions.
"""
import contextlib
import numpy as np
import ml_dtypes
import concourse.bass as bass
import concourse.mybir as mybir

F32 = mybir.dt.float32
I32 = mybir.dt.int32
BF16 = mybir.dt.bfloat16
AF = mybir.ActivationFunctionType
ALU = mybir.AluOpType
AX = mybir.AxisListType

T = 2048
NT = 16
D = 1024
NC = 8


def host_constants():
    c = {}
    c["c_ident"] = np.eye(128, dtype=ml_dtypes.bfloat16)
    c["c_ones"] = np.ones((2, 128), dtype=ml_dtypes.bfloat16)
    c["c_onesf"] = np.ones((128, 2), dtype=np.float32)
    a4 = np.arange(0, 64, 2, dtype=np.float32) / np.int32(64)
    inv4 = (1.0 / (10000.0 ** a4.astype(np.float64))).astype(np.float32)
    freqs4 = (np.arange(T, dtype=np.float32)[:, None] * inv4[None, :]).astype(np.float32)
    pos = np.concatenate([freqs4, freqs4], axis=1).astype(np.float64)
    cosT = np.ascontiguousarray(np.cos(pos).T)
    sinT = np.ascontiguousarray(np.sin(pos).T)
    sinT = np.concatenate([sinT[32:64], sinT[0:32]], axis=0)
    c["c_cos"] = cosT.astype(np.float32)
    c["c_cosl"] = (cosT - c["c_cos"].astype(np.float64)).astype(np.float32)
    c["c_sin"] = sinT.astype(np.float32)
    c["c_sinl"] = (sinT - c["c_sin"].astype(np.float64)).astype(np.float32)
    tri = (np.arange(128)[:, None] <= np.arange(128)[None, :])
    c["c_tri"] = tri.astype(ml_dtypes.bfloat16)
    c["c_recipn"] = np.tile(1.0 / np.arange(1, T + 1, dtype=np.float32)[None, :],
                            (128, 1))
    c["c_onesfr"] = np.ones((1, 128), dtype=np.float32)
    em = np.zeros((8, 4 * 128), dtype=ml_dtypes.bfloat16)
    for krc in range(4):
        for j in range(128):
            em[2 * krc + j // 64, 128 * krc + j] = 1
    c["c_emat"] = em
    return c


CONST_SPECS = [("c_ident", [128, 128], BF16), ("c_ones", [2, 128], BF16),
               ("c_emat", [8, 512], BF16),
               ("c_onesf", [128, 2], F32), ("c_onesfr", [1, 128], F32),
               ("c_cos", [64, T], F32), ("c_cosl", [64, T], F32),
               ("c_sin", [64, T], F32), ("c_sinl", [64, T], F32),
               ("c_tri", [128, 128], BF16),
               ("c_recipn", [128, T], F32)]



class WQuant:
    def __init__(self, core, wname, hbm, R, C, es, wT=None, sink=None,
                 csplit=None, q="sp"):
        self.core, self.p, self.u = core, core.p, core.uid()
        self.q = q
        cw = csplit or C
        self.R, self.C, self.cw = R, C, cw
        self.nrc, self.npc = R // 128, C // cw
        self.nch = self.nrc * self.npc
        self.nic = cw // 128
        assert self.nic <= 16
        self.wT, self.sink = wT, sink
        self.sinkname = wname + "_sink"
        u = self.u
        sb = core.sb
        self.macc = sb(f"macc{u}", [128, self.nch], F32, es)
        self.m1 = sb(f"m1{u}", [128, 1], F32, es)
        self.thr = sb(f"thr{u}", [128, 3], F32, es)
        self.msb = sb(f"msb{u}", [1, 1], F32, es)
        self.hbr = hbm.rearrange("(r p) c -> p r c", p=128)

    def alloc_mean(self, es, junk_ext=None):
        u, cw, sb = self.u, self.cw, self.core.sb
        self.mwch = sb(f"wmch{u}", [128, 2 * cw], F32, es)
        self.mjunk = junk_ext if junk_ext is not None else sb(
            f"wmjk{u}", [128, cw], BF16, es)

    def alloc_comp(self, es, ps_m=None):
        u, cw, sb, ps = self.u, self.cw, self.core.sb, self.core.ps
        self.wchunk = sb(f"wch{u}", [128, 2 * cw], F32, es)
        self.ge = sb(f"ge{u}", [128, 2 * cw], BF16, es)
        self.sg = sb(f"sg{u}", [128, 2 * cw], BF16, es)
        if self.sink is not None:
            self.stg = sb(f"wst{u}", [128, 2 * cw], BF16, es)
        self.ps_m = ps_m if ps_m is not None else ps(f"ps_wm{u}", [128, 512], F32, es)
        self.ps_t = ps(f"ps_w{u}", [128, cw], BF16, es)

    def _src(self, i):
        rc, pc = divmod(i, self.npc)
        return self.hbr[:, rc, self.cw * pc: self.cw * pc + self.cw]

    def mean_chunk(self, i):
        p, u = self.p, self.u
        s = i % 2
        wsl = self.mwch[:, self.cw * s: self.cw * s + self.cw]
        src = self._src(i)
        p.dma(self.q, f"wm{u}_{s}", lambda e, wsl=wsl, src=src: e.dma_start(
            out=wsl, in_=src), w=[(self.mwch.name, s)])
        p.act(lambda e, wsl=wsl, i=i: e.activation(
            self.mjunk[:], wsl, AF.Abs, accum_out=self.macc[:, i:i+1]),
              r=[(self.mwch.name, s)], w=[(self.macc.name, i), self.mjunk.name])

    def calc_thr(self, m_out=None, mkey=None):
        p, core = self.p, self.core
        ps_m = self.ps_m
        p.dve(lambda e: e.reduce_sum(self.m1[:], self.macc[:, 0:self.nch], axis=AX.X),
              r=[(self.macc.name, i) for i in range(self.nch)], w=[self.m1.name])
        p.pe(lambda e: e.matmul(ps_m[0:1, 0:1], core.t["c_onesf"][:, 0:1], self.m1[:],
                                start=True, stop=True),
             r=[self.m1.name, "c_onesf"], w=[(ps_m.name, "s")])
        p.dve(lambda e: e.tensor_scalar(self.msb[:], ps_m[0:1, 0:1],
                                        1.0 / (self.R * self.C), 1e-5,
                                        ALU.mult, ALU.max),
              r=[(ps_m.name, "s")], w=[self.msb.name])
        p.pe(lambda e: e.matmul(ps_m[:, 2:3], core.t["c_onesfr"][0:1, :],
                                self.msb[0:1, 0:1], start=True, stop=True),
             r=[self.msb.name, "c_onesfr"], w=[(ps_m.name, "bc")])
        p.dve(lambda e: e.tensor_scalar(self.thr[:, 0:1], ps_m[:, 2:3], 0.5, None,
                                        ALU.mult),
              r=[(ps_m.name, "bc")], w=[(self.thr.name, 0)])
        p.dve(lambda e: e.tensor_copy(self.thr[:, 1:2], ps_m[:, 2:3]),
              r=[(ps_m.name, "bc")], w=[(self.thr.name, 1)])
        p.dve(lambda e: e.tensor_scalar(self.thr[:, 2:3], ps_m[:, 2:3], -0.5, None,
                                        ALU.mult),
              r=[(ps_m.name, "bc")], w=[(self.thr.name, 2)])
        if m_out is not None:
            p.dve(lambda e: e.tensor_copy(m_out, self.thr[:, 1:2]),
                  r=[(self.thr.name, 1)], w=[mkey])

    def comp_load(self, i):
        if i >= self.nch:
            return
        p, u = self.p, self.u
        s = i % 2
        cw = self.cw
        wsl = self.wchunk[:, cw * s: cw * s + cw]
        src = self._src(i)
        p.dma(self.q, f"wl{u}_{s}", lambda e, wsl=wsl, src=src: e.dma_start(
            out=wsl, in_=src), w=[(self.wchunk.name, s)])

    def comp_calc(self, i):
        p, u = self.p, self.u
        s = i % 2
        cw, nic = self.cw, self.nic
        wsl = self.wchunk[:, cw * s: cw * s + cw]
        gesl = self.ge[:, cw * s: cw * s + cw]
        sgsl = self.sg[:, cw * s: cw * s + cw]
        p.dve(lambda e, wsl=wsl, gesl=gesl: e.tensor_single_scalar(
            gesl, wsl, self.thr[:, 0:1], ALU.is_ge),
              r=[(self.wchunk.name, s), (self.thr.name, 0)], w=[(self.ge.name, s)])
        p.dve(lambda e, wsl=wsl, sgsl=sgsl: e.tensor_single_scalar(
            sgsl, wsl, self.thr[:, 2:3], ALU.is_le),
              r=[(self.wchunk.name, s), (self.thr.name, 2)], w=[(self.sg.name, s)])
        p.dve(lambda e, gesl=gesl, sgsl=sgsl: e.tensor_sub(gesl, gesl, sgsl),
              r=[(self.sg.name, s)], w=[(self.ge.name, s)])
        for j in range(nic):
            p.pe(lambda e, gesl=gesl, j=j, s=s: e.transpose(
                self.ps_t[:, 128*j:128*j+128], gesl[:, 128*j:128*j+128],
                self.core.t["c_ident"][:]),
                 r=[(self.ge.name, s), "c_ident"], w=[(self.ps_t.name, j)])
        rks = [(self.ps_t.name, j) for j in range(nic)]
        rc, pc = divmod(i, self.npc)
        psrc = self.ps_t[:, 0:128*nic].rearrange("p (ic r) -> p ic r", r=128)
        if self.sink is None:
            dst = self.wT[:].rearrange("p (ic r) -> p ic r", r=self.R)[
                :, nic*pc:nic*pc+nic, 128*rc:128*rc+128]
            if i % 2 == 0:
                p.dve(lambda e, dst=dst: e.tensor_copy(dst, psrc),
                      r=rks, w=[(self.wT.name, ("rc", i))])
            else:
                p.act(lambda e, dst=dst: e.activation(dst, psrc, AF.Copy),
                      r=rks, w=[(self.wT.name, ("rc", i))])
        else:
            stg_sl = self.stg[:, cw*s:cw*s+cw]
            if i % 2 == 0:
                p.dve(lambda e, stg_sl=stg_sl: e.tensor_copy(stg_sl, psrc),
                      r=rks, w=[(self.stg.name, s)])
            else:
                p.act(lambda e, stg_sl=stg_sl: e.activation(stg_sl, psrc, AF.Copy),
                      r=rks, w=[(self.stg.name, s)])
            p.dma(self.q, f"ws{u}", lambda e, stg_sl=stg_sl, i=i: e.dma_start(
                out=self.sink[i], in_=stg_sl),
                  r=[(self.stg.name, s)], w=[(self.sinkname, i)])


class Core:
    def __init__(self, nc):
        self.nc = nc
        self.p = Prog(nc)
        self.es = contextlib.ExitStack()
        self.t = {}
        self.inp = {}
        self.out = {}
        self._uid = 0

    def dram_in(self, name, shape, dtype=F32):
        self.inp[name] = self.nc.dram_tensor(name, shape, dtype,
                                             kind="ExternalInput").ap()
        return self.inp[name]

    def dram_out(self, name, shape, dtype=F32):
        self.out[name] = self.nc.dram_tensor(name, shape, dtype,
                                             kind="ExternalOutput").ap()
        return self.out[name]

    def dram_tmp(self, name, shape, dtype):
        return self.nc.dram_tensor(name, shape, dtype, kind="Internal").ap()

    def sb(self, name, shape, dtype, es=None, side=None):
        kw = {"side": side} if side else {}
        h = (es or self.es).enter_context(self.nc.sbuf_tensor(name, shape, dtype, **kw))
        self.t[name] = h
        return h

    def ps(self, name, shape, dtype=F32, es=None):
        h = (es or self.es).enter_context(self.nc.psum_tensor(name, shape, dtype))
        self.t[name] = h
        return h

    def uid(self):
        self._uid += 1
        return self._uid

    def load_constants(self, names=("c_ident", "c_ones", "c_onesf", "c_onesfr", "c_emat")):
        p = self.p
        for name, shape, dt in CONST_SPECS:
            if name not in names:
                self.dram_in(name, shape, dt)
                continue
            ap = self.dram_in(name, shape, dt)
            t = self.sb(name + "_s", shape, dt)
            self.t[name] = t
            p.dma("sp", "const", lambda e, t=t, ap=ap: e.dma_start(out=t[:], in_=ap),
                  w=[name])
        eps = self.sb("c_eps", [128, 1], F32)
        p.dve(lambda e: e.memset(eps[:], 1e-5), w=["c_eps"])
        half = self.sb("c_half", [128, 1], F32)
        p.dve(lambda e: e.memset(half[:], 0.5), w=["c_half"])

    def load_const_named(self, name, es):
        spec = {n: (s, d) for n, s, d in CONST_SPECS}[name]
        t = self.sb(name + "_s", spec[0], spec[1], es)
        self.t[name] = t
        self.p.dma("sp", "const2", lambda e: e.dma_start(out=t[:], in_=self.inp[name]),
                   w=[name])
        return t

    def load_stream(self, x_hbm, stream):
        p = self.p
        xr = x_hbm.rearrange("(t p) d -> p t d", p=128)
        for g in range(4):
            dst = stream[:].rearrange("p (t d) -> p t d", d=D)[:, 4*g:4*g+4, :]
            p.dma("sp", f"ldst{g % 2}",
                  lambda e, dst=dst, g=g: e.dma_start(out=dst, in_=xr[:, 4*g:4*g+4, :]),
                  w=[(stream.name, ("tile", tt)) for tt in range(4*g, 4*g+4)])

    def store_stream(self, stream, y_hbm, wreg=None):
        p = self.p
        yr = y_hbm.rearrange("(t p) d -> p t d", p=128)
        for g in range(4):
            src = stream[:].rearrange("p (t d) -> p t d", d=D)[:, 4*g:4*g+4, :]
            p.dma("sp", f"stst{g % 2}",
                  lambda e, src=src, g=g: e.dma_start(out=yr[:, 4*g:4*g+4, :], in_=src),
                  r=[(stream.name, ("tile", tt)) for tt in range(4*g, 4*g+4)],
                  w=[] if wreg is None else [(wreg, g)])

    def transpose_stream(self, stream, xt, es, src_bf16=None, tiles=range(NT),
                         xt_cols=None, colbase=0, ptr_ext=None, xb_ext=None,
                         cast_eng="dve"):
        p = self.p
        u = self.uid()
        ptr = ptr_ext if ptr_ext is not None else self.ps(f"ps_tr{u}", [128, 2048], BF16, es)
        xb = xb_ext
        xbtag_none = xb_ext is not None
        if src_bf16 is None and xb is None:
            xb = self.sb(f"xb{u}", [128, 2 * D], BF16, es)
        for t in tiles:
            s = t % 2
            if src_bf16 is None:
                xbs = xb[:, D*s:D*s+D]
                ceng = p.pool if cast_eng == "pool" else p.dve
                ceng(lambda e, xbs=xbs, t=t: e.tensor_copy(xbs, stream[:, D*t:D*t+D]),
                     r=[(stream.name, ("tile", t))],
                     w=[(xb.name, None if xbtag_none else s)])
                src_name = xb.name
                src_tag = None if xbtag_none else s
            else:
                xbs = src_bf16[:, D*t:D*t+D]
                src_name = src_bf16.name
                src_tag = ("tile", t)
            for c in range(NC):
                p.pe(lambda e, xbs=xbs, c=c, s=s:
                     e.transpose(ptr[:, 1024*s+128*c:1024*s+128*c+128],
                                 xbs[:, 128*c:128*c+128], self.t["c_ident"][:]),
                     r=[(src_name, src_tag), "c_ident"], w=[(ptr.name, (s, c))])
            qw = xt_cols if xt_cols is not None else T
            dst = xt[:].rearrange("p (c q) -> p c q", q=qw)[:, :, colbase+128*t:colbase+128*t+128]
            src = ptr[:, 1024*s:1024*s+1024].rearrange("p (c q) -> p c q", q=128)
            rks = [(ptr.name, (s, c)) for c in range(NC)]
            if t % 2 == 0:
                p.dve(lambda e, dst=dst, src=src: e.tensor_copy(dst, src),
                      r=rks, w=[(xt.name, ("tt", t))])
            else:
                p.act(lambda e, dst=dst, src=src: e.activation(dst, src, AF.Copy),
                      r=rks, w=[(xt.name, ("tt", t))])

    def ln_stages(self, r_sl, out_sl, keys_r, keys_w, scr, s=0):
        p = self.p
        u = self.uid()
        sx, sq, mt, vt, junk = scr["sx"], scr["sq"], scr["mt"], scr["vt"], scr["junk"]
        sxs, sqs = sx[:, s:s+1], sq[:, s:s+1]
        mts, vts = mt[:, s:s+1], vt[:, s:s+1]
        return [
            lambda: p.dve(lambda e: e.reduce_sum(sxs, r_sl, axis=AX.X),
                          r=keys_r, w=[(sx.name, (u, s))]),
            lambda: p.act(lambda e: e.activation(junk[:], r_sl, AF.Square,
                                                 accum_out=sqs),
                          r=keys_r, w=[(sq.name, (u, s)), junk.name]),
            lambda: p.dve(lambda e: e.tensor_scalar(mts, sxs, 1.0 / D, None,
                                                    ALU.mult),
                          r=[(sx.name, (u, s))], w=[(mt.name, (u, s))]),
            lambda: p.dve(lambda e: e.scalar_tensor_tensor(vts, sqs, 1.0 / D, sqs,
                                                           ALU.mult, ALU.bypass),
                          r=[(sq.name, (u, s))], w=[(vt.name, (u, s, 0))]),
            lambda: p.dve(lambda e: e.scalar_tensor_tensor(vts, mts, mts, vts,
                                                           ALU.mult, ALU.subtract),
                          r=[(mt.name, (u, s)), (vt.name, (u, s, 0))],
                          w=[(vt.name, (u, s, 1))]),
            lambda: p.act(lambda e: e.activation(vts, vts, AF.Ln, scale=-1.0,
                                                 bias=self.t["c_eps"][:, 0:1]),
                          r=[(vt.name, (u, s, 1)), "c_eps"], w=[(vt.name, (u, s, 2))]),
            lambda: p.act(lambda e: e.activation(vts, vts, AF.Exp, scale=-0.5),
                          r=[(vt.name, (u, s, 2))], w=[(vt.name, (u, s, 3))]),
            lambda: p.dve(lambda e: e.tensor_scalar(out_sl, r_sl, mts, vts,
                                                    ALU.subtract, ALU.mult),
                          r=keys_r + [(mt.name, (u, s)), (vt.name, (u, s, 3))],
                          w=keys_w)]

    def ln_tile(self, r_sl, out_sl, keys_r, keys_w, scr, s=0):
        for f in self.ln_stages(r_sl, out_sl, keys_r, keys_w, scr, s):
            f()

    def ln_pair(self, specs, scr):
        chains = [self.ln_stages(r, o, kr, kw, scr, s=i)
                  for i, (r, o, kr, kw) in enumerate(specs)]
        for stg in range(8):
            for ch in chains:
                ch[stg]()

    def ln_scratch(self, pref, es):
        return {"sx": self.sb(pref + "_sx", [128, 2], F32, es),
                "sq": self.sb(pref + "_sq", [128, 2], F32, es),
                "mt": self.sb(pref + "_mt", [128, 2], F32, es),
                "vt": self.sb(pref + "_vt", [128, 2], F32, es),
                "junk": self.sb(pref + "_junk", [128, D], BF16, es)}

    def emit_l0_attn(self, stream, aux_alloc=None, aux_step=None):
        p = self.p
        with contextlib.ExitStack() as es:
            if aux_alloc is not None:
                aux_alloc(es)
            xt = self.sb("xt0", [128, NC * T], BF16, es)
            with contextlib.ExitStack() as es2:
                self.transpose_stream(stream, xt, es2)
                p.barrier()
            vaug = self.sb("vaug", [128, NT * 1040], BF16, es)
            wv = self.sb("wv", [128, NC * D], BF16, es)
            bvr = self.sb("bvr", [1, D], BF16, es)
            ps_a = self.ps("ps_a", [128, 512], F32, es)
            ps_s = self.ps("ps_s", [128, 2048], F32, es)
            ps_o = self.ps("ps_o", [128, 1024], F32, es)
            ps_tb = self.ps("ps_tb", [128, 1024], BF16, es)
            wvr = self.inp["L0_attn_Wv"].rearrange("(c p) o -> p c o", p=128)
            p.dma("pool", "wv", lambda e: e.dma_start(
                out=wv[:].rearrange("p (c o) -> p c o", o=D), in_=wvr),
                w=[wv.name])
            p.dma("pool", "bv", lambda e: e.dma_start(
                out=bvr[:], in_=self.inp["L0_attn_bv"].rearrange("(a o) -> a o", a=1)),
                w=[bvr.name])
            vg = vaug[:].rearrange("p (k h) -> p k h", h=65)
            for kt in range(NT):
                p.pool(lambda e, kt=kt: e.memset(vg[:, 16*kt:16*kt+16, 64:65], 1.0),
                       w=[(vaug.name, ("ones", kt))])
                for half in range(2):
                    for kc in range(NC):
                        lhs = xt[:].rearrange("p (c q) -> p c q", q=T)[:, kc, 128*kt:128*kt+128]
                        rhs = wv[:, D*kc + 512*half: D*kc + 512*half + 512]
                        p.pe(lambda e, lhs=lhs, rhs=rhs, kc=kc:
                             e.matmul(ps_a[:], lhs, rhs, start=(kc == 0), stop=False),
                             r=[(xt.name, ("tt", kt)), wv.name], w=[(ps_a.name, 0)])
                    p.pe(lambda e, half=half:
                         e.matmul(ps_a[:], self.t["c_ones"][0:1, :],
                                  bvr[0:1, 512*half:512*half+512],
                                  start=False, stop=True),
                         r=["c_ones", bvr.name], w=[(ps_a.name, 0)])
                    dst = vg[:, 16*kt + 8*half:16*kt + 8*half + 8, 0:64]
                    src = ps_a[:].rearrange("p (h d) -> p h d", d=64)
                    if (2 * kt + half) % 2 == 0:
                        p.dve(lambda e, dst=dst, src=src: e.tensor_copy(dst, src),
                              r=[(ps_a.name, 0)], w=[(vaug.name, ("v", kt, half))])
                    else:
                        p.act(lambda e, dst=dst, src=src: e.activation(dst, src, AF.Copy),
                              r=[(ps_a.name, 0)], w=[(vaug.name, ("v", kt, half))])
            qT = self.sb("qT", [64, 2 * T], BF16, es)
            kT = self.sb("kT", [64, 2 * T], BF16, es)
            wqk = self.sb("wqk", [128, 2 * NC * 128], BF16, es)
            bq = self.sb("bq", [128, 16], F32, es)
            for h in range(16):
                p.dma("sp", f"bq{h % 2}", lambda e, h=h: e.dma_start(
                    out=bq[:, h:h+1],
                    in_=self.inp["L0_attn_bqk"].rearrange("(a p) -> p a", p=128)[:, h:h+1]),
                    w=[(bq.name, h)])
            pT = self.sb("pT", [128, 2 * 1024], BF16, es)
            ssum = self.sb("ssum", [1, 512], F32, es)
            oT = self.sb("oT", [64, 512], BF16, es)
            rcol = self.sb("rcol", [128, 4], F32, es)
            wqkr = self.inp["L0_attn_Wqk"].rearrange("(c p) m -> p c m", p=128)
            for h in range(16):
                hs = h % 2
                wsl = wqk[:].rearrange("p (s c m) -> p s c m", s=2, m=128)[:, hs]
                p.dma("pool", f"wqk{hs}", lambda e, wsl=wsl, h=h: e.dma_start(
                    out=wsl, in_=wqkr[:, :, 128*h:128*h+128]),
                    w=[(wqk.name, hs)])
                for qc in range(4):
                    if qc % 2 == 0:
                        pa, wkey = ps_a[:], (ps_a.name, 0)
                    else:
                        pa, wkey = ps_s[:, 0:512], (ps_s.name, (0, 0))
                    for kc in range(NC):
                        lhs = wsl[:, kc, :]
                        rhs = xt[:].rearrange("p (c q) -> p c q", q=T)[:, kc, 512*qc:512*qc+512]
                        p.pe(lambda e, pa=pa, lhs=lhs, rhs=rhs, kc=kc:
                             e.matmul(pa, lhs, rhs, start=(kc == 0), stop=(kc == NC - 1)),
                             r=[(wqk.name, hs)] + [(xt.name, ("tt", tt))
                                                   for tt in range(4*qc, 4*qc+4)],
                             w=[wkey])
                    dstq = qT[:, T*hs + 512*qc: T*hs + 512*qc + 512]
                    p.dve(lambda e, pa=pa, dstq=dstq, h=h: e.tensor_scalar(
                        dstq, pa[0:64, :], bq[0:64, h:h+1], None, ALU.add),
                          r=[wkey, (bq.name, h)], w=[(qT.name, (hs, qc))])
                    dstk = kT[:, T*hs + 512*qc: T*hs + 512*qc + 512]
                    p.act(lambda e, pa=pa, dstk=dstk, h=h: e.activation(
                        dstk, pa[64:128, :], AF.Identity, bias=bq[64:128, h:h+1]),
                          r=[wkey, (bq.name, h)], w=[(kT.name, (hs, qc))])
                qk_keys = [(qT.name, (hs, qcc)) for qcc in range(4)] + \
                          [(kT.name, (hs, qcc)) for qcc in range(4)]
                pend0 = [None]
                for qh in range(2):
                    def emit_S(kt, qh=qh):
                        ss = kt % 2
                        for j in range(2):
                            lhs = kT[:, T*hs + 128*kt: T*hs + 128*kt + 128]
                            rhs = qT[:, T*hs + 1024*qh + 512*j: T*hs + 1024*qh + 512*j + 512]
                            p.pe(lambda e, lhs=lhs, rhs=rhs, ss=ss, j=j:
                                 e.matmul(ps_s[:, 1024*ss+512*j:1024*ss+512*j+512],
                                          lhs, rhs, start=True, stop=True),
                                 r=qk_keys, w=[(ps_s.name, (ss, j))])
                    emit_S(0)
                    if pend0[0] is not None:
                        pend0[0]()
                        pend0[0] = None
                    for kt in range(NT):
                        ss = kt % 2
                        p.act(lambda e, ss=ss:
                              e.activation(pT[:, 1024*ss:1024*ss+1024],
                                           ps_s[:, 1024*ss:1024*ss+1024],
                                           AF.Exp, scale=1.0 / 65536.0),
                              r=[(ps_s.name, (ss, 0)), (ps_s.name, (ss, 1))],
                              w=[(pT.name, ss)])
                        if kt + 1 < NT:
                            emit_S(kt + 1)
                        for j in range(2):
                            lhs = vaug[:, 1040*kt + 65*h: 1040*kt + 65*h + 65]
                            rhs = pT[:, 1024*ss + 512*j: 1024*ss + 512*j + 512]
                            p.pe(lambda e, lhs=lhs, rhs=rhs, j=j, kt=kt:
                                 e.matmul(ps_o[0:65, 512*j:512*j+512], lhs, rhs,
                                          start=(kt == 0), stop=(kt == NT - 1)),
                                 r=[(vaug.name, None), (pT.name, ss)],
                                 w=[(ps_o.name, j)])
                    def ep0(qh=qh):
                        for j in range(2):
                            qc = 2 * qh + j
                            p.dve(lambda e, j=j: e.tensor_scalar(
                                ssum[:], ps_o[64:65, 512*j:512*j+512], 32.0, None, ALU.mult),
                                  r=[(ps_o.name, j)], w=[ssum.name])
                            p.dve(lambda e, j=j: e.tensor_copy(
                                oT[:], ps_o[0:64, 512*j:512*j+512]),
                                  r=[(ps_o.name, j)], w=[oT.name])
                            for k in range(4):
                                p.pe(lambda e, k=k: e.transpose(
                                    ps_a[:, k:k+1], ssum[0:1, 128*k:128*k+128],
                                    self.t["c_onesf"][0:1, 0:1]),
                                     r=[ssum.name, "c_onesf"], w=[(ps_a.name, 0)])
                                p.pe(lambda e, k=k: e.transpose(
                                    ps_tb[:, 64*k:64*k+64], oT[:, 128*k:128*k+128],
                                    self.t["c_ident"][0:64, 0:64]),
                                     r=[oT.name, "c_ident"], w=[(ps_tb.name, k)])
                            p.dve(lambda e: e.reciprocal(rcol[:], ps_a[:, 0:4]),
                                  r=[(ps_a.name, 0)], w=[rcol.name])
                            for k in range(4):
                                tt = 4 * qc + k
                                out_sl = stream[:, 1024*tt + 64*h: 1024*tt + 64*h + 64]
                                p.dve(lambda e, k=k, out_sl=out_sl:
                                      e.scalar_tensor_tensor(
                                          out_sl, ps_tb[:, 64*k:64*k+64],
                                          rcol[:, k:k+1], out_sl, ALU.mult, ALU.add),
                                      r=[(ps_tb.name, k), rcol.name,
                                         (stream.name, ("tile", tt))],
                                      w=[(stream.name, ("tile", tt))])
                    if qh == 0:
                        pend0[0] = ep0
                    else:
                        ep0()
                if aux_step is not None:
                    aux_step(h)
            scr = self.ln_scratch("l0a", es)
            for t0 in range(0, NT, 2):
                specs = [(stream[:, D*t:D*t+D], stream[:, D*t:D*t+D],
                          [(stream.name, ("tile", t))],
                          [(stream.name, ("tile", t))]) for t in (t0, t0 + 1)]
                self.ln_pair(specs, scr)
            p.barrier()

    def quantize_weight_to_T(self, wname, hbm, R, C, wT, es):
        p = self.p
        u = self.uid()
        nrc, nic = R // 128, C // 128
        nbuf = 2 if C <= 1024 else 1
        wchunk = self.sb(f"wch{u}", [128, nbuf * C], F32, es)
        junk = self.sb(f"wjk{u}", [128, C], BF16, es)
        macc = self.sb(f"macc{u}", [128, nrc], F32, es)
        m1 = self.sb(f"m1{u}", [128, 1], F32, es)
        thr = self.sb(f"thr{u}", [128, 3], F32, es)
        nb2 = 2 if C <= 1024 else 1
        ge = self.sb(f"ge{u}", [128, nb2 * C], BF16, es)
        sg = self.sb(f"sg{u}", [128, nb2 * C], BF16, es)
        ps_m = self.ps(f"ps_m{u}", [128, 512], F32, es)
        ps_t = self.ps(f"ps_t{u}", [128, 2048], BF16, es)
        hbr = hbm.rearrange("(r p) c -> p r c", p=128)
        for rc in range(nrc):
            s = rc % nbuf
            wsl = wchunk[:, C*s:C*s+C]
            p.dma("sp", f"wl{u}_{s}", lambda e, wsl=wsl, rc=rc: e.dma_start(
                out=wsl, in_=hbr[:, rc, :]), w=[(wchunk.name, s)])
            p.act(lambda e, wsl=wsl, rc=rc: e.activation(
                junk[:], wsl, AF.Abs, accum_out=macc[:, rc:rc+1]),
                r=[(wchunk.name, s)], w=[(macc.name, rc), junk.name])
        p.dve(lambda e: e.reduce_sum(m1[:], macc[:, 0:nrc], axis=AX.X),
              r=[(macc.name, rc) for rc in range(nrc)], w=[m1.name])
        msb = self.sb(f"msb{u}", [1, 1], F32, es)
        p.pe(lambda e: e.matmul(ps_m[0:1, 0:1], self.t["c_onesf"][:, 0:1], m1[:],
                                start=True, stop=True),
             r=[m1.name, "c_onesf"], w=[(ps_m.name, "s")])
        p.dve(lambda e: e.tensor_scalar(msb[:], ps_m[0:1, 0:1],
                                        1.0 / (R * C), 1e-5, ALU.mult, ALU.max),
              r=[(ps_m.name, "s")], w=[msb.name])
        p.pe(lambda e: e.matmul(ps_m[:, 2:3], self.t["c_onesfr"][0:1, :],
                                msb[0:1, 0:1], start=True, stop=True),
             r=[msb.name, "c_onesfr"], w=[(ps_m.name, "bc")])
        p.dve(lambda e: e.tensor_scalar(thr[:, 0:1], ps_m[:, 2:3], 0.5, None,
                                        ALU.mult),
              r=[(ps_m.name, "bc")], w=[(thr.name, 0)])
        p.dve(lambda e: e.tensor_copy(thr[:, 1:2], ps_m[:, 2:3]),
              r=[(ps_m.name, "bc")], w=[(thr.name, 1)])
        p.dve(lambda e: e.tensor_scalar(thr[:, 2:3], ps_m[:, 2:3], -0.5, None,
                                        ALU.mult),
              r=[(ps_m.name, "bc")], w=[(thr.name, 2)])
        for rc in range(nrc):
            s = rc % nbuf
            s2 = rc % nb2
            wsl = wchunk[:, C*s:C*s+C]
            gesl = ge[:, C*s2:C*s2+C]
            sgsl = sg[:, C*s2:C*s2+C]
            wint = gesl
            p.dma("sp", f"wl{u}_{s}", lambda e, wsl=wsl, rc=rc: e.dma_start(
                out=wsl, in_=hbr[:, rc, :]), w=[(wchunk.name, s)])
            p.dve(lambda e, wsl=wsl, gesl=gesl: e.tensor_single_scalar(
                gesl, wsl, thr[:, 0:1], ALU.is_ge),
                  r=[(wchunk.name, s), (thr.name, 0)], w=[(ge.name, s2)])
            p.dve(lambda e, wsl=wsl, sgsl=sgsl: e.tensor_single_scalar(
                sgsl, wsl, thr[:, 2:3], ALU.is_le),
                  r=[(wchunk.name, s), (thr.name, 2)], w=[(sg.name, s2)])
            p.dve(lambda e, wint=wint, gesl=gesl, sgsl=sgsl: e.tensor_sub(
                wint, gesl, sgsl),
                  r=[(sg.name, s2)], w=[(ge.name, s2)])
            for g in range(nic // 16 if nic >= 16 else 1):
                nb = min(16, nic)
                for icb in range(nb):
                    ic = g * 16 + icb
                    p.pe(lambda e, wint=wint, ic=ic, icb=icb: e.transpose(
                        ps_t[:, 128*icb:128*icb+128], wint[:, 128*ic:128*ic+128],
                        self.t["c_ident"][:]),
                         r=[(ge.name, s2), "c_ident"], w=[(ps_t.name, icb)])
                dst = wT[:].rearrange("p (ic r) -> p ic r", r=R)[:, 16*g:16*g+nb,
                                                                128*rc:128*rc+128]
                eng = p.dve if rc % 2 == 0 else p.act
                if rc % 2 == 0:
                    p.dve(lambda e, dst=dst, nb=nb: e.tensor_copy(
                        dst, ps_t[:, 0:128*nb].rearrange("p (ic r) -> p ic r", r=128)),
                          r=[(ps_t.name, icb) for icb in range(nb)],
                          w=[(wT.name, ("rc", rc, g))])
                else:
                    p.act(lambda e, dst=dst, nb=nb: e.activation(
                        dst, ps_t[:, 0:128*nb].rearrange("p (ic r) -> p ic r", r=128),
                        AF.Copy),
                          r=[(ps_t.name, icb) for icb in range(nb)],
                          w=[(wT.name, ("rc", rc, g))])
        return thr

    def quantize_act_tile(self, xte, xcol, keys_r, rbf, m_out, scr, cols, mkey,
                          nsplit=1, nwide=None, split_keys=None, tail_pool=False):
        p = self.p
        u = self.uid()
        mr = scr["mr"]
        w = cols // nsplit
        x_full = xte[:, xcol:xcol+cols]
        p.dve(lambda e: e.reduce_max(mr[:], x_full, axis=AX.X,
                                     apply_absolute_value=True),
              r=keys_r, w=[(mr.name, (u, 0))])
        p.dve(lambda e: e.tensor_scalar_max(m_out, mr[:], 1e-5),
              r=[(mr.name, (u, 0))], w=[mkey])
        p.dve(lambda e: e.reciprocal(mr[:], m_out),
              r=[mkey], w=[(mr.name, (u, 1))])
        p.dve(lambda e: e.tensor_scalar(mr[:], mr[:], 127.0, None, ALU.mult),
              r=[(mr.name, (u, 1))], w=[(mr.name, (u, 2))])
        umr = u
        half = self.t["c_half"]

        def split_stages(i):
            s = i % 2
            tt = scr["t"][:, w*s:w*s+w]
            dd = scr["d"][:, w*s:w*s+w]
            vv = scr["v"][:, w*s:w*s+w]
            gt = scr["gt"][:, w*s:w*s+w]
            tk, dk, vk, gk = ("t", s), ("d", s), ("v", s), ("g", s)
            x_sl = xte[:, xcol + w*i: xcol + w*i + w]
            kr_i = split_keys[i] if split_keys is not None else keys_r
            ui = self.uid()
            st = [lambda: p.dve(lambda e: e.tensor_scalar(
                      tt, x_sl, mr[:, 0:1], None, ALU.mult),
                      r=kr_i + [(mr.name, (umr, 2))], w=[("qs", tk)]),
                  lambda: p.act(lambda e: e.activation(dd, tt, AF.Copy),
                      r=[("qs", tk)], w=[("qs", dk)]),
                  lambda: p.dve(lambda e: e.scalar_tensor_tensor(
                      vv, dd, -1.0, tt, ALU.mult, ALU.add),
                      r=[("qs", tk), ("qs", dk)], w=[("qs", vk)])]
            for k in (4.0, 2.0, 1.0):
                st.append(lambda: p.act(lambda e: e.activation(
                    gt, vv, AF.Sign, bias=half[:, 0:1]),
                    r=[("qs", vk), "c_half"], w=[("qs", gk)]))
                st.append(lambda k=k: p.dve(lambda e: e.scalar_tensor_tensor(
                    vv, gt, -k, vv, ALU.mult, ALU.add),
                    r=[("qs", gk)], w=[("qs", vk)]))
            st.append(lambda: p.dve(lambda e: e.tensor_single_scalar(
                gt, vv, -0.5, ALU.is_lt),
                r=[("qs", vk)], w=[("qs", gk)]))
            if tail_pool:
                st.append(lambda: p.pool(lambda e: e.tensor_sub(tt, tt, vv),
                    r=[("qs", vk)], w=[("qs", tk)]))
                st.append(lambda: p.pool(lambda e: e.tensor_sub(
                    rbf[:, w*i:w*i+w], tt, gt),
                    r=[("qs", tk), ("qs", gk)], w=[("rbf_out", (ui, i))]))
            else:
                st.append(lambda: p.dve(lambda e: e.scalar_tensor_tensor(
                    tt, vv, -1.0, tt, ALU.mult, ALU.add),
                    r=[("qs", vk)], w=[("qs", tk)]))
                st.append(lambda: p.dve(lambda e: e.tensor_sub(
                    rbf[:, w*i:w*i+w], tt, gt),
                    r=[("qs", tk), ("qs", gk)], w=[("rbf_out", (ui, i))]))
            return st

        for i0 in range(0, nsplit, 2):
            pair = [split_stages(i) for i in range(i0, min(i0 + 2, nsplit))]
            for stg in range(len(pair[0])):
                for sp in pair:
                    sp[stg]()
        return ("rbf_out", None)

    def l0_part1_start(self, x1sp, rt, mraw, es):
        p = self.p
        x1r = x1sp.rearrange("(t p) d -> p t d", p=128)
        x1rb = self.sb("x1rb", [128, 2 * D], F32, es)
        scr = {"mr": self.sb("q1mr", [128, 1], F32, es),
               "t": self.sb("q1t", [128, D], F32, es),
               "d": self.sb("q1d", [128, D], I32, es),
               "gt": self.sb("q1gt", [128, D], BF16, es),
               "v": self.sb("q1v", [128, D], F32, es)}
        rtile = self.sb("q1r", [128, D], BF16, es)
        ps_t = self.ps("q1ps", [128, 1024], BF16, es)

        def load(t):
            if t >= NT:
                return
            s = t % 2
            xsl = x1rb[:, D*s:D*s+D]
            p.dma("pool", f"x1rb{s}", lambda e, xsl=xsl, t=t: e.dma_start(
                out=xsl, in_=x1r[:, t, :]),
                  r=[("x1sp_r", t // 4)], w=[(x1rb.name, s)])

        def tile(t):
            s = t % 2
            self.quantize_act_tile(x1rb, D*s, [(x1rb.name, s)], rtile,
                                   mraw[:, t:t+1], scr, D,
                                   (mraw.name, t), nsplit=2)
            for c in range(NC):
                p.pe(lambda e, c=c: e.transpose(
                    ps_t[:, 128*c:128*c+128], rtile[:, 128*c:128*c+128],
                    self.t["c_ident"][:]),
                     r=[("rbf_out", None), "c_ident"], w=[(ps_t.name, c)])
            dst = rt[:].rearrange("p (c q) -> p c q", q=T)[:, :, 128*t:128*t+128]
            psrc = ps_t[:, 0:1024].rearrange("p (c q) -> p c q", q=128)
            rks = [(ps_t.name, c) for c in range(NC)]
            if t % 2 == 0:
                p.dve(lambda e, dst=dst: e.tensor_copy(dst, psrc),
                      r=rks, w=[(rt.name, ("tt", t))])
            else:
                p.act(lambda e, dst=dst: e.activation(dst, psrc, AF.Copy),
                      r=rks, w=[(rt.name, ("tt", t))])
        return load, tile

    def make_l0_wquants(self, wgT, wuT_sp, wdT_sp, es):
        INNER = 4096
        qg = WQuant(self, "wg", self.inp["L0_ffn_w_gate"], INNER, D, es,
                    wT=wgT, q="sp")
        qu = WQuant(self, "wu", self.inp["L0_ffn_w_up"], INNER, D, es,
                    sink=wuT_sp, q="pool")
        qd = WQuant(self, "wd", self.inp["L0_ffn_w_down"], D, INNER, es,
                    sink=wdT_sp, csplit=1024, q="act")
        return qg, qu, qd

    def emit_l0_ffn_phaseA(self, x1sp, rt, mraw, qs, mwg, mwu, mwd, es,
                           means_done=False):
        qg, qu, qd = qs
        if not means_done:
            with contextlib.ExitStack() as es_m:
                for q in qs:
                    q.alloc_mean(es_m)
                for i in range(32):
                    qg.mean_chunk(i)
                    qu.mean_chunk(i)
                    qd.mean_chunk(i)
            self.p.barrier()
        ps_m = self.ps("ps_mq", [128, 512], F32, es)
        for q in qs:
            q.alloc_comp(es, ps_m=ps_m)
        p1_load, p1 = self.l0_part1_start(x1sp, rt, mraw, es)
        qg.calc_thr(mwg[:, 0:1], mwg.name)
        qu.calc_thr(mwu[:, 0:1], mwu.name)
        qd.calc_thr(mwd[:, 0:1], mwd.name)
        for q in qs:
            q.comp_load(0)
            q.comp_load(1)
        p1_load(0)
        for t in range(NT):
            p1_load(t + 1)
            p1(t)
            for q in qs:
                q.comp_calc(2*t)
                q.comp_load(2*t + 2)
                q.comp_calc(2*t + 1)
                q.comp_load(2*t + 3)

    def emit_l0_ffn_part2(self, rt, mraw, rh_sp, scd, wgT, wuT, wuT_sp,
                          mwg, mwu):
        p = self.p
        INNER = 4096
        with contextlib.ExitStack() as es:
            for rc in range(32):
                dst = wuT[:].rearrange("p (ic r) -> p ic r", r=INNER)[
                    :, :, 128*rc:128*rc+128]
                wsrc = wuT_sp[rc].rearrange("p (ic r) -> p ic r", r=128)
                p.dma("sp", f"wult{rc % 2}", lambda e, dst=dst, wsrc=wsrc:
                      e.dma_start(out=dst, in_=wsrc),
                      r=[("wu_sink", rc)], w=[(wuT.name, ("ld", rc))])
            h = self.sb("hbuf", [128, INNER], F32, es)
            rh = self.sb("rhbuf", [128, INNER], BF16, es)
            scg = self.sb("scg", [128, 2], F32, es)
            scr = {"mr": self.sb("q2mr", [128, 1], F32, es),
                   "t": self.sb("q2t", [128, INNER // 4], F32, es),
                   "d": self.sb("q2d", [128, INNER // 4], I32, es),
                   "gt": self.sb("q2gt", [128, INNER // 4], BF16, es),
                   "v": self.sb("q2v", [128, INNER // 4], F32, es)}
            mh = self.sb("mh", [128, 1], F32, es)
            ps_g = self.ps("ps_g", [128, 2048], F32, es)
            ps_u = self.ps("ps_u", [128, 2048], F32, es)
            for t in range(NT):
                p.dve(lambda e, t=t: e.tensor_scalar(scg[:, 0:1], mraw[:, t:t+1],
                                                     mwg[:, 0:1], 1.0 / 127.0,
                                                     ALU.mult, ALU.mult),
                      r=[(mraw.name, t), mwg.name], w=[(scg.name, ("g", t))])
                p.dve(lambda e, t=t: e.tensor_scalar(scg[:, 1:2], mraw[:, t:t+1],
                                                     mwu[:, 0:1], 1.0 / 127.0,
                                                     ALU.mult, ALU.mult),
                      r=[(mraw.name, t), mwu.name], w=[(scg.name, ("u", t))])
                for oc in range(8):
                    s = oc % 4
                    gps = ps_g[:, 512*s:512*s+512]
                    ups = ps_u[:, 512*s:512*s+512]
                    for kc in range(NC):
                        lhs = rt[:].rearrange("p (c q) -> p c q", q=T)[:, kc, 128*t:128*t+128]
                        rhsg = wgT[:].rearrange("p (c o) -> p c o", o=INNER)[:, kc, 512*oc:512*oc+512]
                        rhsu = wuT[:].rearrange("p (c o) -> p c o", o=INNER)[:, kc, 512*oc:512*oc+512]
                        p.pe(lambda e, gps=gps, lhs=lhs, rhsg=rhsg, kc=kc:
                             e.matmul(gps, lhs, rhsg, start=(kc == 0), stop=(kc == NC-1)),
                             r=[(rt.name, ("tt", t)), (wgT.name, None)],
                             w=[(ps_g.name, s)])
                        p.pe(lambda e, ups=ups, lhs=lhs, rhsu=rhsu, kc=kc:
                             e.matmul(ups, lhs, rhsu, start=(kc == 0), stop=(kc == NC-1)),
                             r=[(rt.name, ("tt", t)), (wuT.name, None)],
                             w=[(ps_u.name, s)])
                    hsl = h[:, 512*oc:512*oc+512]
                    p.act(lambda e, hsl=hsl, gps=gps: e.activation(
                        hsl, gps, AF.Silu, scale=scg[:, 0:1]),
                          r=[(ps_g.name, s), (scg.name, ("g", t))],
                          w=[(h.name, ("g", oc)), (h.name, ("h", oc))])
                    p.dve(lambda e, hsl=hsl, ups=ups: e.scalar_tensor_tensor(
                        hsl, ups, scg[:, 1:2], hsl, ALU.mult, ALU.mult),
                          r=[(ps_u.name, s), (scg.name, ("u", t)), (h.name, ("g", oc))],
                          w=[(h.name, ("h", oc))])
                self.quantize_act_tile(h, 0,
                                       [(h.name, ("h", oc)) for oc in range(8)],
                                       rh, mh[:, 0:1], scr, INNER,
                                       (mh.name, t), nsplit=8,
                                       split_keys=[[(h.name, ("h", i // 1))]
                                                   for i in range(8)])
                p.dve(lambda e, t=t: e.tensor_scalar(scd[:, t:t+1], mh[:, 0:1],
                                                     1.0 / 127.0, None, ALU.mult),
                      r=[(mh.name, t)], w=[(scd.name, t)])
                p.dma("sp", f"rhsp{t % 2}", lambda e, t=t: e.dma_start(
                    out=rh_sp[t, :, :], in_=rh[:]),
                      r=[("rbf_out", None)], w=[("rh_sp", t)], tag=f"rhsp{t}")

    def emit_l0_ffn_part3(self, x1_src, rh_sp, scd, stream2, wdT_sp, mwd):
        p = self.p
        INNER = 4096
        with contextlib.ExitStack() as es:
            wdT = self.sb("wdT", [128, 32 * D], BF16, es)
            for i in range(32):
                rc, pc = divmod(i, 4)
                dst = wdT[:].rearrange("p (ic r) -> p ic r", r=D)[
                    :, 8*pc:8*pc+8, 128*rc:128*rc+128]
                wsrc = wdT_sp[i].rearrange("p (ic r) -> p ic r", r=128)
                p.dma("sp", f"wdlt{i % 4}", lambda e, dst=dst, wsrc=wsrc:
                      e.dma_start(out=dst, in_=wsrc),
                      r=[("wd_sink", i)], w=[(wdT.name, ("ld", i))])
            rhl = self.sb("rhl", [128, 2 * INNER], BF16, es)
            rhT = self.sb("rhTb", [128, 2 * INNER], BF16, es)
            x1t = self.sb("x1t", [128, 2 * D], F32, es)
            scf = self.sb("scf", [128, 1], F32, es)
            ps_d = self.ps("ps_d", [128, 1024], F32, es)
            ps_t3 = self.ps("ps_t3", [128, 4096], BF16, es)
            x1r = x1_src.rearrange("(t p) d -> p t d", p=128)
            scr = self.ln_scratch("l0f", es)
            for t in range(NT):
                s = t % 2
                lsl = rhl[:, INNER*s:INNER*s+INNER]
                xsl = x1t[:, D*s:D*s+D]
                p.dma("sp", f"rhld{s}", lambda e, lsl=lsl, t=t: e.dma_start(
                    out=lsl, in_=rh_sp[t, :, :]),
                      r=[("rh_sp", t)], w=[(rhl.name, s)])
                rsl = rhT[:, INNER*s:INNER*s+INNER]
                for g2 in range(2):
                    st3 = (2 * t + g2) % 2
                    pt3 = ps_t3[:, 2048*st3:2048*st3+2048]
                    for icb in range(16):
                        ic = 16 * g2 + icb
                        p.pe(lambda e, pt3=pt3, lsl=lsl, ic=ic, icb=icb: e.transpose(
                            pt3[:, 128*icb:128*icb+128], lsl[:, 128*ic:128*ic+128],
                            self.t["c_ident"][:]),
                             r=[(rhl.name, s), "c_ident"], w=[(ps_t3.name, (st3, icb))])
                    rks3 = [(ps_t3.name, (st3, icb)) for icb in range(16)]
                    dst3 = rsl[:, 2048*g2:2048*g2+2048]
                    if g2 == 0:
                        p.dve(lambda e, dst3=dst3, pt3=pt3: e.tensor_copy(dst3, pt3),
                              r=rks3, w=[(rhT.name, (s, 0))])
                    else:
                        p.act(lambda e, dst3=dst3, pt3=pt3: e.activation(dst3, pt3, AF.Copy),
                              r=rks3, w=[(rhT.name, (s, 1))])
                p.dma("pool", f"x1ld{s}", lambda e, xsl=xsl, t=t: e.dma_start(
                    out=xsl, in_=x1r[:, t, :]),
                      r=[("x1sp_r", t // 4)], w=[(x1t.name, s)])
                p.dve(lambda e, t=t: e.tensor_scalar(scf[:], scd[:, t:t+1],
                                                     mwd[:, 0:1], None, ALU.mult),
                      r=[(scd.name, t), mwd.name], w=[scf.name])
                for nc2 in range(2):
                    dps = ps_d[:, 512*nc2:512*nc2+512]
                    for kc in range(32):
                        lhs = rsl[:, 128*kc:128*kc+128]
                        rhs = wdT[:, 1024*kc + 512*nc2: 1024*kc + 512*nc2 + 512]
                        p.pe(lambda e, dps=dps, lhs=lhs, rhs=rhs, kc=kc:
                             e.matmul(dps, lhs, rhs, start=(kc == 0), stop=(kc == 31)),
                             r=[(rhT.name, (s, 0)), (rhT.name, (s, 1))]
                             + [(wdT.name, ("ld", 4*rc2 + kc//8)) for rc2 in range(8)],
                             w=[(ps_d.name, nc2)])
                    out_sl = stream2[:, D*t + 512*nc2: D*t + 512*nc2 + 512]
                    xs2 = xsl[:, 512*nc2:512*nc2+512]
                    p.dve(lambda e, dps=dps, out_sl=out_sl, xs2=xs2:
                          e.scalar_tensor_tensor(out_sl, dps, scf[:, 0:1], xs2,
                                                 ALU.mult, ALU.add),
                          r=[(ps_d.name, nc2), scf.name, (x1t.name, s)],
                          w=[(stream2.name, ("pre", t, nc2))])
                if t % 2 == 1:
                    specs = [(stream2[:, D*t2:D*t2+D], stream2[:, D*t2:D*t2+D],
                              [(stream2.name, ("pre", t2, 0)),
                               (stream2.name, ("pre", t2, 1))],
                              [(stream2.name, ("tile", t2))])
                             for t2 in (t - 1, t)]
                    self.ln_pair(specs, scr)
            p.barrier()

    def rope_stages(self, src, dst, W, cb, scr, keys_r, keys_w, s=0):
        p = self.p
        rb, cbuf = scr["rb"], scr["cbuf"]
        o = 512 * s
        cos, sin = self.t["c_cos"], self.t["c_sin"]
        return [
            lambda: p.dve(lambda e: e.scalar_tensor_tensor(
                rb[0:32, o:o+W], src[32:64, 0:W], -1.0,
                sin[32:64, cb:cb+W], ALU.mult, ALU.mult),
                r=keys_r + ["c_sin"], w=[(rb.name, (s, 0))]),
            lambda: p.dve(lambda e: e.scalar_tensor_tensor(
                rb[32:64, o:o+W], src[0:32, 0:W], 1.0,
                sin[0:32, cb:cb+W], ALU.mult, ALU.mult),
                r=keys_r + ["c_sin"], w=[(rb.name, (s, 1))]),
            lambda: p.pool(lambda e: e.tensor_mul(cbuf[0:64, o:o+W], src[0:64, 0:W],
                                                  cos[0:64, cb:cb+W]),
                r=keys_r + ["c_cos"], w=[(cbuf.name, s)]),
            lambda: p.dve(lambda e: e.tensor_add(dst, cbuf[0:64, o:o+W],
                                                 rb[0:64, o:o+W]),
                r=[(rb.name, (s, 0)), (rb.name, (s, 1)), (cbuf.name, s)],
                w=keys_w)]

    def rope(self, src, dst, W, cb, scr, keys_r, keys_w, s=0):
        for f in self.rope_stages(src, dst, W, cb, scr, keys_r, keys_w, s):
            f()

    def emit_l1_attn(self, stream):
        p = self.p
        with contextlib.ExitStack() as es:
            cos = self.load_const_named("c_cos", es)
            sin = self.load_const_named("c_sin", es)
            tri = self.load_const_named("c_tri", es)
            with contextlib.ExitStack() as eslo:
                for nm, hi in [("c_cosl", cos), ("c_sinl", sin)]:
                    lo = self.load_const_named(nm, eslo)
                    p.dve(lambda e, hi=hi, lo=lo: e.tensor_add(hi[:], hi[:], lo[:]),
                          r=[nm, hi.name.replace("_s", "")],
                          w=[hi.name.replace("_s", "")])
                p.barrier()
            xt = self.sb("xt1", [128, NC * T], BF16, es)
            with contextlib.ExitStack() as es2:
                self.transpose_stream(stream, xt, es2)
                p.barrier()
            wq = self.sb("wq1", [128, NC * D], BF16, es)
            wo = self.sb("wo1", [128, NC * D], BF16, es)
            wkv = self.sb("wkv1", [128, NC * 128], BF16, es)
            for nm, hb in [("wq1", "L1_attn_Wq"), ("wo1", "L1_attn_Wo")]:
                t = self.t[nm]
                hr = self.inp[hb].rearrange("(c p) o -> p c o", p=128)
                p.dma("pool", nm, lambda e, t=t, hr=hr: e.dma_start(
                    out=t[:].rearrange("p (c o) -> p c o", o=D), in_=hr), w=[nm])
            hrkv = self.inp["L1_attn_Wkv"].rearrange("(c p) o -> p c o", p=128)
            p.dma("pool", "wkv1", lambda e: e.dma_start(
                out=wkv[:].rearrange("p (c o) -> p c o", o=128), in_=hrkv), w=["wkv1"])
            kTf = self.sb("kTf", [64, 512], F32, es)
            kTb = self.sb("kTb", [64, T], BF16, es)
            vaug1 = self.sb("vaug1", [128, NT * 65], BF16, es)
            scr_r = {"rb": self.sb("rrb", [64, 1024], F32, es),
                     "cbuf": self.sb("rcb", [64, 1024], F32, es)}
            qTq = self.sb("qTq", [64, 16 * 512], BF16, es)
            tmpq = self.sb("tmpq", [128, 1024], F32, es)
            pTb = self.sb("pTb", [128, 2 * 512], BF16, es)
            hi_t = self.sb("hi_t", [1, 512], BF16, es)
            lo_t = self.sb("lo_t", [1, 512], BF16, es)
            rrow = self.sb("rrow", [1, 512], F32, es)
            og = self.sb("og", [64, 512], BF16, es)
            oTn = self.sb("oTn", [128, NC * 512], BF16, es)
            ps_a = self.ps("ps_a1", [128, 1024], F32, es)
            ps_s = self.ps("ps_s1", [128, 1024], F32, es)
            ps_o = self.ps("ps_o1", [128, 1024], F32, es)
            ps_b = self.ps("ps_b1", [128, 1024], F32, es)
            for qc in range(4):
                for kc in range(NC):
                    lhs = wkv[:, 128*kc:128*kc+64]
                    rhs = xt[:].rearrange("p (c q) -> p c q", q=T)[:, kc, 512*qc:512*qc+512]
                    p.pe(lambda e, lhs=lhs, rhs=rhs, kc=kc:
                         e.matmul(ps_a[0:64, 0:512], lhs, rhs, start=(kc == 0), stop=(kc == NC-1)),
                         r=["wkv1"] + [(xt.name, ("tt", tt)) for tt in range(4*qc, 4*qc+4)],
                         w=[(ps_a.name, 0)])
                p.dve(lambda e, qc=qc: e.tensor_copy(kTf[:, 0:512], ps_a[0:64, 0:512]),
                      r=[(ps_a.name, 0)], w=[(kTf.name, None)])
                self.rope(kTf, kTb[0:64, 512*qc:512*qc+512], 512, 512*qc, scr_r,
                          [(kTf.name, None)], [(kTb.name, ("kc", qc))])
            for kt in range(NT):
                sb2 = kt % 2
                pb = ps_b[:, 512*sb2:512*sb2+64]
                for kc in range(NC):
                    lhs = xt[:].rearrange("p (c q) -> p c q", q=T)[:, kc, 128*kt:128*kt+128]
                    rhs = wkv[:, 128*kc+64:128*kc+128]
                    p.pe(lambda e, pb=pb, lhs=lhs, rhs=rhs, kc=kc:
                         e.matmul(pb, lhs, rhs, start=(kc == 0), stop=(kc == NC-1)),
                         r=["wkv1", (xt.name, ("tt", kt))], w=[(ps_b.name, sb2)])
                p.dve(lambda e, pb=pb, kt=kt: e.tensor_copy(vaug1[:, 65*kt:65*kt+64],
                                                            pb),
                      r=[(ps_b.name, sb2)], w=[(vaug1.name, ("v", kt))])
                p.pool(lambda e, kt=kt: e.memset(vaug1[:, 65*kt+64:65*kt+65], 1.0),
                       w=[(vaug1.name, ("o", kt))])
            scr = self.ln_scratch("l1a", es)
            for qc in range(4):
                for oc in range(NC):
                    sa = oc % 2
                    pa = ps_a[:, 512*sa:512*sa+512]
                    for kc in range(NC):
                        lhs = wq[:, 1024*kc + 128*oc: 1024*kc + 128*oc + 128]
                        rhs = xt[:].rearrange("p (c q) -> p c q", q=T)[:, kc, 512*qc:512*qc+512]
                        p.pe(lambda e, pa=pa, lhs=lhs, rhs=rhs, kc=kc:
                             e.matmul(pa, lhs, rhs, start=(kc == 0), stop=(kc == NC-1)),
                             r=["wq1"] + [(xt.name, ("tt", tt)) for tt in range(4*qc, 4*qc+4)],
                             w=[(ps_a.name, sa)])
                    chains = []
                    for hh in range(2):
                        h = 2 * oc + hh
                        tsl = tmpq[:, 512*hh:512*hh+512]
                        def cp(hh=hh, sa=sa, tsl=tsl):
                            p.dve(lambda e, hh=hh: e.tensor_copy(
                                tsl[0:64, :], ps_a[64*hh:64*hh+64, 512*sa:512*sa+512]),
                                  r=[(ps_a.name, sa)], w=[(tmpq.name, hh)])
                        chains.append([cp] + self.rope_stages(
                            tsl, qTq[0:64, 512*h:512*h+512], 512, 512*qc,
                            scr_r, [(tmpq.name, hh)], [(qTq.name, (qc, h))], s=hh))
                    for stg in range(5):
                        chains[0][stg]()
                        chains[1][stg]()
                def make_ep(h, qc=qc):
                    hp = h % 2
                    def ep():
                        p.dve(lambda e, hp=hp: e.reciprocal(
                            rrow[:], ps_o[64:65, 512*hp:512*hp+512]),
                              r=[(ps_o.name, hp)], w=[rrow.name])
                        p.act(lambda e: e.activation(hi_t[:], rrow[:], AF.Copy),
                              r=[rrow.name], w=[hi_t.name])
                        p.dve(lambda e: e.tensor_sub(lo_t[:], rrow[:], hi_t[:]),
                              r=[rrow.name, hi_t.name], w=[lo_t.name])
                        p.pe(lambda e, hp=hp: e.matmul(ps_b[:, 512*hp:512*hp+512],
                                                       self.t["c_ones"][0:1, :], hi_t[:],
                                                       start=True, stop=False),
                             r=[hi_t.name, "c_ones"], w=[(ps_b.name, hp)])
                        p.pe(lambda e, hp=hp: e.matmul(ps_b[:, 512*hp:512*hp+512],
                                                       self.t["c_ones"][0:1, :], lo_t[:],
                                                       start=False, stop=True),
                             r=[lo_t.name, "c_ones"], w=[(ps_b.name, hp)])
                        p.dve(lambda e, hp=hp: e.tensor_copy(og[:], ps_o[0:64, 512*hp:512*hp+512]),
                              r=[(ps_o.name, hp)], w=[og.name])
                        dst = oTn[64*(h % 2):64*(h % 2)+64, 512*(h // 2):512*(h // 2)+512]
                        p.dve(lambda e, dst=dst, hp=hp: e.tensor_mul(dst, og[:],
                                                                     ps_b[0:64, 512*hp:512*hp+512]),
                              r=[og.name, (ps_b.name, hp)], w=[(oTn.name, h)])
                    return ep
                pend = [None]
                for h in range(16):
                    nkt = 4 * qc + 4
                    def emit_S1(kt, h=h):
                        ss = kt % 2
                        psl = ps_s[:, 512*ss:512*ss+512]
                        p.pe(lambda e, psl=psl, kt=kt, h=h:
                             e.matmul(psl, kTb[0:64, 128*kt:128*kt+128],
                                      qTq[0:64, 512*h:512*h+512], start=True, stop=True),
                             r=[(kTb.name, None), (qTq.name, (qc, h))],
                             w=[(ps_s.name, ss)])
                    emit_S1(0)
                    for kt in range(nkt):
                        ss = kt % 2
                        psl = ps_s[:, 512*ss:512*ss+512]
                        ptl = pTb[:, 512*ss:512*ss+512]
                        p.act(lambda e, ptl=ptl, psl=psl: e.activation(
                            ptl, psl, AF.Exp, scale=0.125),
                              r=[(ps_s.name, ss)], w=[(pTb.name, ss)])
                        if kt + 1 < nkt:
                            emit_S1(kt + 1)
                        b = kt - 4 * qc
                        if b >= 0:
                            if b > 0:
                                p.pool(lambda e, ptl=ptl, b=b: e.memset(
                                    ptl[:, 0:128*b], 0.0),
                                       r=[], w=[(pTb.name, ss)])
                            p.pool(lambda e, ptl=ptl, b=b: e.tensor_mul(
                                ptl[:, 128*b:128*b+128], ptl[:, 128*b:128*b+128],
                                tri[:]),
                                   r=["c_tri"], w=[(pTb.name, ss)])
                        hp = h % 2
                        p.pe(lambda e, ptl=ptl, kt=kt, nkt=nkt, hp=hp:
                             e.matmul(ps_o[0:65, 512*hp:512*hp+512],
                                      vaug1[:, 65*kt:65*kt+65], ptl,
                                      start=(kt == 0), stop=(kt == nkt - 1)),
                             r=[(vaug1.name, None), (pTb.name, ss)],
                             w=[(ps_o.name, hp)])
                        if kt == 1 and pend[0] is not None:
                            pend[0]()
                            pend[0] = None
                    pend[0] = make_ep(h)
                if pend[0] is not None:
                    pend[0]()
                    pend[0] = None
                for sub in range(4):
                    tt = 4 * qc + sub
                    for half in range(2):
                        sa = (2 * sub + half) % 2
                        pa = ps_a[:, 512*sa:512*sa+512]
                        for kc in range(NC):
                            lhs = oTn[:, 512*kc + 128*sub: 512*kc + 128*sub + 128]
                            rhs = wo[:, 1024*kc + 512*half: 1024*kc + 512*half + 512]
                            p.pe(lambda e, pa=pa, lhs=lhs, rhs=rhs, kc=kc:
                                 e.matmul(pa, lhs, rhs, start=(kc == 0), stop=(kc == NC-1)),
                                 r=["wo1"] + [(oTn.name, hh) for hh in range(16)],
                                 w=[(ps_a.name, sa)])
                        out_sl = stream[:, D*tt + 512*half: D*tt + 512*half + 512]
                        p.dve(lambda e, pa=pa, out_sl=out_sl: e.tensor_add(out_sl, pa, out_sl),
                              r=[(ps_a.name, sa), (stream.name, ("tile", tt))],
                              w=[(stream.name, ("tile", tt))])
                    if sub % 2 == 1:
                        specs = [(stream[:, D*t2:D*t2+D], stream[:, D*t2:D*t2+D],
                                  [(stream.name, ("tile", t2))],
                                  [(stream.name, ("tile", t2))])
                                 for t2 in (tt - 1, tt)]
                        self.ln_pair(specs, scr)
            p.barrier()

    def prep_wT_stages(self, nm, hbm, es, half=0):
        p = self.p
        u = self.uid()
        sp = self.dram_tmp(f"sp_{nm}", [NC, 128, D], BF16)
        nat32 = self.sb(f"nat32{u}", [128, 2 * D], F32, es)
        nat = self.sb(f"nat{u}", [128, 2 * D], BF16, es)
        stg = self.sb(f"stg{u}", [128, 2 * D], BF16, es)
        ps_t = self.t["ps_prep"]
        ho = 1024 * half
        hbr = hbm.rearrange("(r p) c -> p r c", p=128)

        def load(rc):
            if rc >= NC:
                return
            s = rc % 2
            n3 = nat32[:, D*s:D*s+D]
            p.dma("sp", f"pw{u}{s}", lambda e, n3=n3, rc=rc: e.dma_start(
                out=n3, in_=hbr[:, rc, :]), w=[(nat32.name, s)])

        def mk(rc):
            def f():
                if rc == 0:
                    load(0)
                load(rc + 1)
                s = rc % 2
                nsl = nat[:, D*s:D*s+D]
                n3 = nat32[:, D*s:D*s+D]
                if (rc + half) % 2 == 0:
                    p.act(lambda e, nsl=nsl, n3=n3: e.activation(nsl, n3, AF.Copy),
                          r=[(nat32.name, s)], w=[(nat.name, s)])
                else:
                    p.dve(lambda e, nsl=nsl, n3=n3: e.tensor_copy(nsl, n3),
                          r=[(nat32.name, s)], w=[(nat.name, s)])
                for jc in range(NC):
                    p.pe(lambda e, nsl=nsl, jc=jc: e.transpose(
                        ps_t[:, ho + 128*jc: ho + 128*jc + 128],
                        nsl[:, 128*jc:128*jc+128], self.t["c_ident"][:]),
                         r=[(nat.name, s), "c_ident"], w=[(ps_t.name, (half, jc))])
                ssl = stg[:, D*s:D*s+D]
                rks = [(ps_t.name, (half, jc)) for jc in range(NC)]
                if (rc + half) % 2 == 0:
                    p.dve(lambda e, ssl=ssl: e.tensor_copy(ssl, ps_t[:, ho:ho+1024]),
                          r=rks, w=[(stg.name, s)])
                else:
                    p.act(lambda e, ssl=ssl: e.activation(ssl, ps_t[:, ho:ho+1024],
                                                          AF.Copy),
                          r=rks, w=[(stg.name, s)])
                p.dma("sp", f"pws{u}{s}", lambda e, ssl=ssl, rc=rc: e.dma_start(
                    out=sp.rearrange("c p o -> c p o")[:, :, 128*rc:128*rc+128]
                        .rearrange("c p o -> p c o"),
                    in_=ssl[:].rearrange("p (c o) -> p c o", o=128)),
                      r=[(stg.name, s)], w=[(f"sp_{nm}", rc)])
            return f
        return sp, [mk(rc) for rc in range(NC)]

    def branch(self, qT, kT, qkeys, Vf, Wf, UT, bmat, carry, ci, pP, hT, scr1, qc,
               ps, alpha_col=None, fA=None, bcA=None, totT=None):
        p = self.p
        u = self.uid()
        ps_x, ps_y, ps_u, ps_l = ps
        eg, egb, rw, lg = scr1["eg"], scr1["egb"], scr1["rw"], scr1["lg"]
        cum, tmp = scr1["cum"], scr1["tmp"]
        rb1 = scr1["rb1"]; rb2e = scr1["rb2e"]; hi_t = scr1["hi"]; lo_t = scr1["lo"]
        for oc in range(NC):
            s = oc % 2
            hsl = hT[:, 512*s:512*s+512]
            for kc in range(NC):
                lhs = scr1["rw1T"][:, D*kc + 128*oc: D*kc + 128*oc + 128]
                rhs = qT[:, 512*kc:512*kc+512]
                p.pe(lambda e, lhs=lhs, rhs=rhs, kc=kc:
                     e.matmul(ps_l[:, 0:512], lhs, rhs, start=(kc == 0), stop=(kc == NC-1)),
                     r=qkeys + [("wslot", None)], w=[(ps_l.name, 0)])
            p.act(lambda e, hsl=hsl, oc=oc: e.activation(hsl, ps_l[:, 0:512],
                                                         AF.Gelu, bias=rb1[:, oc:oc+1]),
                  r=[(ps_l.name, 0), "rb1"], w=[(hT.name, s)])
            lhs2 = scr1["rw2T"][:, 8*oc:8*oc+8]
            p.pe(lambda e, lhs2=lhs2, hsl=hsl, oc=oc:
                 e.matmul(ps_l[0:8, 512:1024], lhs2, hsl,
                          start=(oc == 0), stop=(oc == NC-1)),
                 r=[(hT.name, s), ("rw2T", None)], w=[(ps_l.name, 1)])
        p.act(lambda e: e.activation(lg[:], ps_l[0:8, 512:1024], AF.Identity,
                                     bias=rb2e[0:8, 0:1]),
              r=[(ps_l.name, 1), "rb2e"], w=[lg.name])
        p.act(lambda e: e.activation(eg[:], lg[:], AF.Exp),
              r=[lg.name], w=[eg.name])
        p.dve(lambda e: e.tensor_copy(egb[:], eg[:]),
              r=[eg.name], w=[egb.name])
        p.pe(lambda e: e.matmul(ps_l[0:1, 512:1024], self.t["c_onesf"][0:8, 0:1], eg[:],
                                start=True, stop=True),
             r=[eg.name, "c_onesf"], w=[(ps_l.name, 1)])
        p.dve(lambda e: e.reciprocal(rw[:], ps_l[0:1, 512:1024]),
              r=[(ps_l.name, 1)], w=[rw.name])
        p.dve(lambda e: e.tensor_copy(hi_t[:], rw[:]), r=[rw.name], w=[hi_t.name])
        p.dve(lambda e: e.tensor_sub(lo_t[:], rw[:], hi_t[:]),
              r=[rw.name, hi_t.name], w=[lo_t.name])
        if alpha_col is not None:
            p.dve(lambda e: e.tensor_scalar(hi_t[:], hi_t[:], alpha_col[0:1, 0:1],
                                            None, ALU.mult),
                  r=[hi_t.name, "alpha"], w=[hi_t.name])
            p.dve(lambda e: e.tensor_scalar(lo_t[:], lo_t[:], alpha_col[0:1, 0:1],
                                            None, ALU.mult),
                  r=[lo_t.name, "alpha"], w=[lo_t.name])
        p.pe(lambda e: e.matmul(ps_l[:, 512:1024], self.t["c_ones"][0:1, :], hi_t[:],
                                start=True, stop=False),
             r=[hi_t.name, "c_ones"], w=[(ps_l.name, 1)])
        p.pe(lambda e: e.matmul(ps_l[:, 512:1024], self.t["c_ones"][0:1, :], lo_t[:],
                                start=False, stop=True),
             r=[lo_t.name, "c_ones"], w=[(ps_l.name, 1)])
        bca = bcA
        p.act(lambda e: e.activation(bca[:], ps_l[:, 512:1024], AF.Copy),
              r=[(ps_l.name, 1)], w=[bca.name])
        for krc in range(4):
            for dc in range(NC):
                lhsv = Vf[:, 512*dc + 128*krc: 512*dc + 128*krc + 128]
                lhsw = Wf[:, 512*dc + 128*krc: 512*dc + 128*krc + 128]
                rhsq = qT[:, 512*dc:512*dc+512]
                rhsk = kT[:, 512*dc:512*dc+512]
                p.pe(lambda e, lhsv=lhsv, rhsq=rhsq, dc=dc:
                     e.matmul(ps_x[:], lhsv, rhsq, start=(dc == 0), stop=(dc == NC-1)),
                     r=qkeys + [(Vf.name, None)], w=[(ps_x.name, 0)])
                p.pe(lambda e, lhsw=lhsw, rhsk=rhsk, dc=dc:
                     e.matmul(ps_y[:], lhsw, rhsk, start=(dc == 0), stop=(dc == NC-1)),
                     r=[(kT.name, None), (Wf.name, None)], w=[(ps_y.name, 0)])
            p.dve(lambda e, krc=krc: e.tensor_tensor_scan(
                cum[:], ps_y[:], scr1["rc_n"][:], carry[:, 4*ci + krc: 4*ci + krc + 1],
                ALU.add, ALU.bypass),
                  r=[(ps_y.name, 0), (carry.name, (ci, krc)), "rc_n"], w=[cum.name])
            p.dve(lambda e, krc=krc: e.tensor_copy(
                carry[:, 4*ci + krc: 4*ci + krc + 1], cum[:, 511:512]),
                  r=[cum.name], w=[(carry.name, (ci, krc))])
            p.pe(lambda e, krc=krc: e.matmul(
                ps_l[:, 512:1024], self.t["c_emat"][0:8, 128*krc:128*krc+128], egb[:],
                start=True, stop=True),
                 r=[egb.name, "c_emat"], w=[(ps_l.name, 1)])
            p.dve(lambda e: e.tensor_tensor(tmp[:], ps_x[:], cum[:], ALU.mult),
                  r=[(ps_x.name, 0), cum.name], w=[tmp.name])
            p.dve(lambda e, krc=krc: e.tensor_tensor(
                pP[:, 512*krc:512*krc+512], tmp[:], ps_l[:, 512:1024], ALU.mult),
                  r=[tmp.name, (ps_l.name, 1)], w=[(pP.name, krc)])
        for dc in range(NC):
            s = dc % 2
            psu = ps_u[:, 512*s:512*s+512]
            for krc in range(4):
                lhs = UT[:, 1024*krc + 128*dc: 1024*krc + 128*dc + 128]
                p.pe(lambda e, psu=psu, lhs=lhs, krc=krc:
                     e.matmul(psu, lhs, pP[:, 512*krc:512*krc+512],
                              start=(krc == 0), stop=False),
                     r=[(pP.name, krc), (UT.name, None)], w=[(ps_u.name, s)])
            p.pe(lambda e, psu=psu, dc=dc: e.matmul(
                psu, bmat[0:8, 128*dc:128*dc+128], egb[:], start=False, stop=True),
                 r=[egb.name, (bmat.name, None)], w=[(ps_u.name, s)])
            if alpha_col is None:
                p.dve(lambda e, psu=psu, dc=dc: e.tensor_tensor(
                    fA[:, 512*dc:512*dc+512], psu, bca[:], ALU.mult),
                      r=[(ps_u.name, s), bca.name], w=[(fA.name, dc)])
            else:
                p.dve(lambda e, psu=psu: e.scalar_tensor_tensor(
                    tmp[:], psu, 1.0, bca[:], ALU.mult, ALU.mult),
                      r=[(ps_u.name, s), bca.name], w=[tmp.name])
                p.dve(lambda e, dc=dc: e.tensor_add(tmp[:], tmp[:], fA[:, 512*dc:512*dc+512]),
                      r=[tmp.name, (fA.name, dc)], w=[(tmp.name, "b")])
                rsl = scr1["rc_n"]
                p.dve(lambda e, dc=dc, rsl=rsl: e.tensor_tensor(
                    totT[:, 512*dc:512*dc+512], tmp[:], rsl[:], ALU.mult),
                      r=[(tmp.name, "b"), "rc_n"], w=[(totT.name, dc)])

    def emit_l1_ffn(self, stream):
        p = self.p
        with contextlib.ExitStack() as es:
            rc_n = self.sb("rc_n", [128, 512], F32, es)
            flats = {}
            for nm, hb in [("Vf", "L1_ffn_V_fwd"), ("Wf", "L1_ffn_W_fwd"),
                           ("Vi", "L1_ffn_V_inv"), ("Wie", "L1_ffn_W_inv_exp")]:
                t = self.sb(nm, [128, NC * 512], BF16, es)
                flats[nm] = t
            for nm in ["UfT", "UiT"]:
                flats[nm] = self.sb(nm, [128, 4 * D], BF16, es)
            rw2T = self.sb("rw2T", [128, NC * 8], BF16, es)
            self.t["rw2T"] = rw2T
            with contextlib.ExitStack() as esp:
                ps_prep = self.ps("ps_prep", [128, 2048], BF16, esp)
                self.t["ps_prep"] = ps_prep
                stg32 = self.sb("fstg32", [128, 2 * 512], F32, esp)
                cnt = 0
                for nm, hb in [("Vf", "L1_ffn_V_fwd"), ("Wf", "L1_ffn_W_fwd"),
                               ("Vi", "L1_ffn_V_inv"), ("Wie", "L1_ffn_W_inv_exp")]:
                    t = flats[nm]
                    hr = self.inp[hb].rearrange("k (c p) r -> p c k r", p=128)
                    for c in range(NC):
                        s2 = cnt % 2
                        ssl = stg32[:, 512*s2:512*s2+512]
                        p.dma("sp", f"flt{s2}", lambda e, ssl=ssl, hr=hr, c=c:
                              e.dma_start(out=ssl.rearrange("p (k r) -> p k r", r=64),
                                          in_=hr[:, c]),
                              w=[(stg32.name, s2)])
                        dst = t[:, 512*c:512*c+512]
                        if cnt % 2 == 0:
                            p.dve(lambda e, dst=dst, ssl=ssl: e.tensor_copy(dst, ssl),
                                  r=[(stg32.name, s2)], w=[(nm, c)])
                        else:
                            p.act(lambda e, dst=dst, ssl=ssl: e.activation(
                                dst, ssl, AF.Copy),
                                  r=[(stg32.name, s2)], w=[(nm, c)])
                        cnt += 1
                sps = {}
                for pair in [[("wqT", "L1_ffn_W_Q"), ("wkT", "L1_ffn_W_K")],
                             [("wiT", "L1_ffn_W_inv"), ("r1T", "L1_ffn_router_w1")],
                             [("woT", "L1_ffn_W_O")]]:
                    streams = []
                    for half, (nm, hb) in enumerate(pair):
                        sp2, stages = self.prep_wT_stages(nm, self.inp[hb],
                                                          esp, half)
                        sps[nm] = sp2
                        streams.append(stages)
                    for rc in range(NC):
                        for st in streams:
                            st[rc]()
                for nm, hb in [("UfT", "L1_ffn_U_fwd"), ("UiT", "L1_ffn_U_inv")]:
                    t = flats[nm]
                    tmpu = self.sb(nm + "_n", [128, 512], BF16, esp)
                    hr = self.inp[hb].rearrange("k (c p) r -> p c k r", p=128)
                    for c in range(NC):
                        p.dma("pool", nm + "n", lambda e, c=c, hr=hr, tmpu=tmpu: e.dma_start(
                            out=tmpu[:].rearrange("p (k r) -> p k r", r=64), in_=hr[:, c]),
                              w=[(tmpu.name, None)])
                        for b4 in range(4):
                            p.pe(lambda e, b4=b4, tmpu=tmpu: e.transpose(
                                ps_prep[:, 128*b4:128*b4+128], tmpu[:, 128*b4:128*b4+128],
                                self.t["c_ident"][:]),
                                 r=[(tmpu.name, None), "c_ident"],
                                 w=[(ps_prep.name, (0, b4))])
                        dst = t[:].rearrange("p (kc d) -> p kc d", d=D)[:, :, 128*c:128*c+128]
                        p.dve(lambda e, dst=dst: e.tensor_copy(
                            dst, ps_prep[:, 0:512].rearrange("p (kc d) -> p kc d", d=128)),
                              r=[(ps_prep.name, (0, b4)) for b4 in range(4)],
                              w=[(t.name, ("c", c))])
                rw2n = self.sb("rw2n", [8, D], BF16, esp)
                p.dma("pool", "rw2", lambda e: e.dma_start(out=rw2n[:],
                                                           in_=self.inp["L1_ffn_router_w2"]),
                      w=[rw2n.name])
                for c in range(NC):
                    p.pe(lambda e, c=c: e.transpose(ps_prep[:, 0:8], rw2n[0:8, 128*c:128*c+128],
                                                    self.t["c_ident"][0:8, 0:8]),
                         r=[rw2n.name, "c_ident"], w=[(ps_prep.name, (0, 0))])
                    p.dve(lambda e, c=c: e.tensor_copy(rw2T[:, 8*c:8*c+8], ps_prep[:, 0:8]),
                          r=[(ps_prep.name, (0, 0))], w=[("rw2T", c)])
                p.barrier()
            bf = self.sb("bfwd", [8, D], BF16, es)
            bi = self.sb("binv", [8, D], BF16, es)
            p.dma("pool", "bfwd", lambda e: e.dma_start(out=bf[:], in_=self.inp["L1_ffn_b_fwd"]),
                  w=[bf.name])
            p.dma("pool", "binv", lambda e: e.dma_start(out=bi[:], in_=self.inp["L1_ffn_b_inv"]),
                  w=[bi.name])
            rb1 = self.sb("rb1", [128, NC], F32, es)
            for c in range(NC):
                p.dma("sp", "rb1", lambda e, c=c: e.dma_start(
                    out=rb1[:, c:c+1],
                    in_=self.inp["L1_ffn_router_b1"].rearrange("(a p) -> p a", p=128)[:, c:c+1]),
                      w=[("rb1", c)])
            self.t["rb1"] = rb1
            rb2e = self.sb("rb2e", [8, 2], F32, es)
            p.dma("sp", "rb2", lambda e: e.dma_start(
                out=rb2e[0:8, 0:1], in_=self.inp["L1_ffn_router_b2"].rearrange("(a k) -> k a", a=1)),
                  w=[("rb2e", 0)])
            p.dma("sp", "rb2", lambda e: e.dma_start(
                out=rb2e[0:8, 1:2], in_=self.inp["L1_ffn_expert_bias"].rearrange("(a k) -> k a", a=1)),
                  w=[("rb2e", 1)])
            p.dve(lambda e: e.tensor_add(rb2e[0:8, 0:1], rb2e[0:8, 0:1], rb2e[0:8, 1:2]),
                  r=[("rb2e", 0), ("rb2e", 1)], w=["rb2e"])
            alpha_sb = self.sb("alpha_sb", [1, 1], F32, es)
            p.dma("sp", "alph", lambda e: e.dma_start(
                out=alpha_sb[:], in_=self.inp["L1_ffn_alpha_bi"]), w=["alpha"])
            xtc = self.sb("xtc", [128, NC * 512], BF16, es)
            xbshared = self.sb("xbshared", [128, 2 * D], BF16, es)
            pP = xbshared
            wslot = self.sb("wslot", [128, NC * D], BF16, es)
            qTc = self.sb("qTc", [128, NC * 512], BF16, es)
            kTc = self.sb("kTc", [128, NC * 512], BF16, es)
            iqT = self.sb("iqT", [128, NC * 512], BF16, es)
            ikT = self.sb("ikT", [128, NC * 512], BF16, es)
            hT = self.sb("hT1", [128, 2 * 512], BF16, es)
            fA = self.sb("fA", [128, NC * 512], BF16, es)
            totT = fA
            carry = self.sb("carry", [128, 8], F32, es)
            p.dve(lambda e: e.memset(carry[:], 0.0), w=[carry.name])
            scr1 = {"eg": self.sb("eg", [8, 512], F32, es),
                    "egb": self.sb("egb", [8, 512], BF16, es),
                    "rw": self.sb("rw", [1, 512], F32, es),
                    "lg": self.sb("lg", [8, 512], F32, es),
                    "cum": self.sb("cum", [128, 512], F32, es),
                    "tmp": self.sb("tmp1", [128, 512], F32, es),
                    "hi": self.sb("hi1", [1, 512], BF16, es),
                    "lo": self.sb("lo1", [1, 512], BF16, es),
                    "rb1": rb1, "rb2e": rb2e, "rc_n": rc_n,
                    "rw1T": wslot, "rw2T": rw2T}
            bcA = self.sb("bcA", [128, 512], F32, es)
            alpha_col = alpha_sb
            ps_trf = self.ps("ps_trf", [128, 2048], BF16, es)
            ps_x = self.ps("ps_x", [128, 512], F32, es)
            ps_y = self.ps("ps_y", [128, 512], F32, es)
            ps_u = self.ps("ps_u2", [128, 1024], F32, es)
            ps_l = self.ps("ps_l", [128, 1024], F32, es)
            scr = {"sx": self.sb("l1f_sx", [128, 2], F32, es),
                   "sq": self.sb("l1f_sq", [128, 2], F32, es),
                   "mt": self.sb("l1f_mt", [128, 2], F32, es),
                   "vt": self.sb("l1f_vt", [128, 2], F32, es),
                   "junk": hT}

            def load_w(nm, dep):
                spw = sps[nm]
                for c in range(NC):
                    dstw = wslot[:, D*c:D*c+D]
                    p.dma("sp", f"wsl{c % 2}", lambda e, spw=spw, dstw=dstw, c=c:
                          e.dma_start(out=dstw, in_=spw[c]),
                          w=[("wslot", ("ld", c))])

            def proj(dst, dkey, rhs_t, rhs_keys):
                for oc in range(NC):
                    sl = oc % 2
                    for kc in range(NC):
                        lhs = wslot[:, D*kc + 128*oc: D*kc + 128*oc + 128]
                        rhs = rhs_t[:, 512*kc:512*kc+512]
                        p.pe(lambda e, lhs=lhs, rhs=rhs, kc=kc, sl=sl:
                             e.matmul(ps_l[:, 512*sl:512*sl+512], lhs, rhs,
                                      start=(kc == 0), stop=(kc == NC-1)),
                             r=rhs_keys + [("wslot", ("ld", kc))], w=[(ps_l.name, sl)])
                    p.dve(lambda e, oc=oc, sl=sl: e.tensor_copy(
                        dst[:, 512*oc:512*oc+512], ps_l[:, 512*sl:512*sl+512]),
                          r=[(ps_l.name, sl)], w=[(dkey, oc)])

            for qc in range(4):
                with contextlib.ExitStack() as est:
                    self.transpose_stream(stream, xtc, est, tiles=range(4*qc, 4*qc+4),
                                          xt_cols=512, colbase=-512*qc, ptr_ext=ps_trf,
                                          xb_ext=xbshared)
                xkeys = [(xtc.name, ("tt", tt)) for tt in range(4*qc, 4*qc+4)]
                load_w("wqT", None)
                proj(qTc, "qTc", xtc, xkeys)
                load_w("wkT", None)
                proj(kTc, "kTc", xtc, xkeys)
                load_w("wiT", None)
                proj(iqT, "iqT", qTc, [("qTc", oc) for oc in range(NC)])
                proj(ikT, "ikT", kTc, [("kTc", oc) for oc in range(NC)])
                load_w("r1T", None)
                p.dma("sp", "rcn", lambda e, qc=qc: e.dma_start(
                    out=rc_n[:], in_=self.inp["c_recipn"][:, 512*qc:512*qc+512]),
                      w=["rc_n"])
                qkeysA = [("qTc", oc) for oc in range(NC)]
                self.branch(qTc, kTc, qkeysA, flats["Vf"], flats["Wf"], flats["UfT"],
                            bf, carry, 0, pP, hT, scr1, qc,
                            (ps_x, ps_y, ps_u, ps_l),
                            alpha_col=None, fA=fA, bcA=bcA)
                qkeysB = [("iqT", oc) for oc in range(NC)]
                self.branch(iqT, ikT, qkeysB, flats["Wie"], flats["Vi"], flats["UiT"],
                            bi, carry, 1, pP, hT, scr1, qc,
                            (ps_x, ps_y, ps_u, ps_l),
                            alpha_col=alpha_col, fA=fA, bcA=bcA, totT=totT)
                load_w("woT", None)
                for sub in range(4):
                    tt = 4 * qc + sub
                    for half in range(2):
                        sl = (2 * sub + half) % 2
                        pl = ps_l[:, 512*sl:512*sl+512]
                        for kc in range(NC):
                            lhs = totT[:, 512*kc + 128*sub: 512*kc + 128*sub + 128]
                            rhs = wslot[:, D*kc + 512*half: D*kc + 512*half + 512]
                            p.pe(lambda e, pl=pl, lhs=lhs, rhs=rhs, kc=kc:
                                 e.matmul(pl, lhs, rhs,
                                          start=(kc == 0), stop=(kc == NC-1)),
                                 r=[("fA", kc2) for kc2 in range(NC)]
                                 + [("wslot", ("ld", kc))],
                                 w=[(ps_l.name, sl)])
                        out_sl = stream[:, D*tt + 512*half: D*tt + 512*half + 512]
                        p.dve(lambda e, pl=pl, out_sl=out_sl: e.tensor_add(
                            out_sl, pl, out_sl),
                              r=[(ps_l.name, sl), (stream.name, ("tile", tt))],
                              w=[(stream.name, ("tile", tt))])
                    if sub % 2 == 1:
                        specs = [(stream[:, D*t2:D*t2+D], stream[:, D*t2:D*t2+D],
                                  [(stream.name, ("tile", t2))],
                                  [(stream.name, ("tile", t2))])
                                 for t2 in (tt - 1, tt)]
                        self.ln_pair(specs, scr)
            p.barrier()


from concourse.bass_utils import run_bass_kernel_spmd

L1FFN_SHAPES = {"W_Q": [1024, 1024], "W_K": [1024, 1024], "W_O": [1024, 1024],
                "W_inv": [1024, 1024], "V_fwd": [8, 1024, 64], "W_fwd": [8, 1024, 64],
                "U_fwd": [8, 1024, 64], "b_fwd": [8, 1024], "V_inv": [8, 1024, 64],
                "W_inv_exp": [8, 1024, 64], "U_inv": [8, 1024, 64], "b_inv": [8, 1024],
                "router_w1": [1024, 1024], "router_b1": [1024], "router_w2": [8, 1024],
                "router_b2": [8], "alpha_bi": [1, 1], "expert_bias": [8]}


def _build_fused():
    nc = bass.Bass("TRN2", target_bir_lowering=False, debug=False)
    core = Core(nc)
    x = core.dram_in("x", [T, D])
    core.dram_in("L0_attn_Wqk", [1024, 2048])
    core.dram_in("L0_attn_bqk", [2048])
    core.dram_in("L0_attn_Wv", [1024, 1024])
    core.dram_in("L0_attn_bv", [1024])
    core.dram_in("L0_ffn_w_gate", [4096, 1024])
    core.dram_in("L0_ffn_w_up", [4096, 1024])
    core.dram_in("L0_ffn_w_down", [1024, 4096])
    core.dram_in("L1_attn_Wq", [1024, 1024])
    core.dram_in("L1_attn_Wkv", [1024, 128])
    core.dram_in("L1_attn_Wo", [1024, 1024])
    for n2, shp in L1FFN_SHAPES.items():
        core.dram_in("L1_ffn_" + n2, shp)
    y = core.dram_out("y", [T, D])
    core.load_constants()
    x1sp = core.dram_tmp("x1sp", [T, D], F32)
    rh_sp = core.dram_tmp("rh_sp", [NT, 128, 4096], BF16)
    wuT_sp = core.dram_tmp("wuT_sp", [32, 128, 1024], BF16)
    wdT_sp = core.dram_tmp("wdT_sp", [32, 128, 1024], BF16)
    scd0 = core.sb("scd_l", [128, NT], F32)
    mwd = core.sb("mwd", [128, 1], F32)
    qs = core.make_l0_wquants(None, wuT_sp, wdT_sp, None)
    with contextlib.ExitStack() as es_r:
        with contextlib.ExitStack() as es1:
            stream1 = core.sb("stream1", [128, NT * D], F32, es1)
            core.load_stream(x, stream1)

            def aux_alloc(es_attn):
                mjunk = core.sb("wmjks", [128, 1024], BF16, es_attn)
                for q in qs:
                    q.alloc_mean(es_attn, junk_ext=mjunk)

            def aux_step(h):
                for q in qs:
                    q.mean_chunk(2*h)
                    q.mean_chunk(2*h+1)
            core.emit_l0_attn(stream1, aux_alloc, aux_step)
            rt = core.sb("rt", [128, NC * T], BF16, es_r, side="right")
            mraw = core.sb("mraw", [128, NT], F32, es_r, side="right")
            core.store_stream(stream1, x1sp, wreg="x1sp_r")
        core.p.barrier()
        with contextlib.ExitStack() as es_f:
            wgT = core.sb("wgT", [128, 8 * 4096], BF16, es_f)
            mwg = core.sb("mwg", [128, 1], F32, es_f)
            mwu = core.sb("mwu", [128, 1], F32, es_f)
            qs[0].wT = wgT
            with contextlib.ExitStack() as es_A:
                core.emit_l0_ffn_phaseA(x1sp, rt, mraw, qs, mwg, mwu, mwd,
                                        es_A, means_done=True)
            core.p.barrier()
            wuT = core.sb("wuT", [128, 8 * 4096], BF16, es_f)
            core.emit_l0_ffn_part2(rt, mraw, rh_sp, scd0, wgT, wuT, wuT_sp,
                                   mwg, mwu)
        core.p.barrier()
    stream2 = core.sb("stream2", [128, NT * D], F32)
    core.emit_l0_ffn_part3(x1sp, rh_sp, scd0, stream2, wdT_sp, mwd)
    core.emit_l1_attn(stream2)
    core.emit_l1_ffn(stream2)
    core.store_stream(stream2, y)
    core.p.barrier()
    core.p.finalize()
    return nc


def kernel(**inputs):
    f = np.float32
    consts = host_constants()
    xfull = np.ascontiguousarray(np.asarray(inputs["L0_attn_x"], f))
    n = 8
    shared = {}
    for k in ["L0_attn_Wqk", "L0_attn_bqk", "L0_attn_Wv", "L0_attn_bv",
              "L0_ffn_w_gate", "L0_ffn_w_up", "L0_ffn_w_down",
              "L1_attn_Wq", "L1_attn_Wkv", "L1_attn_Wo"]:
        shared[k] = np.ascontiguousarray(np.asarray(inputs[k], f))
    for nme in L1FFN_SHAPES:
        v = np.asarray(inputs["L1_ffn_" + nme], f)
        if nme == "alpha_bi":
            v = v.reshape(1, 1)
        shared["L1_ffn_" + nme] = np.ascontiguousarray(v)
    shared.update(consts)
    in_maps = [{"x": np.ascontiguousarray(xfull[c]), **shared} for c in range(n)]
    nc = _build_fused()
    res = run_bass_kernel_spmd(nc, in_maps, core_ids=list(range(n)))
    return np.stack([np.asarray(res.results[c]["y"], f) for c in range(n)], axis=0)

